# version 16
# baseline (speedup 1.0000x reference)
import numpy as np
from contextlib import ExitStack
import concourse.bass as bass
import concourse.mybir as mybir
from concourse.alu_op_type import AluOpType as OP

F32 = mybir.dt.float32
BF16 = mybir.dt.bfloat16
I32 = mybir.dt.int32
AF = mybir.ActivationFunctionType
AX = mybir.AxisListType

D = 1024
T = 2048
NDT = D // 128
NTT = T // 128
NB = T // 512
MAGIC = 12582912.0


class Sem:
    __slots__ = ("name", "n", "h")
    def __init__(self, name):
        self.name = name
        self.n = 0
        self.h = None


class Prog:
    ENGS = ("sync", "gpsimd", "vector", "scalar", "tensor")

    def __init__(self, nc):
        self.nc = nc
        self.thunks = {e: [] for e in self.ENGS}
        self.sems = {}

    def sem(self, name):
        s = self.sems.get(name)
        if s is None:
            s = Sem(name)
            self.sems[name] = s
        return s

    def on(self, eng, fn):
        self.thunks[eng].append(fn)

    def dma(self, eng, out, in_, sem):
        sem.n += 16
        self.on(eng, lambda q, o=out, i=in_, s=sem: q.dma_start(out=o, in_=i).then_inc(s.h, 16))

    def wait(self, eng, sem, val=None):
        v = sem.n if val is None else val
        if v > 0:
            self.on(eng, lambda q, s=sem, vv=v: q.wait_ge(s.h, vv))

    def op(self, eng, fn, sem=None, k=1):
        if sem is not None:
            sem.n += k
            self.on(eng, lambda q, f=fn, s=sem, kk=k: f(q).then_inc(s.h, kk))
            return sem.n
        self.on(eng, lambda q, f=fn: f(q))
        return None

    def drain(self, eng):
        self.on(eng, lambda q: q.drain())

    def finalize(self, es):
        for s in self.sems.values():
            s.h = es.enter_context(self.nc.semaphore(s.name))
        with self.nc.Block() as block:
            @block.sync
            def _(q):
                for t in self.thunks["sync"]:
                    t(q)

            @block.gpsimd
            def _(q):
                for t in self.thunks["gpsimd"]:
                    t(q)

            @block.vector
            def _(q):
                for t in self.thunks["vector"]:
                    t(q)

            @block.scalar
            def _(q):
                for t in self.thunks["scalar"]:
                    t(q)

            @block.tensor
            def _(q):
                for t in self.thunks["tensor"]:
                    t(q)


def col_ap(dram_ap, base, n=128):
    return bass.AP(dram_ap.tensor, base, [[1, n], [1, 1]])


def build_load_x(p, es, B):
    xr, xb = B["xr"], B["xb"]
    pm = B["pm"]
    identf = B["identf"]
    x_in = B["dram"]["x"]
    s_ld = p.sem("s_ld")
    s_tr = p.sem("s_tr")
    s_ve = p.sem("s_ve")
    s_se = p.sem("s_se")
    xin = B["xin"]
    s_x = [p.sem("s_x0"), p.sem("s_x1")]
    tr_done = [0, 0]
    ev_done = [0, 0, 0, 0]
    es_done = [0, 0, 0, 0]
    p.wait("tensor", s_ld, s_ld.n)
    for i in range(NTT):
        buf = i % 2
        if tr_done[buf]:
            p.wait("sync", s_tr, tr_done[buf])
        p.dma("sync", xin[buf][:], x_in[128 * i:128 * (i + 1), :], s_x[buf])
        p.wait("tensor", s_x[buf], s_x[buf].n)
        for half in range(2):
            bank = buf * 2 + half
            if ev_done[bank]:
                p.wait("tensor", s_ve, ev_done[bank])
                p.wait("tensor", s_se, es_done[bank])
            for k in range(4):
                j = half * 4 + k
                tok = p.op("tensor", lambda q, o=pm[:, bank * 512 + 128 * k: bank * 512 + 128 * (k + 1)],
                           i_=xin[buf][:, 128 * j:128 * (j + 1)]:
                           q.transpose(o, i_, identf[:]), s_tr if k == 3 else None)
            tr_done[buf] = s_tr.n
            p.wait("vector", s_tr, s_tr.n)
            for k in range(4):
                j = half * 4 + k
                src = pm[:, bank * 512 + 128 * k: bank * 512 + 128 * (k + 1)]
                p.op("vector", lambda q, o=xr[j][:, 128 * i:128 * (i + 1)], s=src:
                     q.tensor_copy(out=o, in_=s), s_ve if k == 3 else None)
            ev_done[bank] = s_ve.n
            p.wait("scalar", s_ve, s_ve.n)
            for k in range(4):
                j = half * 4 + k
                p.op("scalar", lambda q, o=xb[j][:, 128 * i:128 * (i + 1)],
                     s=xr[j][:, 128 * i:128 * (i + 1)]:
                     q.copy(out=o, in_=s), s_se if k == 3 else None)
            es_done[bank] = s_se.n




def build_block0(p, es, B):
    nc = p.nc
    xr, xb, hb = B["xr"], B["xb"], B["hb"]
    U1, U2, U3 = B["U1"], B["U2"], B["U3"]
    G1, G2, G3 = B["G1"], B["G2"], B["G3"]
    pm, pb = B["pm"], B["pb"]
    identf = B["identf"]
    dram = B["dram"]

    wj = [p.sbuf(es, f"b0_wj{k}", (128, 3 * D), BF16) for k in range(2)] if False else B["wj"]
    wm = B["wm"]
    dbc, bmc = B["dbc"], B["bmc"]

    s_ld = p.sem("s_ld")
    s_wav2 = [p.sem("s_wav0"), p.sem("s_wav1")]
    s_wmx = p.sem("s_wmx")
    s_tr = p.sem("s_tr")
    s_ve = p.sem("s_ve")
    s_se = p.sem("s_se")
    s_ge = p.sem("s_ge")

    x_in = dram["x"]
    W_aiv = dram["W_aiv"]
    db_in = dram["decay_bias"]
    W_mix = dram["W_mix"]
    bm_in = dram["b_mix"]

    for j in range(NDT):
        p.dma("sync", dbc[:, j:j + 1], col_ap(db_in, 128 * j), s_ld)
        p.dma("sync", bmc[:, j:j + 1], col_ap(bm_in, 128 * j), s_ld)
    for j in range(NDT):
        p.dma("gpsimd", wm[j][:], W_mix[128 * j:128 * (j + 1), :], s_wmx)

    def load_wav(j):
        dst = wj[j % 2]
        s_wav = s_wav2[j % 2]
        for qq in range(3):
            srcap = bass.AP(W_aiv.tensor, qq * D + 128 * j,
                            [[3072, 128], [3072 * 128, 8], [1, 128]])
            p.dma("gpsimd", dst[:, qq * D:(qq + 1) * D], srcap, s_wav)

    load_wav(0)
    load_wav(1)

    build_load_x(p, es, B)

    p.wait("tensor", s_ve, s_ve.n)
    p.wait("tensor", s_se, s_se.n)

    evac_a_done = 0
    evac_i_done = 0
    evac_v_done = 0
    h_tok = 0
    for j in range(NDT):
        wtile = wj[j % 2]
        p.wait("tensor", s_wav2[j % 2], 48 * (j // 2 + 1))
        if evac_v_done:
            p.wait("tensor", s_ve, evac_v_done)
            p.wait("tensor", s_se, evac_i_done)
        mmtok = {}
        for qi, ps in [(0, pm), (1, pb), (2, pm)]:
            if qi == 2:
                p.wait("tensor", s_se, evac_a_done)
            for nb in range(NB):
                for dtl in range(NDT):
                    lhsT = wtile[:, qi * D + dtl * 128: qi * D + (dtl + 1) * 128]
                    rhs = xb[dtl][:, nb * 512:(nb + 1) * 512]
                    p.op("tensor", lambda q, o=ps[:, nb * 512:(nb + 1) * 512], l=lhsT, r=rhs,
                         st=(dtl == 0), sp=(dtl == NDT - 1):
                         q.matmul(o, l, r, start=st, stop=sp),
                         s_tr if (dtl == NDT - 1 and nb == NB - 1) else None)
            mmtok[qi] = s_tr.n
            if qi == 0:
                if h_tok:
                    p.wait("scalar", s_ve, h_tok)
                p.wait("scalar", s_tr, mmtok[0])
                p.drain("scalar")
                for nb in range(NB):
                    p.op("scalar", lambda q, o=U1[:, nb * 512:(nb + 1) * 512],
                         i_=pm[:, nb * 512:(nb + 1) * 512], b=dbc[:, j:j + 1]:
                         q.activation(out=o, in_=i_, func=AF.Sigmoid, bias=b),
                         s_se if nb == NB - 1 else None)
                evac_a_done = s_se.n
            elif qi == 1:
                p.wait("scalar", s_tr, mmtok[1])
                for nb in range(NB):
                    p.op("scalar", lambda q, o=U2[:, nb * 512:(nb + 1) * 512],
                         i_=pb[:, nb * 512:(nb + 1) * 512]:
                         q.activation(out=o, in_=i_, func=AF.Sigmoid),
                         s_se if nb == NB - 1 else None)
                evac_i_done = s_se.n
        p.wait("vector", s_tr, mmtok[2])
        if h_tok:
            p.drain("vector")
        for nb in range(NB):
            p.op("vector", lambda q, o=U3[:, nb * 512:(nb + 1) * 512],
                 i_=pm[:, nb * 512:(nb + 1) * 512]:
                 q.tensor_copy(out=o, in_=i_),
                 s_ve if nb == NB - 1 else None)
        evac_v_done = s_ve.n
        if j + 2 < NDT:
            p.wait("gpsimd", s_tr, mmtok[2])
            load_wav(j + 2)

        p.wait("vector", s_se, evac_i_done)
        p.drain("vector")
        p.op("vector", lambda q: q.tensor_tensor(out=U3[:], in0=U2[:], in1=U3[:], op=OP.mult))
        p.drain("vector")
        sq_tok = p.op("vector", lambda q: q.tensor_tensor(out=U2[:], in0=U1[:], in1=U1[:], op=OP.mult), s_ve)
        p.wait("scalar", s_ve, sq_tok)
        p.drain("scalar")
        cm = B["cmisc"]
        rt_tok = p.op("scalar", lambda q, b=cm[:, 0:1]: q.activation(out=U2[:], in_=U2[:], func=AF.Sqrt,
                                                       scale=-1.0, bias=b), s_se)
        p.wait("vector", s_se, rt_tok)
        p.drain("vector")
        p.op("vector", lambda q: q.tensor_tensor(out=U3[:], in0=U3[:], in1=U2[:], op=OP.mult))
        p.drain("vector")
        for c in range(32):
            sl = slice(c * 64, (c + 1) * 64)
            p.op("vector", lambda q, s=sl: q.tensor_tensor_scan(
                out=U2[:, s], data0=U1[:, s], data1=U1[:, s],
                initial=1.0, op0=OP.mult, op1=OP.bypass))
        p.drain("vector")
        p.op("vector", lambda q: q.tensor_scalar(out=U1[:], in0=U2[:], scalar1=1e-10,
                                                 scalar2=None, op0=OP.max))
        p.drain("vector")
        p.op("vector", lambda q: q.reciprocal(out=U1[:], in_=U1[:]))
        p.drain("vector")
        p.op("vector", lambda q: q.tensor_tensor(out=U3[:], in0=U3[:], in1=U1[:], op=OP.mult))
        p.drain("vector")
        for c in range(32):
            sl = slice(c * 64, (c + 1) * 64)
            p.op("vector", lambda q, s=sl: q.tensor_tensor_scan(
                out=U1[:, s], data0=U3[:, s], data1=U3[:, s],
                initial=0.0, op0=OP.add, op1=OP.bypass))
        p.drain("vector")
        p.op("vector", lambda q: q.tensor_tensor(out=U1[:], in0=U2[:], in1=U1[:], op=OP.mult))
        p.drain("vector")
        ctd = bass.AP(U2, 63, [[T, 128], [64, 32]])
        cfs = bass.AP(U1, 63, [[T, 128], [64, 32]])
        p.op("vector", lambda q: q.tensor_scalar(out=G1[:], in0=ctd, scalar1=1e-10,
                                                 scalar2=None, op0=OP.max))
        p.drain("vector")
        p.op("vector", lambda q: q.tensor_tensor_scan(
            out=G2[:], data0=G1[:], data1=G1[:], initial=1.0, op0=OP.mult, op1=OP.bypass))
        p.drain("vector")
        p.op("vector", lambda q: q.tensor_scalar(out=G1[:], in0=G2[:], scalar1=1e-10,
                                                 scalar2=None, op0=OP.max))
        p.drain("vector")
        p.op("vector", lambda q: q.reciprocal(out=G1[:], in_=G1[:]))
        p.drain("vector")
        p.op("vector", lambda q: q.tensor_tensor(out=G3[:], in0=cfs, in1=G1[:], op=OP.mult))
        p.drain("vector")
        p.op("vector", lambda q: q.tensor_tensor_scan(
            out=G3[:], data0=G3[:], data1=G3[:], initial=0.0, op0=OP.add, op1=OP.bypass))
        p.drain("vector")
        p.op("vector", lambda q: q.tensor_tensor(out=G2[:], in0=G2[:], in1=G3[:], op=OP.mult))
        p.drain("vector")
        incb = bass.AP(G2, 0, [[32, 128], [1, 31], [0, 64]])
        u3v = U3[:, 64:T].rearrange("p (c j) -> p c j", c=31)
        u2v = U2[:, 64:T].rearrange("p (c j) -> p c j", c=31)
        p.op("vector", lambda q: q.tensor_tensor(out=u3v, in0=u2v, in1=incb, op=OP.mult))
        p.drain("vector")
        p.op("vector", lambda q, jj=j: q.tensor_tensor(
            out=hb[jj][:, 64:T], in0=U3[:, 64:T], in1=U1[:, 64:T], op=OP.add))
        h_tok = p.op("vector", lambda q, jj=j: q.tensor_copy(out=hb[jj][:, 0:64], in_=U1[:, 0:64]),
                     s_ve)

    B["tok_b0_mm"] = p.sems["s_tr"].n
    p.wait("tensor", s_ve, h_tok)
    p.wait("tensor", s_wmx, s_wmx.n)
    p.wait("tensor", s_se, s_se.n)
    r_tok = [0] * NDT
    for do in range(NDT):
        ps = pm if do % 2 == 0 else pb
        if do >= 2:
            p.wait("tensor", s_ve, r_tok[do - 2])
        for nb in range(NB):
            for dtl in range(NDT):
                lhsT = wm[dtl][:, do * 128:(do + 1) * 128]
                rhs = hb[dtl][:, nb * 512:(nb + 1) * 512]
                p.op("tensor", lambda q, o=ps[:, nb * 512:(nb + 1) * 512], l=lhsT, r=rhs,
                     st=(dtl == 0), sp=(dtl == NDT - 1):
                     q.matmul(o, l, r, start=st, stop=sp),
                     s_tr if (dtl == NDT - 1 and nb == NB - 1) else None)
        p.wait("vector", s_tr, s_tr.n)
        p.drain("vector")
        for nb in range(NB):
            p.op("vector", lambda q, o=xr[do][:, nb * 512:(nb + 1) * 512],
                 i0=ps[:, nb * 512:(nb + 1) * 512], sc=bmc[:, do:do + 1],
                 i1=xr[do][:, nb * 512:(nb + 1) * 512]:
                 q.scalar_tensor_tensor(out=o, in0=i0, scalar=sc, in1=i1,
                                        op0=OP.add, op1=OP.add),
                 s_ve if nb == NB - 1 else None)
        r_tok[do] = s_ve.n

    p.wait("vector", s_ve, r_tok[-1])
    build_ln(p, B, xr, xr, xb, wait_tok=r_tok[-1], gain=None, bias=None)
    B["tok_ln0_v"] = p.sems["s_ve"].n
    B["tok_ln0_g"] = p.sems["s_ge"].n


def build_block1(p, es, B):
    xr, xb = B["xr"], B["xb"]
    kb = B["hb"]
    wqT = B["wm"]
    wj = B["wj"]
    U1, U2, U3 = B["U1"], B["U2"], B["U3"]
    pm, pb = B["pm"], B["pb"]
    identf, identb = B["identf"], B["identb"]
    rowA, rowB = B["rowA"], B["rowB"]
    ones_r = B["ones_r"]
    dram = B["dram"]
    amaxc = B["amaxc"]
    fbc = B["fbc"]
    wsc = B["wsc"]

    s_ld = p.sem("s_ld1")
    s_tr = p.sem("s_tr")
    s_ve = p.sem("s_ve")
    s_se = p.sem("s_se")
    s_ge = p.sem("s_ge")
    s_am = p.sem("s_am")

    pk_in = dram["pk"]
    scr = dram["sc_amax"]

    pku = [wj[0][:, 0:2048].bitcast(I32), wj[0][:, 2048:4096] if False else wj[0][:, 1024:3072].bitcast(I32)]
    pku = [wj[0][:, 0:2048].bitcast(I32), wj[1][:, 0:2048].bitcast(I32)]
    w_u = wj[0][:, 2048:3072]
    itmp = wj[1][:, 2048:3072].bitcast(I32)
    itmp = B["U3h"]

    p.wait("sync", s_tr, B["tok_b0_mm"])
    p.dma("sync", pku[0], pk_in[0:128 * 1024], s_ld)
    p.dma("sync", pku[1], pk_in[128 * 1024:256 * 1024], s_ld)
    for j in range(NDT):
        p.dma("sync", fbc[:, j:j + 1], col_ap(dram["fbias"], 128 * j), s_ld)
    p.dma("sync", wsc[:], dram["wscale"], s_ld)

    p.wait("vector", s_ld, s_ld.n)
    p.wait("vector", s_ge, B["tok_ln0_g"])
    wq_tok = 0
    for blk in range(8):
        sh = 2 * (blk // 2)
        pt = blk % 2
        p.drain("vector")
        p.op("vector", lambda q, i0=pku[pt], s=sh: q.tensor_scalar(
            out=itmp, in0=i0, scalar1=s, scalar2=3, op0=OP.logical_shift_right,
            op1=OP.bitwise_and))
        p.drain("vector")
        p.op("vector", lambda q: q.tensor_copy(out=w_u, in_=itmp))
        p.drain("vector")
        wu_tok = p.op("vector", lambda q: q.tensor_scalar(
            out=w_u, in0=w_u, scalar1=-1.0, scalar2=None, op0=OP.add), s_ve)
        p.wait("tensor", s_tr, B["tok_b0_mm"])
        p.wait("tensor", s_ve, wu_tok)
        bank = blk % 4
        for jp in range(8):
            sl = pb[:, bank * 512 + 64 * jp: bank * 512 + 64 * (jp + 1)].bitcast(BF16)
            p.op("tensor", lambda q, o=sl, i_=w_u[:, 128 * jp:128 * (jp + 1)]:
                 q.transpose(o, i_, identb[:]),
                 s_tr if jp == 7 else None)
        tok = p.sems["s_tr"].n
        p.wait("vector", s_tr, tok)
        p.drain("vector")
        for jp in range(8):
            ssl = pb[:, bank * 512 + 64 * jp: bank * 512 + 64 * (jp + 1)].bitcast(BF16)
            p.op("vector", lambda q, o=wqT[jp][:, blk * 128:(blk + 1) * 128], s2=ssl:
                 q.tensor_copy(out=o, in_=s2),
                 s_ve if jp == 7 else None)
        wq_tok = p.sems["s_ve"].n

    p.drain("vector")
    p.op("vector", lambda q: q.tensor_tensor(out=U1[:], in0=xr[0][:], in1=xr[1][:], op=OP.max))
    p.op("vector", lambda q: q.tensor_tensor(out=U2[:], in0=xr[0][:], in1=xr[1][:], op=OP.min))
    for j in range(2, NDT):
        p.drain("vector")
        p.op("vector", lambda q, jj=j: q.tensor_tensor(out=U1[:], in0=U1[:], in1=xr[jj][:], op=OP.max))
        p.op("vector", lambda q, jj=j: q.tensor_tensor(out=U2[:], in0=U2[:], in1=xr[jj][:], op=OP.min))
    p.drain("vector")
    p.op("vector", lambda q: q.tensor_scalar(out=U2[:], in0=U2[:], scalar1=-1.0, scalar2=None, op0=OP.mult))
    p.drain("vector")
    p.op("vector", lambda q: q.tensor_tensor(out=U1[:], in0=U1[:], in1=U2[:], op=OP.max), s_ve)
    am_tok = p.sems["s_ve"].n
    p.wait("tensor", s_ve, am_tok)
    tok_red = [0] * NTT
    for i in range(NTT):
        bank = i % 4
        if i >= 4:
            p.wait("tensor", s_ve, tok_red[i - 4])
        p.op("tensor", lambda q, o=pm[:, bank * 512:bank * 512 + 128],
             i_=U1[:, 128 * i:128 * (i + 1)]:
             q.transpose(o, i_, identf[:]), s_tr)
        tok = p.sems["s_tr"].n
        p.wait("vector", s_tr, tok)
        p.drain("vector")
        p.op("vector", lambda q, o=amaxc[:, i:i + 1].rearrange("p (a o) -> p a o", a=1),
             i_=pm[:, bank * 512:bank * 512 + 128].rearrange("p (a j) -> p a j", a=1):
             q.tensor_reduce(out=o, in_=i_, axis=AX.X, op=OP.max), s_ve)
        tok_red[i] = p.sems["s_ve"].n
    red_tok = p.sems["s_ve"].n
    p.wait("sync", s_ve, red_tok)
    for i in range(NTT):
        p.dma("sync", bass.AP(scr.tensor, 128 * i, [[1, 128], [1, 1]]), amaxc[:, i:i + 1], s_am)
    p.wait("sync", s_am, p.sems["s_am"].n)
    p.dma("sync", rowA, bass.AP(scr.tensor, 0, [[T, 1], [1, T]]), s_am)
    p.wait("vector", s_am, p.sems["s_am"].n)
    p.drain("vector")
    p.op("vector", lambda q: q.tensor_scalar(out=rowB, in0=rowA, scalar1=1e-5,
                                             scalar2=None, op0=OP.max))
    p.drain("vector")
    p.op("vector", lambda q: q.reciprocal(out=rowA, in_=rowB))
    p.drain("vector")
    p.op("vector", lambda q: q.tensor_scalar(out=rowA, in0=rowA, scalar1=127.0,
                                             scalar2=None, op0=OP.mult))
    p.drain("vector")
    rows_tok = p.op("vector", lambda q: q.tensor_scalar(out=rowB, in0=rowB,
                                                        scalar1=wsc[0:1, 0:1], scalar2=1.0 / 127.0,
                                                        op0=OP.mult, op1=OP.mult), s_ve)
    p.wait("tensor", s_ve, rows_tok)
    p.wait("tensor", s_ve, wq_tok)
    for nb in range(NB):
        p.op("tensor", lambda q, o=pm[:, nb * 512:(nb + 1) * 512],
             r=rowA[0:1, nb * 512:(nb + 1) * 512]:
             q.matmul(o, ones_r[:], r, start=True, stop=True))
        p.op("tensor", lambda q, o=pb[:, nb * 512:(nb + 1) * 512],
             r=rowB[0:1, nb * 512:(nb + 1) * 512]:
             q.matmul(o, ones_r[:], r, start=True, stop=True),
             s_tr if nb == NB - 1 else None)
    bc_tok = p.sems["s_tr"].n
    p.wait("vector", s_tr, bc_tok)
    p.drain("vector")
    for j in range(NDT):
        p.op("vector", lambda q, jj=j: q.tensor_tensor(out=U2[:], in0=xr[jj][:],
                                                       in1=pm[:, 0:T], op=OP.mult))
        p.drain("vector")
        p.op("vector", lambda q, jj=j: q.tensor_scalar(
            out=kb[jj][:], in0=U2[:], scalar1=MAGIC, scalar2=MAGIC,
            op0=OP.add, op1=OP.subtract), s_ve)
        p.drain("vector")
    u3_tok = p.op("vector", lambda q: q.tensor_copy(out=U3[:], in_=pb[:, 0:T]), s_ve)
    kb_tok = u3_tok

    p.wait("tensor", s_ve, kb_tok)
    ge_tok = [0] * NDT
    ve_tok = [0] * NDT
    for do in range(NDT):
        ps = pm if do % 2 == 0 else pb
        ebuf = U2 if do % 2 == 0 else U1
        if do >= 2:
            p.wait("tensor", s_ve, ve_tok[do - 2])
        for nb in range(NB):
            for dtl in range(NDT):
                lhsT = wqT[dtl][:, do * 128:(do + 1) * 128]
                rhs = kb[dtl][:, nb * 512:(nb + 1) * 512]
                p.op("tensor", lambda q, o=ps[:, nb * 512:(nb + 1) * 512], l=lhsT, r=rhs,
                     st=(dtl == 0), sp=(dtl == NDT - 1):
                     q.matmul(o, l, r, start=st, stop=sp),
                     s_tr if (dtl == NDT - 1 and nb == NB - 1) else None)
        mmtok = p.sems["s_tr"].n
        p.wait("vector", s_tr, mmtok)
        p.drain("vector")
        p.op("vector", lambda q, e=ebuf, pss=ps: q.tensor_tensor(out=e[:], in0=pss[:, 0:T],
                                                                 in1=U3[:], op=OP.mult), s_ve)
        ve_tok[do] = p.sems["s_ve"].n
        p.drain("vector")
        p.op("vector", lambda q, dd=do, e=ebuf: q.scalar_tensor_tensor(
            out=xr[dd][:], in0=e[:], scalar=fbc[:, dd:dd + 1], in1=xr[dd][:],
            op0=OP.add, op1=OP.add), s_ve)
        ge_tok[do] = p.sems["s_ve"].n
        p.drain("vector")

    v_tok, g_tok = build_ln(p, B, xr, xr, xb, wait_tok=ge_tok[-1])
    B["tok_ln1_v"] = p.sems["s_ve"].n
    B["tok_ln1_g"] = p.sems["s_ge"].n


NH = 16
HD = 64
QB = 512
NQB = T // QB


def build_block2(p, es, B):
    xr, xb, hb = B["xr"], B["xb"], B["hb"]
    U1, U2, U3 = B["U1"], B["U2"], B["U3"]
    pm, pb = B["pm"], B["pb"]
    identf = B["identf"]
    dram = B["dram"]
    wj, wm = B["wj"], B["wm"]
    ones_r = B["ones_r"]

    s_tr = p.sem("s_tr")
    s_ve = p.sem("s_ve")
    s_se = p.sem("s_se")
    s_ge = p.sem("s_ge")
    s_ld2 = p.sem("s_ld2")
    s_xsp = p.sem("s_xsp")
    s_xre = p.sem("s_xre")
    s_wq = [p.sem("s_wq0"), p.sem("s_wq1")]
    s_wv = [p.sem("s_wv0"), p.sem("s_wv1")]
    s_wo = p.sem("s_wo")
    s_qw = [p.sem("s_qw0"), p.sem("s_qw1")]
    s_qr = [p.sem("s_qr0"), p.sem("s_qr1")]
    s_qrg = [p.sem("s_qrg0"), p.sem("s_qrg1")]
    s_bt = p.sem("s_bt")
    s_ac = p.sem("s_ac")
    s_rbc = p.sem("s_rbc")
    s_bf = p.sem("s_bf")
    s_ew = [p.sem("s_ew0"), p.sem("s_ew1")]
    s_er = [p.sem("s_er0"), p.sem("s_er1")]
    s_qe = [p.sem("s_qe0"), p.sem("s_qe1")]

    xrb = [xr[j][:, 0:T].bitcast(BF16) for j in range(NDT)]
    qkT = []
    for j in range(NDT):
        qkT.append(xrb[j][:, 0:T])
        qkT.append(xrb[j][:, T:2 * T])

    def a_sl(kt, c0, c1):
        base = (kt % 4) * 512
        return xb[kt // 4][:, base + c0: base + c1]

    u1b = U1[:, 0:T].bitcast(BF16)
    u2b = U2[:, 0:T].bitcast(BF16)
    u3b = U3[:, 0:T].bitcast(BF16)
    VE = []
    for ub in (u1b, u2b, u3b):
        for k in range(3):
            VE.append(ub[:, k * 1040:(k + 1) * 1040])
    for k in range(4, 8):
        VE.append(xb[k][:, 0:1040])
    vex = B["vex"]
    VE.extend(v[:, 0:1040] for v in vex)
    assert len(VE) == 16

    KEBt = wj[0]
    QELt = wj[1]
    qer = B["qer_t"]
    xin = B["xin"]
    MQS = xin[0][:, 0:D].bitcast(BF16)[0:16, 0:T]
    scm = dram_scm = B["dram"]["sc_mqs"]


    pekT = p.sbuf(es, "pekT", (64, 132), BF16)
    pekE = wj[0][0:64, 2048:2560]
    skq = wj[1][:, 2048:3072].bitcast(F32)
    pek_f = skq[:, 0:64]
    pkrow_f = skq[0:1, 64:128]
    pv_f = skq[:, 128:192]
    pvrow_f = skq[0:1, 192:256]
    pvb = p.sbuf(es, "pvb", (128, HD), BF16)
    pvb2 = p.sbuf(es, "pvb2", (1, HD), BF16)
    x1f = xin[1]
    x1bf = xin[1][:, 0:D].bitcast(BF16)
    pvE = [x1bf[0:128, 64 * k:64 * (k + 1)] for k in range(4)]
    onesbc = B["onesbc"]
    onesbr = B["onesbr"]
    selq = x1bf[0:128, 256:384]
    pksel0 = [x1bf[0:128, 384 + 16 * d:400 + 16 * d] for d in range(NDT)]
    pksel1 = [x1bf[0:128, 512 + 16 * d:528 + 16 * d] for d in range(NDT)]
    bqkvc = x1f[0:128, 320:344]
    kmx = x1f[0:16, 344:345]
    kpc = x1f[0:16, 345:346]

    boc = B["bmc"]
    zbuf = B["zbuf"]
    bfull = B["bfull"]
    btT = bfull
    srow = wm[1][:, 0:D].bitcast(F32)[0:1, 0:512]
    pkms = x1f[0:1, 346:347]

    wm5f = wm[5][:, 0:D].bitcast(F32)
    ccf = wm[2][:, 0:D].bitcast(F32)[0:64, 0:512]
    sab = wm[3][0:2, 0:1024]
    rbrow = wm[4][0:1, 0:512]
    qrst = [wm[5][:, 0:D].bitcast(F32)[:, 0:512], wm[0][:, 0:D].bitcast(F32)[:, 0:512]]
    wqb = [wm[6], wm[7]]
    wvb = wqb
    bvrow = wm[1][0:1, 0:D]

    W_qkv = dram["W_qkv"]
    b_qkv = dram["b_qkv"]
    W_out = dram["W_out"]
    rel_k = dram["rel_emb_k"]
    rel_v = dram["rel_emb_v"]
    xsp = dram["sc_xspill"]
    QRE = [dram["sc_qre0"], dram["sc_qre1"]]
    E2 = [dram["sc_e0"], dram["sc_e1"]]

    tok_prev_v = B["tok_ln1_v"]
    tok_prev_g = B["tok_ln1_g"]

    p.wait("sync", s_ve, tok_prev_v)
    p.wait("sync", s_ge, tok_prev_g)
    for j in range(NDT):
        p.dma("sync", bass.AP(xsp.tensor, j * 128 * T, [[T, 128], [1, T]]), xr[j][:, 0:T], s_xsp)

    for c in range(24):
        p.dma("sync", bqkvc[:, c:c + 1], col_ap(b_qkv, 128 * c), s_ld2)
    for j in range(NDT):
        p.dma("sync", boc[:, j:j + 1], col_ap(dram["b_out"], 128 * j), s_ld2)
    s_gd = p.sem("s_gd")
    p.dma("gpsimd", bvrow, bass.AP(b_qkv.tensor, 2048, [[1024, 1], [1, 1024]]), s_gd)
    p.dma("sync", selq, dram["c_selq"], s_ld2)
    p.dma("sync", onesbc[:], dram["c_ones_col_bf"], s_ld2)
    p.dma("sync", onesbr[:], dram["c_ones_row_bf"], s_ld2)
    p.dma("sync", pek_f, rel_k[0:128, :], s_ld2)
    p.dma("sync", pkrow_f, bass.AP(rel_k.tensor, 128 * 64, [[64, 1], [1, 64]]), s_ld2)
    p.dma("sync", pv_f, rel_v[0:128, :], s_ld2)
    p.dma("sync", pvrow_f, bass.AP(rel_v.tensor, 128 * 64, [[64, 1], [1, 64]]), s_ld2)
    ld2_tok = s_ld2.n

    p.wait("vector", s_ve, tok_prev_v)
    p.wait("vector", s_ld2, ld2_tok)
    p.drain("vector")
    zv_tok = p.op("vector", lambda q: q.memset(zbuf[:], 0.0), s_ve)
    p.drain("vector")
    bq_tok = p.op("vector", lambda q: q.tensor_scalar(out=bqkvc[:, 0:8], in0=bqkvc[:, 0:8],
                                                      scalar1=0.125, scalar2=None, op0=OP.mult), s_ve)
    p.wait("sync", s_ve, zv_tok)
    for par in range(2):
        for r in range(16):
            p.dma("sync", bass.AP(E2[par].tensor, r * 128 * 640, [[640, 128], [1, 128]]),
                  zbuf[:, 0:128], s_ew[par])
            p.dma("sync", bass.AP(E2[par].tensor, r * 128 * 640 + 512, [[640, 128], [1, 128]]),
                  zbuf[:, 0:128], s_ew[par])
        p.dma("sync", bass.AP(E2[par].tensor, 128, [[640, 128], [1, 128]]),
              zbuf[:, 0:128], s_ew[par])
        p.dma("sync", bass.AP(E2[par].tensor, 128 * 15 * 640 + 384, [[640, 128], [1, 128]]),
              zbuf[:, 0:128], s_ew[par])
    ez_tok = [s_ew[0].n, s_ew[1].n]

    p.wait("tensor", s_ve, tok_prev_v)
    p.wait("tensor", s_ld2, ld2_tok)
    p.op("tensor", lambda q: q.transpose(pm[0:64, 0:128], pek_f, identf[:]), None)
    p.op("tensor", lambda q: q.transpose(pm[0:64, 128:129], pkrow_f, identf[0:1, 0:1]), s_tr)
    tok_pek = p.sems["s_tr"].n
    p.wait("vector", s_tr, tok_pek)
    p.drain("vector")
    p.op("vector", lambda q: q.tensor_copy(out=pekT[:, 0:128], in_=pm[0:64, 0:128]))
    p.op("vector", lambda q: q.tensor_copy(out=pekT[:, 128:129], in_=pm[0:64, 128:129]))
    p.op("vector", lambda q: q.tensor_copy(out=pvb[:], in_=pv_f))
    p.op("vector", lambda q: q.tensor_copy(out=pvb2[:], in_=pvrow_f), s_ve)
    tok_pvb = p.sems["s_ve"].n
    p.drain("vector")
    p.op("vector", lambda q: q.tensor_copy(out=pekE[:, 0:192], in_=pekT[:, 0:1].to_broadcast((64, 192))))
    p.op("vector", lambda q: q.tensor_copy(out=pekE[:, 192:321], in_=pekT[:, 0:129]))
    p.op("vector", lambda q: q.tensor_copy(out=pekE[:, 321:512], in_=pekT[:, 128:129].to_broadcast((64, 191))), s_ve)
    tok_peke = p.sems["s_ve"].n
    p.drain("vector")
    p.op("vector", lambda q: q.memset(qer[64:128, 0:T], 0.0))
    p.op("vector", lambda q: q.memset(wj[1][64:128, 0:2048], 0.0))
    p.op("vector", lambda q: q.memset(wj[0][64:128, 2048:2560], 0.0), s_ve)
    tok_zpad = p.sems["s_ve"].n
    p.wait("tensor", s_ve, tok_pvb)
    p.op("tensor", lambda q: q.matmul(pb[0:128, 0:64], onesbr[:], pvb[0:1, 0:64], start=True, stop=True), None)
    p.op("tensor", lambda q: q.matmul(pb[0:128, 64:128], onesbr[:], pvb2[0:1, 0:64], start=True, stop=True), s_tr)
    tok_pvbc = p.sems["s_tr"].n
    p.wait("vector", s_tr, tok_pvbc)
    p.drain("vector")
    p.op("vector", lambda q: q.tensor_copy(out=pvE[0], in_=pb[0:128, 0:64]))
    p.op("vector", lambda q: q.tensor_copy(out=pvE[1][0:64, 0:64], in_=pb[0:64, 0:64]))
    p.op("vector", lambda q: q.tensor_copy(out=pvE[1][64:128, 0:64], in_=pvb[0:64, 0:64]))
    p.op("vector", lambda q: q.tensor_copy(out=pvE[2][0:64, 0:64], in_=pvb[64:128, 0:64]))
    p.op("vector", lambda q: q.tensor_copy(out=pvE[2][64:128, 0:64], in_=pb[64:128, 64:128]))
    p.op("vector", lambda q: q.tensor_copy(out=pvE[3], in_=pb[0:128, 64:128]), s_ve)
    tok_pve = p.sems["s_ve"].n
    p.drain("vector")
    for d in range(NDT):
        p.op("vector", lambda q, dd=d: q.memset(pksel0[dd], 0.0))
        p.op("vector", lambda q, dd=d: q.memset(pksel1[dd], 0.0))
    p.drain("vector")
    for d in range(NDT):
        p.op("vector", lambda q, dd=d: q.tensor_copy(out=pksel0[dd][0:64, 2 * dd:2 * dd + 1], in_=pekT[:, 0:1]))
        p.op("vector", lambda q, dd=d: q.tensor_copy(out=pksel0[dd][64:128, 2 * dd + 1:2 * dd + 2], in_=pekT[:, 0:1]))
        p.op("vector", lambda q, dd=d: q.tensor_copy(out=pksel1[dd][0:64, 2 * dd:2 * dd + 1], in_=pekT[:, 128:129]))
        p.op("vector", lambda q, dd=d: q.tensor_copy(out=pksel1[dd][64:128, 2 * dd + 1:2 * dd + 2], in_=pekT[:, 128:129]),
             s_ve if d == NDT - 1 else None)
    tok_sel = p.sems["s_ve"].n
    p.wait("vector", s_ge, tok_prev_g)
    p.drain("vector")
    tok_keb1 = p.op("vector", lambda q: q.memset(KEBt[64:66, 0:T], 1.0), s_ve)

    def load_wq(dd):
        dst = wqb[dd % 2]
        srcap = bass.AP(W_qkv.tensor, 128 * dd, [[3072, 128], [3072 * 128, 8], [1, 128]])
        p.dma("gpsimd", dst[:, 0:1024], srcap, s_wq[dd % 2])

    p.wait("gpsimd", s_xsp, s_xsp.n)
    p.wait("gpsimd", s_ve, tok_prev_v)
    load_wq(0)
    load_wq(1)

    p.wait("tensor", s_xsp, s_xsp.n)
    p.wait("tensor", s_ve, tok_pvb)
    evq_tok = 0
    for dd in range(16):
        p.wait("tensor", s_wq[dd % 2], 16 * (dd // 2 + 1))
        if evq_tok:
            p.wait("tensor", s_se, evq_tok)
        for nb in range(NB):
            for dt in range(NDT):
                lhsT = wqb[dd % 2][:, dt * 128:(dt + 1) * 128]
                rhs = xb[dt][:, nb * 512:(nb + 1) * 512]
                p.op("tensor", lambda q, o=pm[:, nb * 512:(nb + 1) * 512], l=lhsT, r=rhs,
                     st=(dt == 0), sp=(dt == NDT - 1):
                     q.matmul(o, l, r, start=st, stop=sp),
                     s_tr if (dt == NDT - 1 and nb == NB - 1) else None)
        mmtok = p.sems["s_tr"].n
        p.wait("scalar", s_tr, mmtok)
        if dd == 0:
            p.wait("scalar", s_xsp, s_xsp.n)
        sc = 0.125 if dd < 8 else 1.0
        for nb in range(NB):
            p.op("scalar", lambda q, o=qkT[dd][:, nb * 512:(nb + 1) * 512],
                 i_=pm[:, nb * 512:(nb + 1) * 512], b=bqkvc[:, dd:dd + 1], s=sc:
                 q.activation(out=o, in_=i_, func=AF.Identity, bias=b, scale=s),
                 s_se if nb == NB - 1 else None)
        evq_tok = p.sems["s_se"].n
        if dd + 2 < 16:
            p.wait("gpsimd", s_tr, mmtok)
            load_wq(dd + 2)

    def load_wv(dt):
        p.dma("gpsimd", wvb[dt % 2][:], W_qkv[128 * dt:128 * (dt + 1), 2048:3072], s_wv[dt % 2])

    p.wait("tensor", s_se, evq_tok)
    p.wait("tensor", s_ve, tok_pve)
    p.wait("tensor", s_gd, p.sems["s_gd"].n)
    ev_tok = 0
    dt_tok = {}
    nwv = [0, 0]
    last_mm_tok = p.sems["s_tr"].n
    for g in range(4):
        p.wait("gpsimd", s_tr, last_mm_tok)
        load_wv(0)
        load_wv(1)
        nwv[0] += 16
        nwv[1] += 16
        if ev_tok:
            p.wait("tensor", s_ve, ev_tok)
        for dt in range(NDT):
            p.wait("tensor", s_wv[dt % 2], nwv[dt % 2])
            for ti in range(4):
                tt = g * 4 + ti
                lhsT = xb[dt][:, tt * 128:(tt + 1) * 128]
                psA = pm if ti < 2 else pb
                off = (ti % 2) * 1024
                for half in range(2):
                    p.op("tensor", lambda q, o=psA[:, off + half * 512:off + (half + 1) * 512],
                         l=lhsT, r=wvb[dt % 2][:, half * 512:(half + 1) * 512], st=(dt == 0):
                         q.matmul(o, l, r, start=st, stop=False),
                         s_tr if (ti == 3 and half == 1) else None)
            dt_tok[(g, dt)] = p.sems["s_tr"].n
            if dt + 2 < NDT:
                p.wait("gpsimd", s_tr, dt_tok[(g, dt)])
                load_wv(dt + 2)
                nwv[dt % 2] += 16
        for ti in range(4):
            tt = g * 4 + ti
            psA = pm if ti < 2 else pb
            off = (ti % 2) * 1024
            for half in range(2):
                p.op("tensor", lambda q, o=psA[:, off + half * 512:off + (half + 1) * 512],
                     r=bvrow[0:1, half * 512:(half + 1) * 512]:
                     q.matmul(o, onesbr[:], r, start=False, stop=True),
                     s_tr if (ti == 3 and half == 1) else None)
        last_mm_tok = p.sems["s_tr"].n
        vtok = last_mm_tok
        p.wait("vector", s_tr, vtok)
        p.drain("vector")
        for ti in range(4):
            tt = g * 4 + ti
            psA = pm if ti < 2 else pb
            off = (ti % 2) * 1024
            for h in range(NH):
                p.op("vector", lambda q, t=tt, hh=h, ps2=psA, o2=off: q.tensor_copy(
                    out=VE[t][:, 65 * hh:65 * hh + 64], in_=ps2[:, o2 + hh * 64:o2 + (hh + 1) * 64]),
                    s_ve if (ti == 3 and h == NH - 1) else None)
        ev_tok = p.sems["s_ve"].n
    for tt in range(NTT):
        for h in range(NH):
            p.op("vector", lambda q, t=tt, hh=h: q.tensor_copy(
                out=VE[t][:, 65 * hh + 64:65 * hh + 65], in_=onesbc[:]),
                s_ve if (tt == NTT - 1 and h == NH - 1) else None)
    tok_ve = p.sems["s_ve"].n

    for (base, ps) in ((0, pm), (8, pb)):
        for nb in range(NB):
            for dt in range(NDT):
                p.op("vector", lambda q, d=dt, b=base, n=nb: q.tensor_tensor(
                    out=bfull[:, 0:512], in0=qkT[b + d][:, n * 512:(n + 1) * 512],
                    in1=qkT[b + d][:, n * 512:(n + 1) * 512], op=OP.mult), s_ve)
                sq_tok = p.sems["s_ve"].n
                p.wait("tensor", s_ve, sq_tok)
                p.op("tensor", lambda q, o=ps[0:16, nb * 512:(nb + 1) * 512], d=dt:
                     q.matmul(o, selq[:, 16 * d:16 * (d + 1)], bfull[:, 0:512],
                              start=(d == 0), stop=(d == NDT - 1)), s_tr)
                p.wait("vector", s_tr, p.sems["s_tr"].n)
    nrm_tok = p.sems["s_tr"].n
    p.wait("scalar", s_tr, nrm_tok)
    p.op("scalar", lambda q: q.activation(out=pm[32:48, 0:T], in_=pm[0:16, 0:T], func=AF.Sqrt), None)
    p.op("scalar", lambda q: q.activation(out=pb[32:48, 0:T], in_=pb[0:16, 0:T], func=AF.Sqrt), s_se)
    sq2_tok = p.sems["s_se"].n
    p.wait("vector", s_se, sq2_tok)
    p.drain("vector")
    p.op("vector", lambda q: q.tensor_reduce(out=kmx.rearrange("p (a o) -> p a o", a=1),
                                             in_=pb[32:48, 0:T].rearrange("p (a j) -> p a j", a=1),
                                             axis=AX.X, op=OP.max), s_ve)
    kmx_tok = p.sems["s_ve"].n
    p.drain("vector")
    p.op("vector", lambda q: q.tensor_tensor(out=btT[0:64, 0:129], in0=pekT[:, 0:129],
                                             in1=pekT[:, 0:129], op=OP.mult), s_ve)
    pks_tok = p.sems["s_ve"].n
    p.wait("tensor", s_ve, pks_tok)
    p.op("tensor", lambda q: q.matmul(pb[0:1, 1536:1665], onesbc[0:64, 0:1], btT[0:64, 0:129],
                                      start=True, stop=True), s_tr)
    pkn_tok = p.sems["s_tr"].n
    p.wait("scalar", s_tr, pkn_tok)
    p.op("scalar", lambda q: q.activation(out=srow[0:1, 0:129], in_=pb[0:1, 1536:1665], func=AF.Sqrt), s_se)
    pkn2_tok = p.sems["s_se"].n
    p.wait("vector", s_se, pkn2_tok)
    p.drain("vector")
    p.op("vector", lambda q: q.tensor_reduce(out=pkms.rearrange("p (a o) -> p a o", a=1),
                                             in_=srow[0:1, 0:129].rearrange("p (a j) -> p a j", a=1),
                                             axis=AX.X, op=OP.max), s_ve)
    pkm_tok = p.sems["s_ve"].n
    p.wait("tensor", s_ve, pkm_tok)
    p.op("tensor", lambda q: q.matmul(pb[0:16, 1792:1793], ones_r[0:1, 0:16], pkms,
                                      start=True, stop=True), s_tr)
    pkb_tok = p.sems["s_tr"].n
    p.wait("vector", s_tr, pkb_tok)
    p.drain("vector")
    p.op("vector", lambda q: q.tensor_tensor(out=kpc, in0=kmx, in1=pb[0:16, 1792:1793],
                                             op=OP.add), s_ve)
    p.drain("vector")
    m16_tok = p.op("vector", lambda q: q.tensor_scalar(out=MQS, in0=pm[32:48, 0:T], scalar1=kpc,
                                                       scalar2=-1.0, op0=OP.mult, op1=OP.mult), s_ve)
    p.wait("sync", s_ve, m16_tok)
    p.dma("sync", bass.AP(scm.tensor, 0, [[T, 16], [1, T]]), MQS, s_ld2)
    p.wait("tensor", s_ve, m16_tok)
    p.wait("tensor", s_ve, tok_sel)
    for (sel, ps) in ((pksel0, pm), (pksel1, pb)):
        for nb in range(NB):
            for d in range(NDT):
                p.op("tensor", lambda q, o=ps[0:16, nb * 512:(nb + 1) * 512], dd=d, s=sel:
                     q.matmul(o, s[dd][:], qkT[dd][:, nb * 512:(nb + 1) * 512],
                              start=(dd == 0), stop=(dd == NDT - 1)),
                     s_tr if d == NDT - 1 else None)
    qs_tok = p.sems["s_tr"].n
    p.wait("scalar", s_tr, qs_tok)
    p.wait("scalar", s_ld2, s_ld2.n)
    qs0_tok = p.op("scalar", lambda q: q.activation(out=MQS, in_=pm[0:16, 0:T], func=AF.Copy), s_se)
    p.wait("sync", s_se, qs0_tok)
    p.dma("sync", bass.AP(scm.tensor, 16 * T, [[T, 16], [1, T]]), MQS, s_ld2)
    p.wait("scalar", s_ld2, s_ld2.n)
    qs1_tok = p.op("scalar", lambda q: q.activation(out=MQS, in_=pb[0:16, 0:T], func=AF.Copy), s_se)
    p.wait("sync", s_se, qs1_tok)
    p.dma("sync", bass.AP(scm.tensor, 32 * T, [[T, 16], [1, T]]), MQS, s_ld2)
    qs2_tok = p.sems["s_se"].n

    p.wait("vector", s_se, qs2_tok)
    p.wait("vector", s_ve, tok_ve)
    p.wait("vector", s_ve, tok_keb1)
    prev_pe_done = 0
    last_sc = 0
    er_need = [0, 0]
    last_skq_use = 0
    prev_exp_tok = [0, 0]
    hb_tok = 0
    stA = [xb[4 + q][:, 1040:1424] for q in range(4)]
    stB = [xb[4 + q][:, 1424:1936] for q in range(4)]
    stC = [u1b[:, 3120:3632], u2b[:, 3120:3632], u3b[:, 3120:3632], x1bf[0:128, 694:1206]]
    pchunk = [pb[:, 512:1024].bitcast(BF16), pb[:, 1024:1536].bitcast(BF16)]

    def bchunk(c):
        return pchunk[c // 8][:, (c % 8) * 128:(c % 8 + 1) * 128]

    st2 = {"pend": None, "be": 0, "hb": 0, "ctxfree": [0, 0]}

    def b2_tail():
        pd = st2["pend"]
        if pd is None:
            return
        qbp, ctxp, rbcp, hp, parp, ern, fep = pd
        p.wait("tensor", s_er[parp], ern)
        p.wait("tensor", s_se, fep)
        for qtl in range(4):
            for ws in range(4):
                p.op("tensor", lambda q, o=bchunk(qtl * 4 + ws),
                     i_=stB[qtl][:, ws * 128:(ws + 1) * 128], idb=B["identb"]:
                     q.matmul(o, i_, idb[:], is_transpose=True, start=True, stop=True),
                     s_bt if (qtl == 3 and ws == 3) else None)
        bt2 = p.sems["s_bt"].n
        p.wait("scalar", s_bt, bt2)
        for qtl in range(4):
            p.op("scalar", lambda q, o=stC[qtl],
                 i_=pchunk[qtl // 2][:, (qtl % 2) * 512:(qtl % 2 + 1) * 512]:
                 q.activation(out=o, in_=i_, func=AF.Copy),
                 s_se if qtl == 3 else None)
        st2["be"] = p.sems["s_se"].n
        p.op("scalar", lambda q: q.activation(out=sab[0:1, 0:512], in_=pb[0:1, 0:512], func=AF.Copy), None)
        p.op("scalar", lambda q: q.activation(out=sab[0:1, 512:1024], in_=pb[0:1, 1536:2048], func=AF.Copy), s_se)
        sat_tok = p.sems["s_se"].n
        p.wait("tensor", s_se, st2["be"])
        for qtl in range(4):
            qtoff = qtl * 128
            for ws in range(4):
                p.op("tensor", lambda q, o=ctxp[0:64, qtoff:qtoff + 128], w=ws, s2=stC[qtl]:
                     q.matmul(o, pvE[w], s2[:, w * 128:(w + 1) * 128], start=False, stop=False), None)
        p.wait("tensor", s_se, sat_tok)
        p.op("tensor", lambda q, o=ctxp: q.matmul(o[0:64, 0:512], pvb[0:1, 0:64], sab[0:1, 0:512], start=False, stop=False), None)
        p.op("tensor", lambda q, o=ctxp: q.matmul(o[0:64, 0:512], pvb2[0:1, 0:64], sab[0:1, 512:1024], start=False, stop=False), None)
        p.op("tensor", lambda q, o=ctxp: q.matmul(o[0:65, 0:512], zbuf[0:1, 0:65], MQS[0:1, 0:512], start=False, stop=True), s_ac)
        a_close = p.sems["s_ac"].n
        p.wait("vector", s_ac, a_close)
        p.drain("vector")
        p.op("vector", lambda q, c2=ctxp: q.reciprocal(out=srow[0:1, 0:512], in_=c2[64:65, 0:512]))
        p.drain("vector")
        rb_tok = p.op("vector", lambda q: q.tensor_copy(out=rbrow[:], in_=srow[0:1, 0:512]), s_ve)
        p.wait("tensor", s_ve, rb_tok)
        p.op("tensor", lambda q, o=rbcp: q.matmul(o, onesbr[0:1, 0:64], rbrow[:], start=True, stop=True), s_rbc)
        rbc_tok = p.sems["s_rbc"].n
        p.drain("vector")
        p.op("vector", lambda q, c2=ctxp: q.tensor_copy(out=ccf[:, 0:512], in_=c2[0:64, 0:512]))
        p.wait("vector", s_rbc, rbc_tok)
        p.drain("vector")
        st2["hb"] = p.op("vector", lambda q, hh=hp, b0=qbp * 512, r2=rbcp: q.tensor_tensor(
            out=hb[hh // 2][64 * (hh % 2):64 * (hh % 2) + 64, b0:b0 + 512],
            in0=ccf[:, 0:512], in1=r2, op=OP.mult), s_ve)
        st2["ctxfree"][qbp % 2] = st2["hb"]
        st2["pend"] = None

    for h in range(NH):
        par = h % 2
        if last_sc:
            p.wait("vector", s_tr, last_sc)
        p.drain("vector")
        p.op("vector", lambda q, hh=h: q.tensor_copy(
            out=KEBt[0:64, 0:T], in_=qkT[8 + hh // 2][64 * (hh % 2):64 * (hh % 2) + 64, 0:T]))
        p.op("vector", lambda q, hh=h: q.tensor_copy(
            out=QELt[0:64, 0:T], in_=qkT[hh // 2][64 * (hh % 2):64 * (hh % 2) + 64, 0:T]))
        cp_tok = p.op("vector", lambda q, hh=h: q.tensor_copy(
            out=qer[0:64, 0:T], in_=qkT[hh // 2][64 * (hh % 2):64 * (hh % 2) + 64, 0:T]), s_ve)
        p.wait("tensor", s_ve, cp_tok)
        if h >= 2:
            p.wait("tensor", s_qr[par], s_qr[par].n)
        p.wait("scalar", s_tr, last_skq_use)
        p.wait("tensor", s_se, p.sems["s_se"].n)
        qr_ev = [0, 0]
        qw_done = [0, 0]
        for qt in range(NTT):
            pp = qt % 2
            bank = pb[:, 512 + pp * 512:1024 + pp * 512]
            if qr_ev[pp]:
                p.wait("tensor", s_se, qr_ev[pp])
            p.op("tensor", lambda q, o=bank, t=qt:
                 q.matmul(o, QELt[0:128, 128 * t:128 * (t + 1)],
                          wj[0][0:128, 2048:2560], start=True, stop=True), s_tr)
            t_tok = p.sems["s_tr"].n
            p.wait("scalar", s_tr, t_tok)
            if qw_done[pp]:
                p.wait("scalar", s_qw[par], s_qw[par].n)
            p.op("scalar", lambda q, o=qrst[pp], i_=bank: q.activation(out=o, in_=i_, func=AF.Copy), s_se)
            qr_ev[pp] = p.sems["s_se"].n
            p.wait("scalar", s_se, qr_ev[pp])
            p.dma("scalar", bass.AP(QRE[par].tensor, 128 * qt * 512, [[512, 128], [1, 512]]),
                  qrst[pp], s_qw[par])
            qw_done[pp] = s_qw[par].n
        qw_tok = s_qw[par].n
        p.wait("tensor", s_se, p.sems["s_se"].n)
        p.wait("sync", s_ve, cp_tok)
        p.wait("sync", s_ld2, s_ld2.n)
        p.wait("sync", s_tr, p.sems["s_tr"].n)
        p.dma("sync", QELt[64:65, 0:T], bass.AP(scm.tensor, h * T, [[T, 1], [1, T]]), s_qe[par])
        p.dma("sync", QELt[65:66, 0:T], bass.AP(scm.tensor, (16 + h) * T, [[T, 1], [1, T]]), s_qe[par])
        p.dma("sync", qer[64:65, 0:T], bass.AP(scm.tensor, h * T, [[T, 1], [1, T]]), s_qe[par])
        p.dma("sync", qer[65:66, 0:T], bass.AP(scm.tensor, (32 + h) * T, [[T, 1], [1, T]]), s_qe[par])
        qe_tok = s_qe[par].n

        tstg = [skq[:, 0:128], skq[:, 128:256], skq[:, 256:384], skq[:, 384:512],
                qrst[0][:, 0:128], qrst[0][:, 128:256], qrst[0][:, 256:384], qrst[0][:, 384:512],
                qrst[1][:, 0:128], qrst[1][:, 128:256], qrst[1][:, 256:384], qrst[1][:, 384:512]]
        for qb in range(NQB):
            qb0 = qb * 512
            cbank = 1024 + 512 * (qb % 2)
            ctxA = pm[0:66, cbank:cbank + 512]
            rbcA = pm[64:128, cbank:cbank + 512]
            tchunks = {}
            tlist = []
            for kt in range(NTT):
                k0 = 128 * kt
                b0 = max(qb0, k0 - 64)
                b1 = min(qb0 + 512, k0 + 192)
                if b0 < b1:
                    c0 = b0
                    while c0 < b1:
                        cw = min(128, b1 - c0)
                        tchunks[(kt, c0)] = (len(tlist), cw)
                        tlist.append((kt, c0, cw))
                        c0 += cw
            assert len(tlist) <= 12
            p.wait("sync", s_qw[par], qw_tok)
            p.wait("sync", s_tr, last_skq_use)
            p.wait("gpsimd", s_qw[par], qw_tok)
            p.wait("gpsimd", s_tr, last_skq_use)
            for idx, (kt, c0, cw) in enumerate(tlist):
                p.dma("sync" if idx % 2 == 0 else "gpsimd", tstg[idx][0:cw, 0:128],
                      bass.AP(QRE[par].tensor, c0 * 511 + 128 * kt + 256, [[511, cw], [1, 128]]),
                      s_qr[par] if idx % 2 == 0 else s_qrg[par])
            tload_n = s_qr[par].n
            tload_g = s_qrg[par].n
            first_t = True
            p.wait("tensor", s_qe[par], qe_tok)
            if st2["ctxfree"][qb % 2]:
                p.wait("tensor", s_ve, st2["ctxfree"][qb % 2])
            for kt in range(NTT):
                if kt == 3:
                    b2_tail()
                k0 = 128 * kt
                b0 = max(qb0, k0 - 64)
                b1 = min(qb0 + 512, k0 + 192)
                Ls = max(qb0, k0 + 192)
                Rs = min(qb0 + 512, k0 - 64)
                pp = kt % 2
                sc_ps = pm[:, pp * 512:(pp + 1) * 512]
                if prev_exp_tok[pp]:
                    p.wait("tensor", s_se, prev_exp_tok[pp])
                ops = []
                if b0 < b1:
                    c0 = b0
                    while c0 < b1:
                        cw = min(128, b1 - c0)
                        ops.append(("T", c0, cw))
                        c0 += cw
                if Rs > qb0:
                    ops.append(("R", qb0, Rs))
                if b0 < b1:
                    ops.append(("B", b0, b1))
                if Ls < qb0 + 512:
                    ops.append(("L", Ls, qb0 + 512))
                nops = len(ops)
                for oi, op_ in enumerate(ops):
                    st = (oi == 0)
                    sp = (oi == nops - 1)
                    if op_[0] == "T":
                        _, c0, cw = op_
                        if first_t:
                            p.wait("tensor", s_qr[par], tload_n)
                            p.wait("tensor", s_qrg[par], tload_g)
                            first_t = False
                        idx, _cw = tchunks[(kt, c0)]
                        p.op("tensor", lambda q, o=sc_ps[:, c0 - qb0:c0 - qb0 + cw], i_=tstg[idx][0:cw, 0:128],
                             idf=identf[0:cw, 0:cw], s1=st, s2=sp:
                             q.matmul(o, i_, idf, is_transpose=True, start=s1, stop=s2), s_tr)
                    elif op_[0] == "R":
                        _, a0, a1 = op_
                        p.op("tensor", lambda q, o=sc_ps[:, a0 - qb0:a1 - qb0], l=KEBt[0:128, k0:k0 + 128],
                             r=qer[0:128, a0:a1], s1=st, s2=sp:
                             q.matmul(o, l, r, start=s1, stop=s2), s_tr)
                    elif op_[0] == "B":
                        _, a0, a1 = op_
                        p.op("tensor", lambda q, o=sc_ps[:, a0 - qb0:a1 - qb0], l=KEBt[0:65, k0:k0 + 128],
                             r=QELt[0:65, a0:a1], s1=st, s2=sp:
                             q.matmul(o, l, r, start=s1, stop=s2), s_tr)
                    else:
                        _, a0, a1 = op_
                        p.op("tensor", lambda q, o=sc_ps[:, a0 - qb0:a1 - qb0], l=KEBt[0:128, k0:k0 + 128],
                             r=QELt[0:128, a0:a1], s1=st, s2=sp:
                             q.matmul(o, l, r, start=s1, stop=s2), s_tr)
                sc_tok = p.sems["s_tr"].n
                last_sc = sc_tok
                if any(o[0] == "T" for o in ops):
                    last_skq_use = sc_tok
                p.wait("scalar", s_tr, sc_tok)
                p.op("scalar", lambda q, o=a_sl(kt, 0, 512), i_=sc_ps:
                     q.activation(out=o, in_=i_, func=AF.Exp), s_se)
                prev_exp_tok[pp] = p.sems["s_se"].n
                p.wait("tensor", s_se, prev_exp_tok[pp])
                p.op("tensor", lambda q, o=ctxA[0:65, 0:512], l=VE[kt][:, 65 * h:65 * h + 65],
                     r=a_sl(kt, 0, 512), st=(kt == 0):
                     q.matmul(o, l, r, start=st, stop=False), None)
            p.op("tensor", lambda q: q.matmul(pb[0:1, 0:512], zbuf[0:1, 0:1], MQS[0:1, 0:512],
                                              start=True, stop=False), None)
            p.op("tensor", lambda q: q.matmul(pb[0:1, 1536:2048], zbuf[0:1, 0:1], MQS[0:1, 0:512],
                                              start=True, stop=False), None)
            for kt in range(NTT):
                k0 = 128 * kt
                Ls = max(qb0, k0 + 192)
                Rs = min(qb0 + 512, k0 - 64)
                if Ls < qb0 + 512:
                    p.op("tensor", lambda q, o=pb[0:1, Ls - qb0:512], r=a_sl(kt, Ls - qb0, 512):
                         q.matmul(o, onesbc[:], r, start=False, stop=False), None)
                if Rs > qb0:
                    p.op("tensor", lambda q, o=pb[0:1, 1536:1536 + Rs - qb0], r=a_sl(kt, 0, Rs - qb0):
                         q.matmul(o, onesbc[:], r, start=False, stop=False), None)
            p.op("tensor", lambda q: q.matmul(pb[0:1, 0:512], zbuf[0:1, 0:1], MQS[0:1, 0:512],
                                              start=False, stop=True), None)
            p.op("tensor", lambda q: q.matmul(pb[0:1, 1536:2048], zbuf[0:1, 0:1], MQS[0:1, 0:512],
                                              start=False, stop=True), s_tr)
            if st2["be"]:
                p.wait("tensor", s_se, st2["be"])
            fops = [(qtl, i, qb * 4 + qtl + i - 1) for qtl in range(4) for i in range(3)
                    if 0 <= qb * 4 + qtl + i - 1 < NTT]
            for oi, (qtl, i, kt2) in enumerate(fops):
                p.op("tensor", lambda q, o=bchunk(qtl * 4 + i),
                     i_=a_sl(kt2, qtl * 128, qtl * 128 + 128), idb=B["identb"]:
                     q.matmul(o, i_, idb[:], is_transpose=True, start=True, stop=True),
                     s_bf if oi == len(fops) - 1 else None)
            ft = p.sems["s_bf"].n
            vi = [(1 if qb * 4 + qtl == 0 else 0, 2 if qb * 4 + qtl == NTT - 1 else 3)
                  for qtl in range(4)]
            p.wait("scalar", s_bf, ft)
            p.wait("scalar", s_ew[par], s_ew[par].n)
            for qtl in range(4):
                v0, v1 = vi[qtl]
                p.op("scalar", lambda q, o=stA[qtl][:, 0:(v1 - v0) * 128],
                     i_=pchunk[qtl // 2][:, (qtl % 2 * 4 + v0) * 128:(qtl % 2 * 4 + v1) * 128]:
                     q.activation(out=o, in_=i_, func=AF.Copy),
                     s_se if qtl == 3 else None)
            fe = p.sems["s_se"].n
            p.wait("sync", s_se, fe)
            p.wait("scalar", s_se, fe)
            if h >= 2:
                p.wait("sync", s_er[par], er_need[par])
                p.wait("scalar", s_er[par], er_need[par])
            for qtl in range(4):
                qt = qb * 4 + qtl
                v0, v1 = vi[qtl]
                p.dma("sync" if qtl % 2 == 0 else "scalar",
                      bass.AP(E2[par].tensor, 128 * qt * 640 + 128 + v0 * 128,
                              [[640, 128], [1, (v1 - v0) * 128]]),
                      stA[qtl][:, 0:(v1 - v0) * 128], s_ew[par])
            p.wait("sync", s_ew[par], s_ew[par].n)
            for qtl in range(4):
                qt = qb * 4 + qtl
                p.dma("sync", stB[qtl], bass.AP(E2[par].tensor, 128 * qt * 640,
                                                [[641, 128], [1, 512]]), s_er[par])
            er_need[par] = s_er[par].n
            st2["pend"] = (qb, ctxA, rbcA, h, par, s_er[par].n, fe)
        p.op("tensor", lambda q: q.nop(), s_tr)
        prev_pe_done = p.sems["s_tr"].n

    b2_tail()
    hb_tok = st2["hb"]
    p.op("tensor", lambda q: q.nop(), s_tr)
    prev_pe_done = p.sems["s_tr"].n

    p.wait("sync", s_tr, prev_pe_done)
    for j in range(NDT):
        p.dma("sync", xr[j][:, 0:T], bass.AP(xsp.tensor, j * 128 * T, [[T, 128], [1, T]]), s_xre)
    p.wait("gpsimd", s_tr, prev_pe_done)
    for dt in range(NDT):
        p.dma("gpsimd", VE[dt][:, 0:1024], W_out[128 * dt:128 * (dt + 1), :], s_wo)
    p.wait("tensor", s_wo, s_wo.n)
    p.wait("tensor", s_xre, s_xre.n)
    p.wait("tensor", s_ve, hb_tok)
    r_tok = [0] * NDT
    for do in range(NDT):
        ps = pm if do % 2 == 0 else pb
        if do >= 2:
            p.wait("tensor", s_ve, r_tok[do - 2])
        for nb in range(NB):
            for dt in range(NDT):
                p.op("tensor", lambda q, o=ps[:, nb * 512:(nb + 1) * 512], d=dt, dd=do, n=nb:
                     q.matmul(o, VE[d][:, dd * 128:(dd + 1) * 128], hb[d][:, n * 512:(n + 1) * 512],
                              start=(d == 0), stop=(d == NDT - 1)),
                     s_tr if (dt == NDT - 1 and nb == NB - 1) else None)
        mmtok = p.sems["s_tr"].n
        p.wait("vector", s_tr, mmtok)
        p.drain("vector")
        for nb in range(NB):
            p.op("vector", lambda q, o=xr[do][:, nb * 512:(nb + 1) * 512],
                 i0=ps[:, nb * 512:(nb + 1) * 512], sc=boc[:, do:do + 1],
                 i1=xr[do][:, nb * 512:(nb + 1) * 512]:
                 q.scalar_tensor_tensor(out=o, in0=i0, scalar=sc, in1=i1,
                                        op0=OP.add, op1=OP.add),
                 s_ve if nb == NB - 1 else None)
        r_tok[do] = p.sems["s_ve"].n

    build_ln(p, B, xr, xr, xb, wait_tok=r_tok[-1])
    B["tok_ln2_v"] = p.sems["s_ve"].n
    B["tok_ln2_g"] = p.sems["s_ge"].n


LH3 = 8
NW3 = 4
NHT3 = 22


def build_block3(p, es, B):
    xr, xb, hb = B["xr"], B["xb"], B["hb"]
    U1, U2, U3 = B["U1"], B["U2"], B["U3"]
    pm, pb = B["pm"], B["pb"]
    dram = B["dram"]
    wj, wm = B["wj"], B["wm"]
    xin = B["xin"]
    vex = B["vex"]
    identb = B["identb"]
    onesbc, onesbr = B["onesbc"], B["onesbr"]
    zbuf, bfull = B["zbuf"], B["bfull"]
    qert = B["qer_t"]
    gext = B["gext"]
    tbk = gext
    cnL, cnR = B["cnL"], B["cnR"]
    sel8h, selfq = B["sel8h"], B["selfq"]
    cm = B["cmisc"]
    b1c = B["b1c"]
    amx = B["amaxc"]

    s_tr = p.sem("s_tr")
    s_ve = p.sem("s_ve")
    s_se = p.sem("s_se")
    s_ge = p.sem("s_ge")
    s_l3 = p.sem("s_l3")
    s_w3 = [p.sem("s_w30"), p.sem("s_w31")]
    s_wo3 = p.sem("s_wo3")
    s_mb = p.sem("s_mb")
    s_mg = p.sem("s_mg")
    s_mb2 = p.sem("s_mb2")
    s_mg2 = p.sem("s_mg2")
    s_tg = p.sem("s_tg")
    s_w2 = [p.sem("s_w20"), p.sem("s_w21")]
    s_x3 = p.sem("s_x3")
    s_xr3 = [p.sem("s_xr3a"), p.sem("s_xr3b"), p.sem("s_xr3c")]

    u1b = U1[:, 0:T].bitcast(BF16)
    u2b = U2[:, 0:T].bitcast(BF16)
    u3b = U3[:, 0:T].bitcast(BF16)
    x0b = xin[0][:, 0:D].bitcast(BF16)
    x1b = xin[1][:, 0:D].bitcast(BF16)

    QKV = [hb[j][:, 0:T] for j in range(8)] + \
          [u1b[:, 0:T], u1b[:, T:2 * T], u2b[:, 0:T], u2b[:, T:2 * T]]
    hsh = [xb[j][:, 0:T] for j in range(4)] + \
          [u3b[:, 0:T], u3b[:, T:2 * T], x0b[0:128, 0:T], x1b[0:128, 0:T]]
    T1q = wj[0][:, 0:T]
    T2q = wj[1][:, 0:T]
    T1k = qert[:, 0:T]
    T2k = tbk[:, 0:T]
    VEL = [wj[0][:, c * 520:(c + 1) * 520] for c in range(3)] + \
          [wj[1][:, c * 520:(c + 1) * 520] for c in range(3)] + \
          [qert[:, c * 520:(c + 1) * 520] for c in range(3)] + \
          [gext[:, c * 520:(c + 1) * 520] for c in range(3)] + \
          [vex[0][:, 0:520], vex[0][:, 520:1040], vex[1][:, 0:520], vex[1][:, 520:1040]]
    assert len(VEL) == 16
    gg = [u1b[:, 0:T], u1b[:, T:2 * T], u2b[:, 0:T], u2b[:, T:2 * T]] + \
         [hb[j][:, 0:T] for j in range(8)] + \
         [xb[j][:, 0:T] for j in range(4, 8)] + \
         [wj[0][:, 0:T], wj[1][:, 0:T], qert[:, 0:T], gext[:, 0:T]]
    assert len(gg) == NHT3 - 2
    gg20c = [vex[0][:, 0:512], vex[0][:, 512:1024], vex[1][:, 0:512], vex[1][:, 512:1024]]
    gg21c = [wj[0][0:42, 2048:2560], wj[0][0:42, 2560:3072],
             wj[1][0:42, 2048:2560], wj[1][0:42, 2560:3072]]

    def ggsl(k, nb):
        if k < 20:
            return gg[k][0:(128 if k < 21 else 42), nb * 512:(nb + 1) * 512]
        if k == 20:
            return gg20c[nb]
        return gg21c[nb]

    HLq = (u3b[0:8, 0:T], u3b[0:8, T:2 * T])
    HLk = (x0b[0:8, 0:T], x1b[0:8, 0:T])
    wqb = [wm[6], wm[7]]
    wodst = [wm[0], wm[5], wm[1], wm[3]]
    w2slabs = [(wm[0], wm[5], wm[3]), (wm[6], wm[7], wm[1])]
    rbrow3 = wm[4][0:1, 0:512]
    ccrow = wm[2][:, 0:D].bitcast(F32)[0:1, 0:512]
    ctxs = wm[2][:, 0:D].bitcast(F32)[0:64, 0:512]
    mu_bf = u2b[0:1, 0:T]
    rstd_bf = u2b[0:1, T:2 * T]
    gbc, bbc, b2c = B["dbc"], B["bmc"], B["fbc"]
    qsc = amx
    scb = dram["sc_amax"]

    def asl(kt):
        return xb[kt // 4][:, (kt % 4) * 512:((kt % 4) + 1) * 512]

    tok_v = B["tok_ln2_v"]
    tok_g = B["tok_ln2_g"]

    p.wait("sync", s_ve, tok_v)
    p.wait("sync", s_ge, tok_g)
    p.wait("vector", s_tr, p.sems["s_tr"].n)
    p.wait("gpsimd", s_tr, p.sems["s_tr"].n)
    p.wait("gpsimd", s_ve, tok_v)
    p.wait("scalar", s_ge, tok_g)
    p.wait("tensor", s_ve, tok_v)
    p.wait("tensor", s_ge, tok_g)

    xsp3 = dram["sc_xspill"]
    for j in range(NDT):
        p.dma("sync", bass.AP(xsp3.tensor, j * 128 * T, [[T, 128], [1, T]]),
              xr[j][:, 0:T], s_x3)
    x3_spilled = s_x3.n

    carry = {"w2mm": 0, "w2ve": 0}

    for L in range(2):
        if L == 1:
            p.wait("sync", s_tr, carry["w2mm"])
            p.wait("sync", s_ve, carry["w2ve"])
            p.wait("vector", s_tr, carry["w2mm"])
            p.wait("gpsimd", s_tr, carry["w2mm"])
            p.wait("gpsimd", s_ve, carry["w2ve"])
            p.wait("scalar", s_ve, carry["w2ve"])
            p.wait("tensor", s_ve, carry["w2ve"])
        p.dma("sync", qsc[:, 0:1], col_ap(dram["l1f_qs"], L * 128), s_l3)
        p.dma("sync", qsc[:, 2:3], col_ap(dram["l1f_ks"], L * 128), s_l3)
        for j in range(NDT):
            p.dma("sync", gbc[:, j:j + 1], col_ap(dram["l1f_lng"], L * D + 128 * j), s_l3)
            p.dma("sync", bbc[:, j:j + 1], col_ap(dram["l1f_lnb"], L * D + 128 * j), s_l3)
            p.dma("sync", b2c[:, j:j + 1], col_ap(dram["l1f_b2"], L * D + 128 * j), s_l3)
        for half in range(2):
            for c in range(22):
                n = 128 if c < 21 else 42
                p.dma("sync", b1c[0:n, 22 * half + c:22 * half + c + 1],
                      col_ap(dram["l1f_b1"], L * 5460 + half * 2730 + 128 * c, n=n), s_l3)
        p.dma("sync", T1q, dram["c_cosq"][0:128, :], s_tg)
        p.dma("sync", T2q, dram["c_sinqS"][0:128, :], s_tg)
        p.dma("sync", T1k, dram["c_cosk"][0:128, :], s_tg)
        p.dma("sync", T2k, dram["c_sinkS"][0:128, :], s_tg)
        ld3 = s_l3.n
        tg3 = s_tg.n

        p.wait("vector", s_l3, ld3)
        p.wait("vector", s_tg, tg3)
        p.drain("vector")
        p.op("vector", lambda q: q.tensor_scalar(out=T1q, in0=T1q, scalar1=qsc[:, 0:1],
                                                 scalar2=8.0, op0=OP.mult, op1=OP.mult))
        p.op("vector", lambda q: q.tensor_scalar(out=T2q, in0=T2q, scalar1=qsc[:, 0:1],
                                                 scalar2=8.0, op0=OP.mult, op1=OP.mult))
        p.op("vector", lambda q: q.tensor_scalar(out=T1k, in0=T1k, scalar1=qsc[:, 2:3],
                                                 scalar2=None, op0=OP.mult))
        tab_tok = p.op("vector", lambda q: q.tensor_scalar(out=T2k, in0=T2k, scalar1=qsc[:, 2:3],
                                                           scalar2=None, op0=OP.mult), s_ve)

        p.drain("vector")
        if L == 1:
            for j in range(4):
                p.op("vector", lambda q, jj=j: q.tensor_copy(out=hsh[jj], in_=xr[jj][:, 0:T]))
        for j in range(4):
            p.op("vector", lambda q, jj=j: q.memset(hsh[4 + jj][:, 0:1], 0.0))
            p.op("vector", lambda q, jj=j: q.tensor_copy(out=hsh[4 + jj][:, 1:T],
                                                         in_=xr[4 + jj][:, 0:T - 1]),
                 s_ve if j == 3 else None)
        hsh_tok = p.sems["s_ve"].n

        wqkv_d = dram["l1f_wqkv"]

        def load_wq3(dd):
            srcap = bass.AP(wqkv_d.tensor, L * D * 1536 + 128 * dd,
                            [[1536, 128], [1536 * 128, 8], [1, 128]])
            p.dma("gpsimd", wqb[dd % 2][:, 0:1024], srcap, s_w3[dd % 2])

        bq0, bq1 = s_w3[0].n, s_w3[1].n
        load_wq3(0)
        load_wq3(1)
        p.wait("tensor", s_ve, hsh_tok)
        evq = 0
        qkv_mm_last = 0
        for dd in range(12):
            p.wait("tensor", s_w3[dd % 2], (bq0 if dd % 2 == 0 else bq1) + 16 * (dd // 2 + 1))
            if evq:
                p.wait("tensor", s_se, evq)
            for nb in range(NB):
                for dt in range(NDT):
                    p.op("tensor", lambda q, o=pm[:, nb * 512:(nb + 1) * 512],
                         l=wqb[dd % 2][:, dt * 128:(dt + 1) * 128],
                         r=hsh[dt][:, nb * 512:(nb + 1) * 512],
                         st=(dt == 0), sp=(dt == NDT - 1):
                         q.matmul(o, l, r, start=st, stop=sp),
                         s_tr if (dt == NDT - 1 and nb == NB - 1) else None)
            mmtok = p.sems["s_tr"].n
            qkv_mm_last = mmtok
            p.wait("scalar", s_tr, mmtok)
            for nb in range(NB):
                p.op("scalar", lambda q, o=QKV[dd][:, nb * 512:(nb + 1) * 512],
                     i_=pm[:, nb * 512:(nb + 1) * 512]:
                     q.activation(out=o, in_=i_, func=AF.Copy),
                     s_se if nb == NB - 1 else None)
            evq = p.sems["s_se"].n
            if dd + 2 < 12:
                p.wait("gpsimd", s_tr, mmtok)
                load_wq3(dd + 2)
        qkv_tok = evq

        p.wait("gpsimd", s_tr, qkv_mm_last)
        for dt in range(4):
            p.dma("gpsimd", wodst[dt][:, 0:1024],
                  bass.AP(dram["l1f_wo"].tensor, L * 512 * 1024 + dt * 128 * 1024,
                          [[1024, 128], [1, 1024]]), s_wo3)
        wo_ld = s_wo3.n
        w1_d = dram["l1f_w1"]
        b30, b31 = s_w3[0].n, s_w3[1].n

        def load_w1(k, half):
            M = 128 if k < 21 else 42
            off = (2730 if half == 0 else 0) + k * 128
            srcw = bass.AP(w1_d.tensor, L * D * 5460 + off,
                           [[5460, 128], [5460 * 128, 8], [1, M]])
            p.dma("gpsimd", wqb[half][:, 0:8 * M], srcw, s_w3[half])

        load_w1(0, 0)
        load_w1(0, 1)

        p.wait("vector", s_se, qkv_tok)
        p.wait("tensor", s_se, qkv_tok)
        for (b0, ps) in ((0, pm), (4, pb)):
            for nb in range(NB):
                for dt in range(4):
                    p.op("vector", lambda q, d=dt, b=b0, n=nb: q.tensor_tensor(
                        out=bfull[:, 0:512], in0=QKV[b + d][:, n * 512:(n + 1) * 512],
                        in1=QKV[b + d][:, n * 512:(n + 1) * 512], op=OP.mult), s_ve)
                    sqtok = p.sems["s_ve"].n
                    p.wait("tensor", s_ve, sqtok)
                    p.op("tensor", lambda q, o=ps[0:8, nb * 512:(nb + 1) * 512], d=dt:
                         q.matmul(o, selfq[:, 8 * d:8 * (d + 1)], bfull[:, 0:512],
                                  start=(d == 0), stop=(d == 3)), s_tr)
                    p.wait("vector", s_tr, p.sems["s_tr"].n)
        nrm_tok = p.sems["s_tr"].n
        p.wait("scalar", s_tr, nrm_tok)
        p.op("scalar", lambda q: q.activation(out=pm[32:40, 0:T], in_=pm[0:8, 0:T], func=AF.Sqrt), None)
        p.op("scalar", lambda q: q.activation(out=pb[32:40, 0:T], in_=pb[0:8, 0:T], func=AF.Sqrt), s_se)
        sq2 = p.sems["s_se"].n
        p.wait("vector", s_se, sq2)
        p.drain("vector")
        p.op("vector", lambda q: q.memset(u3b[0:128, 0:2 * T], 0.0))
        p.op("vector", lambda q: q.memset(x0b[0:128, 0:T], 0.0))
        p.op("vector", lambda q: q.memset(x1b[0:128, 0:T], 0.0))
        p.drain("vector")
        for (ps, HL) in ((pm, HLq), (pb, HLk)):
            p.drain("vector")
            p.op("vector", lambda q, ps2=ps: q.tensor_scalar(out=ps2[64:72, 0:T], in0=ps2[32:40, 0:T],
                                                             scalar1=1e-12, scalar2=None, op0=OP.max))
            p.drain("vector")
            p.op("vector", lambda q, ps2=ps: q.reciprocal(out=ps2[64:72, 0:T], in_=ps2[64:72, 0:T]))
            p.drain("vector")
            p.op("vector", lambda q, ps2=ps, hl=HL: q.tensor_copy(out=hl[0], in_=ps2[64:72, 0:T]))
            p.drain("vector")
            p.op("vector", lambda q, ps2=ps, hl=HL: q.tensor_tensor(out=hl[1], in0=ps2[64:72, 0:T],
                                                                    in1=hl[0], op=OP.subtract), s_ve)
        hl_tok = p.sems["s_ve"].n
        hlz_tok = hl_tok
        HLFq = (u3b[0:128, 0:T], u3b[0:128, T:2 * T])
        HLFk = (x0b[0:128, 0:T], x1b[0:128, 0:T])

        rotscr = xb[0]
        prodscr = xb[1]
        prev_qp = 0
        prev_s = 0
        p.wait("gpsimd", s_ve, tab_tok)
        p.wait("gpsimd", s_se, qkv_tok)
        for side in range(2):
            b0 = 4 * side
            TT1 = T1q if side == 0 else T1k
            TT2 = T2q if side == 0 else T2k
            HL = HLFq if side == 0 else HLFk
            for dt in range(4):
                src = QKV[b0 + dt]
                p.wait("tensor", s_ve, hl_tok)
                p.wait("tensor", s_ve, hlz_tok)
                if prev_qp:
                    p.wait("tensor", s_ve, prev_qp)
                for nb in range(NB):
                    p.op("tensor", lambda q, o=pm[:, nb * 512:(nb + 1) * 512], d=dt, hl=HL, n=nb:
                         q.matmul(o, sel8h[0:128, 128 * d:128 * (d + 1)],
                                  hl[0][0:128, n * 512:(n + 1) * 512], start=True, stop=False), None)
                    p.op("tensor", lambda q, o=pm[:, nb * 512:(nb + 1) * 512], d=dt, hl=HL, n=nb:
                         q.matmul(o, sel8h[0:128, 128 * d:128 * (d + 1)],
                                  hl[1][0:128, n * 512:(n + 1) * 512], start=False, stop=True),
                         s_tr if nb == NB - 1 else None)
                rsb = p.sems["s_tr"].n
                if prev_qp:
                    p.wait("gpsimd", s_ve, prev_qp)
                p.drain("gpsimd")
                p.op("gpsimd", lambda q, s2=src, t1=TT1: q.tensor_tensor(
                    out=prodscr[:, 0:T], in0=s2, in1=t1, op=OP.mult), s_ge)
                m1tok = p.sems["s_ge"].n
                p.wait("vector", s_ve, hl_tok)
                if prev_s:
                    p.wait("vector", s_ge, prev_s)
                for blk in range(3):
                    r0 = 32 * blk
                    rin = r0 + 32 if blk % 2 == 0 else r0 - 32
                    p.op("vector", lambda q, a=r0, b=rin, s2=src, t2=TT2: q.tensor_tensor(
                        out=rotscr[a:a + 32, 0:T], in0=s2[b:b + 32, 0:T],
                        in1=t2[b:b + 32, 0:T], op=OP.mult),
                        s_ve if blk == 2 else None)
                m2tok = p.sems["s_ve"].n
                p.drain("gpsimd")
                p.op("gpsimd", lambda q, s2=src, t2=TT2: q.tensor_tensor(
                    out=rotscr[96:128, 0:T], in0=s2[64:96, 0:T],
                    in1=t2[64:96, 0:T], op=OP.mult), s_ge)
                m2g = p.sems["s_ge"].n
                p.wait("gpsimd", s_ve, m2tok)
                p.drain("gpsimd")
                prev_s = p.op("gpsimd", lambda q: q.tensor_tensor(
                    out=prodscr[:, 0:T], in0=prodscr[:, 0:T], in1=rotscr[:, 0:T], op=OP.add), s_ge)
                p.wait("vector", s_ge, prev_s)
                p.wait("vector", s_tr, rsb)
                p.drain("vector")
                prev_qp = p.op("vector", lambda q, s2=src: q.tensor_tensor(
                    out=s2, in0=prodscr[:, 0:T], in1=pm[0:128, 0:T], op=OP.mult), s_ve)
        rot_tok = prev_qp
        rot_s_tok = prev_s

        p.wait("vector", s_ge, rot_s_tok)
        p.wait("tensor", s_ve, hl_tok)
        evtok = [0, 0, 0, 0]
        for vd in range(4):
            for tt in range(NTT):
                bank = tt % 4
                if evtok[bank]:
                    p.wait("tensor", s_ve, evtok[bank])
                p.op("tensor", lambda q, o=pb[:, bank * 512:(bank + 1) * 512].bitcast(BF16)[:, 0:128],
                     i_=QKV[8 + vd][:, tt * 128:(tt + 1) * 128], idb=identb:
                     q.matmul(o, i_, idb[:], is_transpose=True, start=True, stop=True), s_tr)
                ttok = p.sems["s_tr"].n
                p.wait("vector", s_tr, ttok)
                p.drain("vector")
                for hh in range(2):
                    h = 2 * vd + hh
                    p.op("vector", lambda q, t=tt, b=bank, h2=h, hh2=hh: q.tensor_copy(
                        out=VEL[t][:, 65 * h2:65 * h2 + 64],
                        in_=pb[:, b * 512:(b + 1) * 512].bitcast(BF16)[:, 64 * hh2:64 * (hh2 + 1)]),
                        s_ve if hh == 1 else None)
                evtok[bank] = p.sems["s_ve"].n
        p.drain("vector")
        for tt in range(NTT):
            for h in range(LH3):
                p.op("vector", lambda q, t=tt, h2=h: q.tensor_copy(
                    out=VEL[t][:, 65 * h2 + 64:65 * h2 + 65], in_=onesbc[:]),
                    s_ve if (tt == NTT - 1 and h == LH3 - 1) else None)
        vel_tok = p.sems["s_ve"].n

        def ke_ap(bb, r0, r1, c0, c1):
            if bb == 0:
                return wj[0][r0:r1, 2048 + c0:2048 + c1]
            return vex[2][r0:r1, c0:c1]

        def qe_ap(bb, r0, r1, c0, c1):
            return wj[1][r0:r1, 2048 + 512 * bb + c0:2048 + 512 * bb + c1]

        smb = [s_mb, s_mb2]
        smg = [s_mg, s_mg2]
        mcs = [amx[:, 8:12], amx[:, 12:16]]
        scoff = [0, 1024]
        p.drain("vector")
        p.op("vector", lambda q: q.memset(ke_ap(0, 64, 65, 0, 1024), 1.0))
        p.drain("vector")
        p.op("vector", lambda q: q.memset(ke_ap(1, 64, 65, 0, 1024), 1.0))
        p.drain("vector")
        p.op("vector", lambda q: q.memset(qe_ap(0, 64, 128, 0, 512), 0.0))
        p.drain("vector")
        ke1_tok = p.op("vector", lambda q: q.memset(qe_ap(1, 64, 128, 0, 512), 0.0), s_ve)
        p.wait("tensor", s_ve, vel_tok)
        p.wait("tensor", s_ve, ke1_tok)
        p.wait("scalar", s_ve, rot_tok)
        last_p2b = [0, 0]
        st = {"av": 0, "sel": 0, "ctx": 0, "pend": None}
        prev_exp = [0, 0]

        def attn_tail():
            pd = st["pend"]
            if pd is None:
                return
            h2, w2, fk = pd
            p.wait("tensor", s_ge, st["sel"])
            for kt in range(fk, 8):
                ktt = 4 * w2 + kt - 4
                p.op("tensor", lambda q, t=ktt, hh=h2, a=kt:
                     q.matmul(pb[0:65, 0:512], VEL[t][:, 65 * hh:65 * hh + 65], asl(a),
                              start=False, stop=(a == 7)), s_tr)
            st["av"] = p.sems["s_tr"].n
            p.wait("vector", s_tr, st["av"])
            p.op("vector", lambda q: q.reciprocal(out=ccrow, in_=pb[64:65, 0:512]))
            p.drain("vector")
            rb = p.op("vector", lambda q: q.tensor_copy(out=rbrow3, in_=ccrow), s_ve)
            p.wait("tensor", s_ve, rb)
            p.op("tensor", lambda q: q.matmul(pb[0:64, 1536:2048], onesbr[0:1, 0:64],
                                              rbrow3, start=True, stop=True), s_tr)
            rbc = p.sems["s_tr"].n
            p.wait("scalar", s_ve, rb)
            p.op("scalar", lambda q: q.activation(out=ctxs, in_=pb[0:64, 0:512],
                                                  func=AF.Copy), s_se)
            ctok = p.sems["s_se"].n
            p.wait("vector", s_tr, rbc)
            p.wait("vector", s_se, ctok)
            p.drain("vector")
            st["ctx"] = p.op("vector", lambda q, hh=h2, a=512 * w2: q.tensor_tensor(
                out=xb[4 + hh // 2][64 * (hh % 2):64 * (hh % 2) + 64, a:a + 512],
                in0=ctxs, in1=pb[0:64, 1536:2048], op=OP.mult), s_ve)
            st["pend"] = None

        iters = [(h, w) for h in range(LH3) for w in range(NW3)]
        cpq_tok = [0] * (len(iters) + 1)

        def stage_i(i2):
            if i2 >= len(iters):
                return
            h2, w2 = iters[i2]
            b2 = i2 % 2
            r0 = 64 * (h2 % 2)
            kt2 = QKV[4 + h2 // 2]
            qt2 = QKV[h2 // 2]
            if last_p2b[b2]:
                p.wait("vector", s_tr, last_p2b[b2])
            p.drain("vector")
            if w2 == 0:
                p.op("vector", lambda q, o=ke_ap(b2, 0, 64, 512, 1024), kt=kt2, rr=r0:
                     q.tensor_copy(out=o, in_=kt[rr:rr + 64, 0:512]))
            else:
                p.op("vector", lambda q, o=ke_ap(b2, 0, 64, 0, 1024), kt=kt2, rr=r0,
                     a=(w2 - 1) * 512:
                     q.tensor_copy(out=o, in_=kt[rr:rr + 64, a:a + 1024]))
            p.op("vector", lambda q, o=qe_ap(b2, 0, 64, 0, 512), qt=qt2, rr=r0, a=w2 * 512:
                 q.tensor_copy(out=o, in_=qt[rr:rr + 64, a:a + 512]))
            cpq_tok[i2] = p.op("vector", lambda q, o=qe_ap(b2, 64, 65, 0, 512):
                               q.memset(o, 0.0), s_ve)

        stage_i(0)
        for idx, (h, w) in enumerate(iters):
            b = idx % 2
            hb0 = 64 * (h % 2)
            qtile = QKV[h // 2]
            ktile = QKV[4 + h // 2]
            q0 = w * 512
            p.wait("vector", smb[b], smb[b].n)
            redtok = {}
            p.wait("tensor", s_ve, cpq_tok[idx])
            for qt in range(4):
                base = (qt % 2) * 1024
                if qt >= 2:
                    p.wait("tensor", s_ve, redtok[qt - 2])
                lq = qe_ap(b, 0, 128, qt * 128, (qt + 1) * 128)
                if w >= 1:
                    segs = [(0, 128, cnL), (128, 384, None), (512, 128, cnR)]
                    rlo = 0
                else:
                    segs = ([(512 - qt * 128, qt * 128, None)] if qt > 0 else []) + \
                           [(512, 128, cnR)]
                    rlo = 512 - qt * 128
                for si, (r0, nlen, cn) in enumerate(segs):
                    last = (si == len(segs) - 1) and cn is None
                    p.op("tensor", lambda q, o=pm[:, base + r0:base + r0 + nlen], l=lq,
                         r=ke_ap(b, 0, 128, qt * 128 + r0, qt * 128 + r0 + nlen),
                         sp=(cn is None):
                         q.matmul(o, l, r, start=True, stop=sp),
                         s_tr if last else None)
                    if cn is not None:
                        p.op("tensor", lambda q, o=pm[:, base + r0:base + r0 + nlen],
                             c2=cn, idb=identb:
                             q.matmul(o, idb[:], c2[:], start=False, stop=True),
                             s_tr if si == len(segs) - 1 else None)
                p1tok = p.sems["s_tr"].n
                p.wait("vector", s_tr, p1tok)
                p.op("vector", lambda q, bs=base, r=rlo, qt2=qt, bb=b: q.tensor_reduce(
                    out=mcs[bb][:, qt2:qt2 + 1].rearrange("p (a o) -> p a o", a=1),
                    in_=pm[:, bs + r:bs + 640].rearrange("p (a j) -> p a j", a=1),
                    axis=AX.X, op=OP.max), s_ve)
                redtok[qt] = p.sems["s_ve"].n
            p.drain("vector")
            neg_tok = p.op("vector", lambda q, bb=b: q.tensor_scalar(
                out=mcs[bb], in0=mcs[bb], scalar1=-1.0, scalar2=None, op0=OP.mult), s_ve)
            stage_i(idx + 1)
            p.wait("sync", s_ve, neg_tok)
            p.wait("sync", smg[b], smg[b].n)
            for qt in range(4):
                p.dma("sync", bass.AP(scb.tensor, scoff[b] + qt * 128, [[1, 128], [1, 1]]),
                      mcs[b][:, qt:qt + 1], smb[b])
            p.wait("gpsimd", smb[b], smb[b].n)
            if last_p2b[b]:
                p.wait("gpsimd", s_tr, last_p2b[b])
            p.dma("gpsimd", qe_ap(b, 64, 65, 0, 512),
                  bass.AP(scb.tensor, scoff[b], [[512, 1], [1, 512]]), smg[b])
            attn_tail()
            p.wait("tensor", smg[b], smg[b].n)
            if st["ctx"]:
                p.wait("tensor", s_ve, st["ctx"])
            p.op("tensor", lambda q: q.matmul(pb[0:65, 0:512], zbuf[0:1, 0:65],
                                              rbrow3, start=True, stop=False), None)
            first_kt = 4 if w == 0 else 0
            for kt in range(8):
                if kt < first_kt:
                    continue
                pp = kt % 2
                sc_ps = pb[:, 512 + pp * 512:1024 + pp * 512]
                if prev_exp[pp]:
                    p.wait("tensor", s_se, prev_exp[pp])
                p.op("tensor", lambda q, o=sc_ps, l=ke_ap(b, 0, 128, kt * 128, (kt + 1) * 128),
                     r=qe_ap(b, 0, 128, 0, 512):
                     q.matmul(o, l, r, start=True, stop=True), s_tr)
                sctok = p.sems["s_tr"].n
                if kt == 7:
                    last_p2b[b] = sctok
                p.wait("scalar", s_tr, sctok)
                if kt == first_kt:
                    if st["av"]:
                        p.wait("scalar", s_tr, st["av"])
                    if st["sel"]:
                        p.wait("scalar", s_ge, st["sel"])
                p.op("scalar", lambda q, o=asl(kt), i_=sc_ps:
                     q.activation(out=o, in_=i_, func=AF.Exp), s_se)
                prev_exp[pp] = p.sems["s_se"].n
            p.wait("gpsimd", s_se, p.sems["s_se"].n)
            p.drain("gpsimd")
            if w >= 1:
                p.op("gpsimd", lambda q: q.affine_select(
                    out=xb[0][:, 0:T], in_=xb[0][:, 0:T], pattern=[[128, 4], [-1, 512]],
                    base=0, channel_multiplier=1, compare_op=OP.is_ge, fill=0.0), None)
                p.drain("gpsimd")
            p.op("gpsimd", lambda q: q.affine_select(
                out=xb[1][:, 0:T], in_=xb[1][:, 0:T], pattern=[[-128, 4], [1, 512]],
                base=0, channel_multiplier=-1, compare_op=OP.is_ge, fill=0.0), s_ge)
            st["sel"] = p.sems["s_ge"].n
            st["pend"] = (h, w, first_kt)
        attn_tail()
        ctx_ev = st["ctx"]
        last_av = st["av"]

        p.wait("tensor", s_wo3, wo_ld)
        p.wait("tensor", s_ve, ctx_ev)
        if L == 0:
            p.wait("vector", s_x3, x3_spilled)
        r3tok = [0] * NDT
        womm_last = 0
        for do in range(NDT):
            ps = pm if do % 2 == 0 else pb
            if do >= 2:
                p.wait("tensor", s_ve, r3tok[do - 2])
            for nb in range(NB):
                for dt in range(4):
                    p.op("tensor", lambda q, o=ps[:, nb * 512:(nb + 1) * 512], d=dt, dd=do, n=nb:
                         q.matmul(o, wodst[d][:, dd * 128:(dd + 1) * 128],
                                  xb[4 + d][:, n * 512:(n + 1) * 512],
                                  start=(d == 0), stop=(d == 3)),
                         s_tr if (dt == 3 and nb == NB - 1) else None)
            womm_last = p.sems["s_tr"].n
            p.wait("vector", s_tr, womm_last)
            for nb in range(NB):
                p.op("vector", lambda q, o=xr[do][:, nb * 512:(nb + 1) * 512],
                     i0=ps[:, nb * 512:(nb + 1) * 512], i1=xr[do][:, nb * 512:(nb + 1) * 512]:
                     q.tensor_tensor(out=o, in0=i0, in1=i1, op=OP.add),
                     s_ve if nb == NB - 1 else None)
            r3tok[do] = p.sems["s_ve"].n

        p.wait("gpsimd", s_ve, r3tok[-1])
        p.wait("gpsimd", s_tr, last_av)
        p.drain("gpsimd")
        for j in range(4):
            p.op("gpsimd", lambda q, jj=j: q.tensor_copy(out=hsh[jj], in_=xr[jj][:, 0:T]),
                 s_ge if j == 3 else None)
            p.drain("gpsimd")
        h2g = p.sems["s_ge"].n
        p.wait("vector", s_ve, r3tok[-1])
        p.drain("vector")
        for j in range(4):
            p.op("vector", lambda q, jj=j: q.memset(hsh[4 + jj][:, 0:1], 0.0))
            p.op("vector", lambda q, jj=j: q.tensor_copy(out=hsh[4 + jj][:, 1:T],
                                                         in_=xr[4 + jj][:, 0:T - 1]),
                 s_ve if j == 3 else None)
        h2v = p.sems["s_ve"].n
        p.wait("tensor", s_ge, h2g)
        p.wait("tensor", s_ve, h2v)
        p.wait("vector", s_ge, h2g)
        for nb in range(NB):
            for j in range(8):
                p.op("tensor", lambda q, o=pm[0:1, nb * 512:(nb + 1) * 512], jj=j, n=nb:
                     q.matmul(o, onesbc[:], hsh[jj][:, n * 512:(n + 1) * 512],
                              start=(jj == 0), stop=(jj == 7)),
                     s_tr if j == 7 else None)
        for j in range(8):
            for nb in range(NB):
                p.op("vector", lambda q, jj=j, n=nb: q.tensor_tensor(
                    out=bfull[:, 0:512], in0=hsh[jj][:, n * 512:(n + 1) * 512],
                    in1=hsh[jj][:, n * 512:(n + 1) * 512], op=OP.mult), s_ve)
                sqtok = p.sems["s_ve"].n
                p.wait("tensor", s_ve, sqtok)
                p.op("tensor", lambda q, o=pb[0:1, nb * 512:(nb + 1) * 512], jj=j:
                     q.matmul(o, onesbc[:], bfull[:, 0:512],
                              start=(jj == 0), stop=(jj == 7)), s_tr)
                p.wait("vector", s_tr, p.sems["s_tr"].n)
        st_tok = p.sems["s_tr"].n
        p.wait("vector", s_tr, st_tok)
        p.drain("vector")
        p.op("vector", lambda q: q.tensor_scalar(out=U1[0:1, 0:T], in0=pm[0:1, 0:T],
                                                 scalar1=1.0 / D, scalar2=None, op0=OP.mult))
        p.drain("vector")
        p.op("vector", lambda q: q.tensor_scalar(out=pb[0:1, 0:T], in0=pb[0:1, 0:T],
                                                 scalar1=1.0 / D, scalar2=None, op0=OP.mult))
        p.drain("vector")
        p.op("vector", lambda q: q.tensor_tensor(out=U2[0:1, 0:T], in0=U1[0:1, 0:T],
                                                 in1=U1[0:1, 0:T], op=OP.mult))
        p.drain("vector")
        var_tok = p.op("vector", lambda q: q.tensor_tensor(out=pb[0:1, 0:T], in0=pb[0:1, 0:T],
                                                           in1=U2[0:1, 0:T], op=OP.subtract), s_ve)
        p.wait("scalar", s_ve, var_tok)
        srt = p.op("scalar", lambda q, b=cm[0:1, 1:2]: q.activation(out=pb[0:1, 0:T], in_=pb[0:1, 0:T],
                                                                    func=AF.Sqrt, bias=b), s_se)
        p.wait("vector", s_se, srt)
        p.drain("vector")
        p.op("vector", lambda q: q.reciprocal(out=pb[0:1, 0:T], in_=pb[0:1, 0:T]))
        p.drain("vector")
        p.op("vector", lambda q: q.tensor_copy(out=mu_bf, in_=U1[0:1, 0:T]))
        p.drain("vector")
        rows_tok = p.op("vector", lambda q: q.tensor_copy(out=rstd_bf, in_=pb[0:1, 0:T]), s_ve)
        p.wait("tensor", s_ve, rows_tok)
        for nb in range(NB):
            p.op("tensor", lambda q, o=pb[:, nb * 512:(nb + 1) * 512],
                 r=mu_bf[0:1, nb * 512:(nb + 1) * 512]:
                 q.matmul(o, onesbr[:], r, start=True, stop=True))
            p.op("tensor", lambda q, o=pm[:, nb * 512:(nb + 1) * 512],
                 r=rstd_bf[0:1, nb * 512:(nb + 1) * 512]:
                 q.matmul(o, onesbr[:], r, start=True, stop=True),
                 s_tr if nb == NB - 1 else None)
        bc_tok = p.sems["s_tr"].n
        p.wait("vector", s_tr, bc_tok)
        p.drain("vector")
        for j in range(8):
            p.op("vector", lambda q, jj=j: q.tensor_tensor(out=U1[:], in0=hsh[jj],
                                                           in1=pb[:, 0:T], op=OP.subtract))
            p.drain("vector")
            p.op("vector", lambda q: q.tensor_tensor(out=U1[:], in0=U1[:],
                                                     in1=pm[:, 0:T], op=OP.mult))
            p.drain("vector")
            p.op("vector", lambda q, jj=j: q.tensor_scalar(
                out=hsh[jj], in0=U1[:], scalar1=gbc[:, jj:jj + 1], scalar2=bbc[:, jj:jj + 1],
                op0=OP.mult, op1=OP.add), s_ve)
            p.drain("vector")
        ln_tok = p.sems["s_ve"].n

        w2_d = dram["l1f_w2"]
        w2cnt = {}

        def load_w2(do):
            sl = w2slabs[do % 2]
            base = L * 2730 * 1024 + do * 128
            p.dma("gpsimd", sl[0][:, 0:1024],
                  bass.AP(w2_d.tensor, base, [[1024, 128], [128 * 1024, 8], [1, 128]]), s_w2[do % 2])
            p.dma("gpsimd", sl[1][:, 0:1024],
                  bass.AP(w2_d.tensor, base + 8 * 128 * 1024,
                          [[1024, 128], [128 * 1024, 8], [1, 128]]), s_w2[do % 2])
            p.dma("gpsimd", sl[2][:, 0:640],
                  bass.AP(w2_d.tensor, base + 16 * 128 * 1024,
                          [[1024, 128], [128 * 1024, 5], [1, 128]]), s_w2[do % 2])
            p.dma("gpsimd", sl[2][0:42, 640:768],
                  bass.AP(w2_d.tensor, base + 21 * 128 * 1024, [[1024, 42], [1, 128]]), s_w2[do % 2])
            w2cnt[do] = s_w2[do % 2].n

        p.wait("gpsimd", s_tr, womm_last)
        load_w2(0)
        prevst = 0
        prevge = 0
        amm = 0
        for k in range(NHT3):
            M = 128 if k < 21 else 42
            p.wait("tensor", s_w3[0], b30 + 16 * (k + 1))
            if prevge:
                p.wait("tensor", s_se, prevge)
            if k == 0:
                p.wait("tensor", s_ve, ln_tok)
            for nb in range(NB):
                for dt in range(8):
                    p.op("tensor", lambda q, o=pm[0:M, nb * 512:(nb + 1) * 512],
                         l=wqb[0][:, dt * M:(dt + 1) * M], r=hsh[dt][:, nb * 512:(nb + 1) * 512],
                         st=(dt == 0), sp=(dt == 7):
                         q.matmul(o, l, r, start=st, stop=sp),
                         s_tr if (dt == 7 and nb == NB - 1) else None)
            gmm = p.sems["s_tr"].n
            p.wait("scalar", s_tr, gmm)
            for nb in range(NB):
                p.op("scalar", lambda q, o=ggsl(k, nb),
                     i_=pm[0:M, nb * 512:(nb + 1) * 512], b=b1c[0:M, 22 + k:23 + k]:
                     q.activation(out=o, in_=i_, func=AF.Gelu, bias=b),
                     s_se if nb == NB - 1 else None)
            prevge = p.sems["s_se"].n
            if k + 1 < NHT3:
                p.wait("gpsimd", s_tr, gmm)
                load_w1(k + 1, 0)
            p.wait("tensor", s_w3[1], b31 + 16 * (k + 1))
            if prevst:
                p.wait("tensor", s_ve, prevst)
            for nb in range(NB):
                for dt in range(8):
                    p.op("tensor", lambda q, o=pb[0:M, nb * 512:(nb + 1) * 512],
                         l=wqb[1][:, dt * M:(dt + 1) * M], r=hsh[dt][:, nb * 512:(nb + 1) * 512],
                         st=(dt == 0), sp=(dt == 7):
                         q.matmul(o, l, r, start=st, stop=sp),
                         s_tr if (dt == 7 and nb == NB - 1) else None)
            amm = p.sems["s_tr"].n
            p.wait("vector", s_tr, amm)
            p.wait("vector", s_se, prevge)
            if k < 20:
                p.op("vector", lambda q, kk=k, M2=M: q.scalar_tensor_tensor(
                    out=gg[kk][0:M2, 0:T], in0=pb[0:M2, 0:T], scalar=b1c[0:M2, kk:kk + 1],
                    in1=gg[kk][0:M2, 0:T], op0=OP.add, op1=OP.mult), s_ve)
            else:
                for nb in range(NB):
                    p.op("vector", lambda q, o=ggsl(k, nb), n=nb, M2=M, kk=k:
                         q.scalar_tensor_tensor(
                             out=o, in0=pb[0:M2, n * 512:(n + 1) * 512],
                             scalar=b1c[0:M2, kk:kk + 1], in1=o, op0=OP.add, op1=OP.mult),
                         s_ve if nb == NB - 1 else None)
            prevst = p.sems["s_ve"].n
            if k + 1 < NHT3:
                p.wait("gpsimd", s_tr, amm)
                load_w1(k + 1, 1)
        gg_done = prevst

        p.wait("gpsimd", s_tr, amm)
        load_w2(1)
        w2ve = [0] * NDT
        w2mm = [0] * NDT
        for do in range(NDT):
            p.wait("tensor", s_w2[do % 2], w2cnt[do])
            ps = pm if do % 2 == 0 else pb
            if do == 0:
                p.wait("tensor", s_se, prevge)
                p.wait("tensor", s_ve, gg_done)
            if do >= 2:
                p.wait("tensor", s_ve, w2ve[do - 2])
            sl = w2slabs[do % 2]
            for nb in range(NB):
                for k in range(NHT3):
                    if k < 16:
                        lhsT = sl[k // 8][:, (k % 8) * 128:(k % 8 + 1) * 128]
                    elif k < 21:
                        lhsT = sl[2][:, (k - 16) * 128:(k - 15) * 128]
                    else:
                        lhsT = sl[2][0:42, 640:768]
                    p.op("tensor", lambda q, o=ps[:, nb * 512:(nb + 1) * 512], l=lhsT,
                         r=ggsl(k, nb), st=(k == 0), sp=(k == NHT3 - 1):
                         q.matmul(o, l, r, start=st, stop=sp),
                         s_tr if (k == NHT3 - 1 and nb == NB - 1) else None)
            w2mm[do] = p.sems["s_tr"].n
            p.wait("vector", s_tr, w2mm[do])
            for nb in range(NB):
                p.op("vector", lambda q, o=xr[do][:, nb * 512:(nb + 1) * 512],
                     i0=ps[:, nb * 512:(nb + 1) * 512], sc=b2c[:, do:do + 1],
                     i1=xr[do][:, nb * 512:(nb + 1) * 512]:
                     q.scalar_tensor_tensor(out=o, in0=i0, scalar=sc, in1=i1,
                                            op0=OP.add, op1=OP.add),
                     s_ve if nb == NB - 1 else None)
            w2ve[do] = p.sems["s_ve"].n
            if do + 2 < NDT:
                p.wait("gpsimd", s_tr, w2mm[do])
                load_w2(do + 2)
        carry["w2mm"] = w2mm[-1]
        carry["w2ve"] = w2ve[-1]

    xbufs = [U1[:, 0:T], U2[:, 0:T], U3[:, 0:T]]
    p.wait("sync", s_tr, carry["w2mm"])
    p.wait("sync", s_ve, carry["w2ve"])
    addtok = [0] * NDT
    for j in range(NDT):
        if j >= 3:
            p.wait("sync", s_ve, addtok[j - 3])
        p.dma("sync", xbufs[j % 3], bass.AP(xsp3.tensor, j * 128 * T, [[T, 128], [1, T]]),
              s_xr3[j % 3])
        p.wait("vector", s_xr3[j % 3], s_xr3[j % 3].n)
        p.drain("vector")
        addtok[j] = p.op("vector", lambda q, jj=j, b=xbufs[j % 3]: q.tensor_tensor(
            out=xr[jj][:, 0:T], in0=xr[jj][:, 0:T], in1=b, op=OP.add), s_ve)

    build_ln(p, B, xr, xr, xb, wait_tok=addtok[-1])
    B["tok_ln3_v"] = p.sems["s_ve"].n
    B["tok_ln3_g"] = p.sems["s_ge"].n



def build_ln(p, B, rin, yf, ybf, wait_tok, gain=None, bias=None, wait_ge_tok=0):
    U1, U2 = B["U1"], B["U2"]
    pm, pb = B["pm"], B["pb"]
    rowA, rowB = B["rowA"], B["rowB"]
    invd_c, ones_r = B["invd_c"], B["ones_r"]
    s_tr, s_ve, s_se, s_ge = p.sem("s_tr"), p.sem("s_ve"), p.sem("s_se"), p.sem("s_ge")
    n = len(rin)

    p.wait("tensor", s_ve, wait_tok)
    if wait_ge_tok:
        p.wait("tensor", s_ge, wait_ge_tok)
        p.wait("vector", s_ge, wait_ge_tok)
    for nb in range(NB):
        for j in range(n):
            p.op("tensor", lambda q, o=pb[0:1, nb * 512:(nb + 1) * 512],
                 r=rin[j][:, nb * 512:(nb + 1) * 512], st=(j == 0), sp=(j == n - 1):
                 q.matmul(o, invd_c[:], r, start=st, stop=sp),
                 s_tr if j == n - 1 else None)
    p.wait("vector", s_ve, wait_tok)
    p.drain("vector")
    for j in range(n):
        sq_tok = p.op("vector", lambda q, jj=j: q.tensor_tensor(
            out=U1[:], in0=rin[jj][:], in1=rin[jj][:], op=OP.mult), s_ve)
        p.wait("tensor", s_ve, sq_tok)
        for nb in range(NB):
            p.op("tensor", lambda q, o=pm[0:1, nb * 512:(nb + 1) * 512],
                 r=U1[:, nb * 512:(nb + 1) * 512], st=(j == 0), sp=(j == n - 1):
                 q.matmul(o, invd_c[:], r, start=st, stop=sp),
                 s_tr if nb == NB - 1 else None)
        if j < n - 1:
            p.wait("vector", s_tr, s_tr.n)
            p.drain("vector")
    m2_tok = p.sems["s_tr"].n
    p.wait("vector", s_tr, m2_tok)
    p.drain("vector")
    p.op("vector", lambda q: q.tensor_copy(out=rowA, in_=pb[0:1, :]))
    p.drain("vector")
    p.op("vector", lambda q: q.tensor_tensor(out=rowB, in0=rowA, in1=rowA, op=OP.mult))
    p.drain("vector")
    var_tok = p.op("vector", lambda q: q.tensor_tensor(out=rowB, in0=pm[0:1, :], in1=rowB,
                                                       op=OP.subtract), s_ve)
    p.wait("scalar", s_ve, var_tok)
    cm = B["cmisc"]
    sr_tok = p.op("scalar", lambda q, b=cm[0:1, 1:2]: q.activation(out=rowB, in_=rowB, func=AF.Sqrt,
                                                   bias=b), s_se)
    p.wait("vector", s_se, sr_tok)
    p.drain("vector")
    p.op("vector", lambda q: q.reciprocal(out=rowB, in_=rowB))
    p.drain("vector")
    rows_tok = p.op("vector", lambda q: q.engine_nop(), s_ve) if False else p.sems["s_ve"].n
    rows_tok = p.op("vector", lambda q: q.tensor_copy(out=rowA, in_=rowA), s_ve)
    p.wait("tensor", s_ve, rows_tok)
    for nb in range(NB):
        p.op("tensor", lambda q, o=pm[:, nb * 512:(nb + 1) * 512],
             r=rowB[0:1, nb * 512:(nb + 1) * 512]:
             q.matmul(o, ones_r[:], r, start=True, stop=True))
        p.op("tensor", lambda q, o=pb[:, nb * 512:(nb + 1) * 512],
             r=rowA[0:1, nb * 512:(nb + 1) * 512]:
             q.matmul(o, ones_r[:], r, start=True, stop=True),
             s_tr if nb == NB - 1 else None)
    bc_tok = p.sems["s_tr"].n
    p.wait("vector", s_tr, bc_tok)
    p.drain("vector")
    last_ge = 0
    for j in range(len(rin)):
        p.op("vector", lambda q, jj=j: q.tensor_tensor(out=U2[:], in0=rin[jj][:],
                                                       in1=pb[:, 0:T], op=OP.subtract))
        p.drain("vector")
        if gain is not None:
            p.op("vector", lambda q: q.tensor_tensor(out=U2[:], in0=U2[:], in1=pm[:, 0:T],
                                                     op=OP.mult))
            p.drain("vector")
            ytok = p.op("vector", lambda q, jj=j, g=gain, b=bias: q.tensor_scalar(
                out=yf[jj][:], in0=U2[:], scalar1=g[jj], scalar2=b[jj],
                op0=OP.mult, op1=OP.add), s_ve)
        else:
            ytok = p.op("vector", lambda q, jj=j: q.tensor_tensor(out=yf[jj][:], in0=U2[:],
                                                                  in1=pm[:, 0:T], op=OP.mult), s_ve)
        if ybf is not None:
            p.wait("gpsimd", s_ve, ytok)
            if last_ge:
                p.drain("gpsimd")
            last_ge = p.op("gpsimd", lambda q, jj=j: q.tensor_copy(out=ybf[jj][:], in_=yf[jj][:]), s_ge)
        p.drain("vector")
    return p.sems["s_ve"].n, last_ge


def build_output_store(p, B, src_tiles, out_y):
    pm = B["pm"]
    identf = B["identf"]
    yout = B["yout"]
    s_tr, s_ve = p.sem("s_tr"), p.sem("s_ve")
    s_st2 = [p.sem("s_st0"), p.sem("s_st1")]
    st_done = [0, 0]
    p.wait("tensor", s_ve, p.sems["s_ve"].n)
    for i in range(NTT):
        buf = i % 2
        for half in range(2):
            bank = buf * 2 + half
            for k in range(4):
                j = half * 4 + k
                p.op("tensor", lambda q, o=pm[:, bank * 512 + 128 * k: bank * 512 + 128 * (k + 1)],
                     i_=src_tiles[j][:, 128 * i:128 * (i + 1)]:
                     q.transpose(o, i_, identf[:]),
                     s_tr if k == 3 else None)
            p.wait("vector", s_tr, p.sems["s_tr"].n)
            if st_done[buf]:
                p.wait("vector", s_st2[buf], st_done[buf])
            if i >= 1:
                p.drain("vector")
            for k in range(4):
                j = half * 4 + k
                p.op("vector", lambda q, o=yout[buf][:, 128 * j:128 * (j + 1)],
                     s=pm[:, bank * 512 + 128 * k: bank * 512 + 128 * (k + 1)]:
                     q.tensor_copy(out=o, in_=s),
                     s_ve if (half == 1 and k == 3) else None)
        evtok = p.sems["s_ve"].n
        p.wait("sync", s_ve, evtok)
        p.dma("sync", out_y[128 * i:128 * (i + 1), :], yout[buf][:], s_st2[buf])
        st_done[buf] = s_st2[buf].n
        p.wait("tensor", s_ve, evtok)
    p.wait("sync", s_st2[0], s_st2[0].n)
    p.wait("sync", s_st2[1], s_st2[1].n)


class ProgP(Prog):
    def sbuf(self, es, name, shape, dtype):
        return es.enter_context(self.nc.sbuf_tensor(name, list(shape), dtype))

    def psum(self, es, name, shape, dtype=F32):
        return es.enter_context(self.nc.psum_tensor(name, list(shape), dtype))


def _mk_selq():
    import ml_dtypes
    s = np.zeros((128, 128), dtype=np.float32)
    for d in range(8):
        for h in range(16):
            if h // 2 == d:
                r0 = 64 * (h % 2)
                s[r0:r0 + 64, 16 * d + h] = 1.0
    return s.astype(ml_dtypes.bfloat16)


def _mk_b3_consts():
    import ml_dtypes
    bf = ml_dtypes.bfloat16
    t = np.arange(T, dtype=np.float64)
    half_idx = np.arange(0, 64, 2, dtype=np.float64)
    inv_freq = 1.0 / 10000.0 ** (half_idx / 64.0)
    fh = t[:, None] * inv_freq[None, :]
    freqs = np.concatenate([fh, fh], axis=1)
    base = (half_idx + 0.4 * 64) / (1.4 * 64)
    power = (t - T // 2) / 256.0
    sh = base[None, :] ** power[:, None]
    scale = np.concatenate([sh, sh], axis=1)
    cosq = (np.cos(freqs) * scale).T.astype(np.float32)
    sinq = (np.sin(freqs) * scale).T.astype(np.float32)
    cosk = (np.cos(freqs) / scale).T.astype(np.float32)
    sink = (np.sin(freqs) / scale).T.astype(np.float32)
    sw = (np.arange(64) + 32) % 64
    sgn2 = np.where(np.arange(64) < 32, 1.0, -1.0).astype(np.float32)[:, None]
    out = {
        "c_cosq": np.tile(cosq, (2, 1)).astype(bf),
        "c_sinqS": np.tile(sinq[sw, :] * sgn2, (2, 1)).astype(bf),
        "c_cosk": np.tile(cosk, (2, 1)).astype(bf),
        "c_sinkS": np.tile(sink[sw, :] * sgn2, (2, 1)).astype(bf),
    }
    s8 = np.zeros((128, 512), dtype=np.float32)
    for d in range(4):
        for r in range(128):
            s8[2 * d + (r >= 64), 128 * d + r] = 1.0
    out["c_sel8"] = s8.astype(bf)
    sq = np.zeros((128, 32), dtype=np.float32)
    for d in range(4):
        for r in range(128):
            sq[r, 8 * d + 2 * d + (r >= 64)] = 1.0
    out["c_selfq"] = sq.astype(bf)
    pp = np.arange(128)[:, None]
    uu = np.arange(128)[None, :]
    out["c_cnL"] = np.where(uu < pp, -100000.0, 0.0).astype(bf)
    out["c_cnR"] = np.where(uu > pp, -100000.0, 0.0).astype(bf)
    return out


def build_program(n_blocks=1):
    import ml_dtypes
    nc = bass.Bass("TRN2", target_bir_lowering=False)
    p = ProgP(nc)

    dram = {}
    def din(name, shape, dt=F32):
        dram[name] = nc.dram_tensor(name, list(shape), dt, kind="ExternalInput").ap()
    def dout(name, shape, dt=F32):
        dram[name] = nc.dram_tensor(name, list(shape), dt, kind="ExternalOutput").ap()

    din("x", [T, D])
    din("W_aiv", [D, 3 * D])
    din("decay_bias", [D])
    din("W_mix", [D, D])
    din("b_mix", [D])
    din("c_id_bf", [128, 128], BF16)
    din("c_id_f32", [128, 128], F32)
    din("c_ones_row", [1, 128], F32)
    din("c_invd_col", [128, 1], F32)
    din("c_misc", [128, 4], F32)
    din("pk", [256 * 1024], I32)
    din("wscale", [1, 1], F32)
    din("fbias", [D], F32)
    din("W_qkv", [D, 3 * D])
    din("b_qkv", [3 * D])
    din("W_out", [D, D])
    din("b_out", [D])
    din("rel_emb_k", [129, 64])
    din("rel_emb_v", [129, 64])
    din("c_selq", [128, 128], BF16)
    din("c_ones_col_bf", [128, 1], BF16)
    din("c_ones_row_bf", [1, 128], BF16)
    din("l1f_wqkv", [2 * D * 1536])
    din("l1f_qs", [256])
    din("l1f_ks", [256])
    din("l1f_wo", [2 * 512 * D])
    din("l1f_lng", [2 * D])
    din("l1f_lnb", [2 * D])
    din("l1f_w1", [2 * D * 5460])
    din("l1f_b1", [2 * 5460])
    din("l1f_w2", [2 * 2730 * D])
    din("l1f_b2", [2 * D])
    din("c_cosq", [128, T], BF16)
    din("c_sinqS", [128, T], BF16)
    din("c_cosk", [128, T], BF16)
    din("c_sinkS", [128, T], BF16)
    din("c_sel8", [128, 512], BF16)
    din("c_selfq", [128, 32], BF16)
    din("c_cnL", [128, 128], BF16)
    din("c_cnR", [128, 128], BF16)
    dram["sc_amax"] = nc.dram_tensor("sc_amax", [T], F32, kind="Internal").ap()
    dram["sc_xspill"] = nc.dram_tensor("sc_xspill", [D * T], F32, kind="Internal").ap()
    dram["sc_qre0"] = nc.dram_tensor("sc_qre0", [T * 512], F32, kind="Internal").ap()
    dram["sc_qre1"] = nc.dram_tensor("sc_qre1", [T * 512], F32, kind="Internal").ap()
    dram["sc_e0"] = nc.dram_tensor("sc_e0", [T * 640], BF16, kind="Internal").ap()
    dram["sc_e1"] = nc.dram_tensor("sc_e1", [T * 640], BF16, kind="Internal").ap()
    dram["sc_mqs"] = nc.dram_tensor("sc_mqs", [48 * T], BF16, kind="Internal").ap()
    dout("y", [T, D])

    consts = {
        "c_id_bf": np.eye(128).astype(ml_dtypes.bfloat16),
        "c_id_f32": np.eye(128, dtype=np.float32),
        "c_ones_row": np.ones((1, 128), dtype=np.float32),
        "c_invd_col": np.full((128, 1), 1.0 / D, dtype=np.float32),
        "c_misc": np.broadcast_to(np.array([1.0, 1e-5, 1e-8, 0.0], dtype=np.float32), (128, 4)).copy(),
        "c_selq": _mk_selq(),
        "c_ones_col_bf": np.ones((128, 1), dtype=ml_dtypes.bfloat16),
        "c_ones_row_bf": np.ones((1, 128), dtype=ml_dtypes.bfloat16),
    }

    es = ExitStack()
    with es:
        B = {"dram": dram}
        B["xr"] = [p.sbuf(es, f"xr{j}", (128, T), F32) for j in range(NDT)]
        B["xb"] = [p.sbuf(es, f"xb{j}", (128, T), BF16) for j in range(NDT)]
        B["hb"] = [p.sbuf(es, f"hb{j}", (128, T), BF16) for j in range(NDT)]
        B["xin"] = [p.sbuf(es, f"xin{k}", (128, D), F32) for k in range(2)]
        B["wj"] = [p.sbuf(es, f"wj{k}", (128, 3 * D), BF16) for k in range(2)]
        B["wm"] = [p.sbuf(es, f"wm{j}", (128, D), BF16) for j in range(NDT)]
        B["U1"] = p.sbuf(es, "U1", (128, T), F32)
        B["U2"] = p.sbuf(es, "U2", (128, T), F32)
        B["U3"] = p.sbuf(es, "U3", (128, T), F32)
        B["G1"] = p.sbuf(es, "G1", (128, 32), F32)
        B["G2"] = p.sbuf(es, "G2", (128, 32), F32)
        B["G3"] = p.sbuf(es, "G3", (128, 32), F32)
        B["dbc"] = p.sbuf(es, "dbc", (128, NDT), F32)
        B["bmc"] = p.sbuf(es, "bmc", (128, NDT), F32)
        B["identb"] = p.sbuf(es, "identb", (128, 128), BF16)
        B["identf"] = p.sbuf(es, "identf", (128, 128), F32)
        B["ones_r"] = p.sbuf(es, "ones_r", (1, 128), F32)
        B["invd_c"] = p.sbuf(es, "invd_c", (128, 1), F32)
        B["cmisc"] = p.sbuf(es, "cmisc", (128, 4), F32)
        B["rowA"] = None
        B["rowB"] = None
        B["yout"] = B["xin"]
        B["amaxc"] = p.sbuf(es, "amaxc", (128, 16), F32)
        B["fbc"] = p.sbuf(es, "fbc", (128, NDT), F32)
        B["wsc"] = p.sbuf(es, "wsc", (1, 1), F32)
        B["U3h"] = B["U3"][:, 0:1024].bitcast(I32)
        B["rowA"] = B["U3"][0:1, 0:T]
        B["rowB"] = B["U1"][0:1, 0:T]
        B["vex"] = [p.sbuf(es, f"vex{k}", (128, 1040), BF16) for k in range(3)]
        B["qer_t"] = p.sbuf(es, "qer", (128, T), BF16)
        B["onesbc"] = p.sbuf(es, "onesbc", (128, 1), BF16)
        B["onesbr"] = p.sbuf(es, "onesbr", (1, 128), BF16)
        B["zbuf"] = p.sbuf(es, "zbuf", (128, 128), BF16)
        B["bfull"] = p.sbuf(es, "bfull", (128, 512), BF16)
        if n_blocks >= 4:
            B["gext"] = p.sbuf(es, "gext", (128, T), BF16)
            B["b1c"] = p.sbuf(es, "b1c", (128, 44), F32)
            B["sel8h"] = p.sbuf(es, "sel8h", (128, 512), BF16)
            B["selfq"] = p.sbuf(es, "selfq", (128, 32), BF16)
            B["cnL"] = p.sbuf(es, "cnL", (128, 128), BF16)
            B["cnR"] = p.sbuf(es, "cnR", (128, 128), BF16)
        B["pm"] = p.psum(es, "pm", (128, T), F32)
        B["pb"] = p.psum(es, "pb", (128, T), F32)

        s_ld = p.sem("s_ld")
        p.dma("sync", B["identb"][:], dram["c_id_bf"], s_ld)
        p.dma("sync", B["identf"][:], dram["c_id_f32"], s_ld)
        p.dma("sync", B["ones_r"][:], dram["c_ones_row"], s_ld)
        p.dma("sync", B["invd_c"][:], dram["c_invd_col"], s_ld)
        p.dma("sync", B["cmisc"][:], dram["c_misc"], s_ld)
        if n_blocks >= 4:
            p.dma("sync", B["sel8h"][:], dram["c_sel8"], s_ld)
            p.dma("sync", B["selfq"][:], dram["c_selfq"], s_ld)
            p.dma("sync", B["cnL"][:], dram["c_cnL"], s_ld)
            p.dma("sync", B["cnR"][:], dram["c_cnR"], s_ld)
            consts.update(_mk_b3_consts())

        build_block0(p, es, B)
        if n_blocks >= 2:
            build_block1(p, es, B)
        if n_blocks >= 3:
            build_block2(p, es, B)
        if n_blocks >= 4:
            build_block3(p, es, B)
        build_output_store(p, B, B["xr"], dram["y"])

        p.finalize(es)
    return nc, consts



def kernel(**inputs):
    f32 = np.float32
    nc, consts = build_program(n_blocks=4)
    qs = np.asarray(inputs["L1_ffn_q_scale"], f32)
    ks = np.asarray(inputs["L1_ffn_k_scale"], f32)
    common = {
        "W_aiv": np.asarray(inputs["L0_attn_W_aiv"], f32),
        "decay_bias": np.asarray(inputs["L0_attn_decay_bias"], f32),
        "W_mix": np.asarray(inputs["L0_attn_W_mix"], f32),
        "b_mix": np.asarray(inputs["L0_attn_b_mix"], f32),
        "pk": np.asarray(inputs["L0_ffn_packed_weight"], np.int32),
        "wscale": np.asarray(inputs["L0_ffn_weight_scale"], f32).reshape(1, 1),
        "fbias": np.asarray(inputs["L0_ffn_bias"], f32),
        "W_qkv": np.asarray(inputs["L1_attn_W_qkv"], f32),
        "b_qkv": np.asarray(inputs["L1_attn_b_qkv"], f32),
        "W_out": np.asarray(inputs["L1_attn_W_out"], f32),
        "b_out": np.asarray(inputs["L1_attn_b_out"], f32),
        "rel_emb_k": np.asarray(inputs["L1_attn_rel_emb_k"], f32),
        "rel_emb_v": np.asarray(inputs["L1_attn_rel_emb_v"], f32),
        "l1f_wqkv": np.asarray(inputs["L1_ffn_wqkv"], f32).reshape(-1),
        "l1f_qs": np.tile(qs, (1, 2)).reshape(-1),
        "l1f_ks": np.tile(ks, (1, 2)).reshape(-1),
        "l1f_wo": np.asarray(inputs["L1_ffn_wo"], f32).reshape(-1),
        "l1f_lng": np.asarray(inputs["L1_ffn_ln_g"], f32).reshape(-1),
        "l1f_lnb": np.asarray(inputs["L1_ffn_ln_b"], f32).reshape(-1),
        "l1f_w1": np.asarray(inputs["L1_ffn_w1"], f32).reshape(-1),
        "l1f_b1": np.asarray(inputs["L1_ffn_b1"], f32).reshape(-1),
        "l1f_w2": np.asarray(inputs["L1_ffn_w2"], f32).reshape(-1),
        "l1f_b2": np.asarray(inputs["L1_ffn_b2"], f32).reshape(-1),
    }
    common.update(consts)
    xall = np.asarray(inputs["L0_attn_x"], f32)
    in_maps = []
    for b in range(8):
        m = dict(common)
        m["x"] = xall[b]
        in_maps.append(m)
    from concourse.bass_utils import run_bass_kernel_spmd
    res = run_bass_kernel_spmd(nc, in_maps, core_ids=list(range(8)))
    return np.stack([np.asarray(res.results[b]["y"], f32) for b in range(8)], axis=0)


# version 17
# speedup vs baseline: 1.0107x; 1.0107x over previous
import numpy as np
from contextlib import ExitStack
import concourse.bass as bass
import concourse.mybir as mybir
from concourse.alu_op_type import AluOpType as OP

F32 = mybir.dt.float32
BF16 = mybir.dt.bfloat16
I32 = mybir.dt.int32
AF = mybir.ActivationFunctionType
AX = mybir.AxisListType

D = 1024
T = 2048
NDT = D // 128
NTT = T // 128
NB = T // 512
MAGIC = 12582912.0


class Sem:
    __slots__ = ("name", "n", "h")
    def __init__(self, name):
        self.name = name
        self.n = 0
        self.h = None


class Prog:
    ENGS = ("sync", "gpsimd", "vector", "scalar", "tensor")

    def __init__(self, nc):
        self.nc = nc
        self.thunks = {e: [] for e in self.ENGS}
        self.sems = {}

    def sem(self, name):
        s = self.sems.get(name)
        if s is None:
            s = Sem(name)
            self.sems[name] = s
        return s

    def on(self, eng, fn):
        self.thunks[eng].append(fn)

    def dma(self, eng, out, in_, sem):
        sem.n += 16
        self.on(eng, lambda q, o=out, i=in_, s=sem: q.dma_start(out=o, in_=i).then_inc(s.h, 16))

    def wait(self, eng, sem, val=None):
        v = sem.n if val is None else val
        if v > 0:
            self.on(eng, lambda q, s=sem, vv=v: q.wait_ge(s.h, vv))

    def op(self, eng, fn, sem=None, k=1):
        if sem is not None:
            sem.n += k
            self.on(eng, lambda q, f=fn, s=sem, kk=k: f(q).then_inc(s.h, kk))
            return sem.n
        self.on(eng, lambda q, f=fn: f(q))
        return None

    def drain(self, eng):
        self.on(eng, lambda q: q.drain())

    def finalize(self, es):
        for s in self.sems.values():
            s.h = es.enter_context(self.nc.semaphore(s.name))
        with self.nc.Block() as block:
            @block.sync
            def _(q):
                for t in self.thunks["sync"]:
                    t(q)

            @block.gpsimd
            def _(q):
                for t in self.thunks["gpsimd"]:
                    t(q)

            @block.vector
            def _(q):
                for t in self.thunks["vector"]:
                    t(q)

            @block.scalar
            def _(q):
                for t in self.thunks["scalar"]:
                    t(q)

            @block.tensor
            def _(q):
                for t in self.thunks["tensor"]:
                    t(q)


def col_ap(dram_ap, base, n=128):
    return bass.AP(dram_ap.tensor, base, [[1, n], [1, 1]])


def build_load_x(p, es, B):
    xr, xb = B["xr"], B["xb"]
    pm = B["pm"]
    identf = B["identf"]
    x_in = B["dram"]["x"]
    s_ld = p.sem("s_ld")
    s_tr = p.sem("s_tr")
    s_ve = p.sem("s_ve")
    s_se = p.sem("s_se")
    xin = B["xin"]
    s_x = [p.sem("s_x0"), p.sem("s_x1")]
    tr_done = [0, 0]
    ev_done = [0, 0, 0, 0]
    es_done = [0, 0, 0, 0]
    p.wait("tensor", s_ld, s_ld.n)
    for i in range(NTT):
        buf = i % 2
        if tr_done[buf]:
            p.wait("sync", s_tr, tr_done[buf])
        p.dma("sync", xin[buf][:], x_in[128 * i:128 * (i + 1), :], s_x[buf])
        p.wait("tensor", s_x[buf], s_x[buf].n)
        for half in range(2):
            bank = buf * 2 + half
            if ev_done[bank]:
                p.wait("tensor", s_ve, ev_done[bank])
                p.wait("tensor", s_se, es_done[bank])
            for k in range(4):
                j = half * 4 + k
                tok = p.op("tensor", lambda q, o=pm[:, bank * 512 + 128 * k: bank * 512 + 128 * (k + 1)],
                           i_=xin[buf][:, 128 * j:128 * (j + 1)]:
                           q.transpose(o, i_, identf[:]), s_tr if k == 3 else None)
            tr_done[buf] = s_tr.n
            p.wait("vector", s_tr, s_tr.n)
            for k in range(4):
                j = half * 4 + k
                src = pm[:, bank * 512 + 128 * k: bank * 512 + 128 * (k + 1)]
                p.op("vector", lambda q, o=xr[j][:, 128 * i:128 * (i + 1)], s=src:
                     q.tensor_copy(out=o, in_=s), s_ve if k == 3 else None)
            ev_done[bank] = s_ve.n
            p.wait("scalar", s_ve, s_ve.n)
            for k in range(4):
                j = half * 4 + k
                p.op("scalar", lambda q, o=xb[j][:, 128 * i:128 * (i + 1)],
                     s=xr[j][:, 128 * i:128 * (i + 1)]:
                     q.copy(out=o, in_=s), s_se if k == 3 else None)
            es_done[bank] = s_se.n




def build_block0(p, es, B):
    nc = p.nc
    xr, xb, hb = B["xr"], B["xb"], B["hb"]
    U1, U2, U3 = B["U1"], B["U2"], B["U3"]
    G1, G2, G3 = B["G1"], B["G2"], B["G3"]
    pm, pb = B["pm"], B["pb"]
    identf = B["identf"]
    dram = B["dram"]

    wj = [p.sbuf(es, f"b0_wj{k}", (128, 3 * D), BF16) for k in range(2)] if False else B["wj"]
    wm = B["wm"]
    dbc, bmc = B["dbc"], B["bmc"]

    s_ld = p.sem("s_ld")
    s_wav2 = [p.sem("s_wav0"), p.sem("s_wav1")]
    s_wmx = p.sem("s_wmx")
    s_tr = p.sem("s_tr")
    s_ve = p.sem("s_ve")
    s_se = p.sem("s_se")
    s_ge = p.sem("s_ge")

    x_in = dram["x"]
    W_aiv = dram["W_aiv"]
    db_in = dram["decay_bias"]
    W_mix = dram["W_mix"]
    bm_in = dram["b_mix"]

    for j in range(NDT):
        p.dma("sync", dbc[:, j:j + 1], col_ap(db_in, 128 * j), s_ld)
        p.dma("sync", bmc[:, j:j + 1], col_ap(bm_in, 128 * j), s_ld)
    for j in range(NDT):
        p.dma("gpsimd", wm[j][:], W_mix[128 * j:128 * (j + 1), :], s_wmx)

    def load_wav(j):
        dst = wj[j % 2]
        s_wav = s_wav2[j % 2]
        for qq in range(3):
            srcap = bass.AP(W_aiv.tensor, qq * D + 128 * j,
                            [[3072, 128], [3072 * 128, 8], [1, 128]])
            p.dma("gpsimd", dst[:, qq * D:(qq + 1) * D], srcap, s_wav)

    load_wav(0)
    load_wav(1)

    build_load_x(p, es, B)

    p.wait("tensor", s_ve, s_ve.n)
    p.wait("tensor", s_se, s_se.n)

    evac_a_done = 0
    evac_i_done = 0
    evac_v_done = 0
    h_tok = 0
    for j in range(NDT):
        wtile = wj[j % 2]
        p.wait("tensor", s_wav2[j % 2], 48 * (j // 2 + 1))
        if evac_v_done:
            p.wait("tensor", s_ve, evac_v_done)
            p.wait("tensor", s_se, evac_i_done)
        mmtok = {}
        for qi, ps in [(0, pm), (1, pb), (2, pm)]:
            if qi == 2:
                p.wait("tensor", s_se, evac_a_done)
            for nb in range(NB):
                for dtl in range(NDT):
                    lhsT = wtile[:, qi * D + dtl * 128: qi * D + (dtl + 1) * 128]
                    rhs = xb[dtl][:, nb * 512:(nb + 1) * 512]
                    p.op("tensor", lambda q, o=ps[:, nb * 512:(nb + 1) * 512], l=lhsT, r=rhs,
                         st=(dtl == 0), sp=(dtl == NDT - 1):
                         q.matmul(o, l, r, start=st, stop=sp),
                         s_tr if (dtl == NDT - 1 and nb == NB - 1) else None)
            mmtok[qi] = s_tr.n
            if qi == 0:
                if h_tok:
                    p.wait("scalar", s_ve, h_tok)
                p.wait("scalar", s_tr, mmtok[0])
                p.drain("scalar")
                for nb in range(NB):
                    p.op("scalar", lambda q, o=U1[:, nb * 512:(nb + 1) * 512],
                         i_=pm[:, nb * 512:(nb + 1) * 512], b=dbc[:, j:j + 1]:
                         q.activation(out=o, in_=i_, func=AF.Sigmoid, bias=b),
                         s_se if nb == NB - 1 else None)
                evac_a_done = s_se.n
            elif qi == 1:
                p.wait("scalar", s_tr, mmtok[1])
                for nb in range(NB):
                    p.op("scalar", lambda q, o=U2[:, nb * 512:(nb + 1) * 512],
                         i_=pb[:, nb * 512:(nb + 1) * 512]:
                         q.activation(out=o, in_=i_, func=AF.Sigmoid),
                         s_se if nb == NB - 1 else None)
                evac_i_done = s_se.n
        p.wait("vector", s_tr, mmtok[2])
        if h_tok:
            p.drain("vector")
        for nb in range(NB):
            p.op("vector", lambda q, o=U3[:, nb * 512:(nb + 1) * 512],
                 i_=pm[:, nb * 512:(nb + 1) * 512]:
                 q.tensor_copy(out=o, in_=i_),
                 s_ve if nb == NB - 1 else None)
        evac_v_done = s_ve.n
        if j + 2 < NDT:
            p.wait("gpsimd", s_tr, mmtok[2])
            load_wav(j + 2)

        p.wait("vector", s_se, evac_i_done)
        p.drain("vector")
        p.op("vector", lambda q: q.tensor_tensor(out=U3[:], in0=U2[:], in1=U3[:], op=OP.mult))
        p.drain("vector")
        sq_tok = p.op("vector", lambda q: q.tensor_tensor(out=U2[:], in0=U1[:], in1=U1[:], op=OP.mult), s_ve)
        p.wait("scalar", s_ve, sq_tok)
        p.drain("scalar")
        cm = B["cmisc"]
        rt_tok = p.op("scalar", lambda q, b=cm[:, 0:1]: q.activation(out=U2[:], in_=U2[:], func=AF.Sqrt,
                                                       scale=-1.0, bias=b), s_se)
        p.wait("vector", s_se, rt_tok)
        p.drain("vector")
        p.op("vector", lambda q: q.tensor_tensor(out=U3[:], in0=U3[:], in1=U2[:], op=OP.mult))
        p.drain("vector")
        for c in range(32):
            sl = slice(c * 64, (c + 1) * 64)
            p.op("vector", lambda q, s=sl: q.tensor_tensor_scan(
                out=U2[:, s], data0=U1[:, s], data1=U1[:, s],
                initial=1.0, op0=OP.mult, op1=OP.bypass))
        p.drain("vector")
        p.op("vector", lambda q: q.tensor_scalar(out=U1[:], in0=U2[:], scalar1=1e-10,
                                                 scalar2=None, op0=OP.max))
        p.drain("vector")
        p.op("vector", lambda q: q.reciprocal(out=U1[:], in_=U1[:]))
        p.drain("vector")
        p.op("vector", lambda q: q.tensor_tensor(out=U3[:], in0=U3[:], in1=U1[:], op=OP.mult))
        p.drain("vector")
        for c in range(32):
            sl = slice(c * 64, (c + 1) * 64)
            p.op("vector", lambda q, s=sl: q.tensor_tensor_scan(
                out=U1[:, s], data0=U3[:, s], data1=U3[:, s],
                initial=0.0, op0=OP.add, op1=OP.bypass))
        p.drain("vector")
        p.op("vector", lambda q: q.tensor_tensor(out=U1[:], in0=U2[:], in1=U1[:], op=OP.mult))
        p.drain("vector")
        ctd = bass.AP(U2, 63, [[T, 128], [64, 32]])
        cfs = bass.AP(U1, 63, [[T, 128], [64, 32]])
        p.op("vector", lambda q: q.tensor_scalar(out=G1[:], in0=ctd, scalar1=1e-10,
                                                 scalar2=None, op0=OP.max))
        p.drain("vector")
        p.op("vector", lambda q: q.tensor_tensor_scan(
            out=G2[:], data0=G1[:], data1=G1[:], initial=1.0, op0=OP.mult, op1=OP.bypass))
        p.drain("vector")
        p.op("vector", lambda q: q.tensor_scalar(out=G1[:], in0=G2[:], scalar1=1e-10,
                                                 scalar2=None, op0=OP.max))
        p.drain("vector")
        p.op("vector", lambda q: q.reciprocal(out=G1[:], in_=G1[:]))
        p.drain("vector")
        p.op("vector", lambda q: q.tensor_tensor(out=G3[:], in0=cfs, in1=G1[:], op=OP.mult))
        p.drain("vector")
        p.op("vector", lambda q: q.tensor_tensor_scan(
            out=G3[:], data0=G3[:], data1=G3[:], initial=0.0, op0=OP.add, op1=OP.bypass))
        p.drain("vector")
        p.op("vector", lambda q: q.tensor_tensor(out=G2[:], in0=G2[:], in1=G3[:], op=OP.mult))
        p.drain("vector")
        incb = bass.AP(G2, 0, [[32, 128], [1, 31], [0, 64]])
        u3v = U3[:, 64:T].rearrange("p (c j) -> p c j", c=31)
        u2v = U2[:, 64:T].rearrange("p (c j) -> p c j", c=31)
        p.op("vector", lambda q: q.tensor_tensor(out=u3v, in0=u2v, in1=incb, op=OP.mult))
        p.drain("vector")
        p.op("vector", lambda q, jj=j: q.tensor_tensor(
            out=hb[jj][:, 64:T], in0=U3[:, 64:T], in1=U1[:, 64:T], op=OP.add))
        h_tok = p.op("vector", lambda q, jj=j: q.tensor_copy(out=hb[jj][:, 0:64], in_=U1[:, 0:64]),
                     s_ve)

    B["tok_b0_mm"] = p.sems["s_tr"].n
    p.wait("tensor", s_ve, h_tok)
    p.wait("tensor", s_wmx, s_wmx.n)
    p.wait("tensor", s_se, s_se.n)
    r_tok = [0] * NDT
    for do in range(NDT):
        ps = pm if do % 2 == 0 else pb
        if do >= 2:
            p.wait("tensor", s_ve, r_tok[do - 2])
        for nb in range(NB):
            for dtl in range(NDT):
                lhsT = wm[dtl][:, do * 128:(do + 1) * 128]
                rhs = hb[dtl][:, nb * 512:(nb + 1) * 512]
                p.op("tensor", lambda q, o=ps[:, nb * 512:(nb + 1) * 512], l=lhsT, r=rhs,
                     st=(dtl == 0), sp=(dtl == NDT - 1):
                     q.matmul(o, l, r, start=st, stop=sp),
                     s_tr if (dtl == NDT - 1 and nb == NB - 1) else None)
        p.wait("vector", s_tr, s_tr.n)
        p.drain("vector")
        for nb in range(NB):
            p.op("vector", lambda q, o=xr[do][:, nb * 512:(nb + 1) * 512],
                 i0=ps[:, nb * 512:(nb + 1) * 512], sc=bmc[:, do:do + 1],
                 i1=xr[do][:, nb * 512:(nb + 1) * 512]:
                 q.scalar_tensor_tensor(out=o, in0=i0, scalar=sc, in1=i1,
                                        op0=OP.add, op1=OP.add),
                 s_ve if nb == NB - 1 else None)
        r_tok[do] = s_ve.n

    p.wait("vector", s_ve, r_tok[-1])
    build_ln(p, B, xr, xr, xb, wait_tok=r_tok[-1], gain=None, bias=None)
    B["tok_ln0_v"] = p.sems["s_ve"].n
    B["tok_ln0_g"] = p.sems["s_ge"].n


def build_block1(p, es, B):
    xr, xb = B["xr"], B["xb"]
    kb = B["hb"]
    wqT = B["wm"]
    wj = B["wj"]
    U1, U2, U3 = B["U1"], B["U2"], B["U3"]
    pm, pb = B["pm"], B["pb"]
    identf, identb = B["identf"], B["identb"]
    rowA, rowB = B["rowA"], B["rowB"]
    ones_r = B["ones_r"]
    dram = B["dram"]
    amaxc = B["amaxc"]
    fbc = B["fbc"]
    wsc = B["wsc"]

    s_ld = p.sem("s_ld1")
    s_tr = p.sem("s_tr")
    s_ve = p.sem("s_ve")
    s_se = p.sem("s_se")
    s_ge = p.sem("s_ge")
    s_am = p.sem("s_am")

    pk_in = dram["pk"]
    scr = dram["sc_amax"]

    pku = [wj[0][:, 0:2048].bitcast(I32), wj[0][:, 2048:4096] if False else wj[0][:, 1024:3072].bitcast(I32)]
    pku = [wj[0][:, 0:2048].bitcast(I32), wj[1][:, 0:2048].bitcast(I32)]
    w_u = wj[0][:, 2048:3072]
    itmp = wj[1][:, 2048:3072].bitcast(I32)
    itmp = B["U3h"]

    p.wait("sync", s_tr, B["tok_b0_mm"])
    p.dma("sync", pku[0], pk_in[0:128 * 1024], s_ld)
    p.dma("sync", pku[1], pk_in[128 * 1024:256 * 1024], s_ld)
    for j in range(NDT):
        p.dma("sync", fbc[:, j:j + 1], col_ap(dram["fbias"], 128 * j), s_ld)
    p.dma("sync", wsc[:], dram["wscale"], s_ld)

    p.wait("vector", s_ld, s_ld.n)
    p.wait("vector", s_ge, B["tok_ln0_g"])
    wq_tok = 0
    for blk in range(8):
        sh = 2 * (blk // 2)
        pt = blk % 2
        p.drain("vector")
        p.op("vector", lambda q, i0=pku[pt], s=sh: q.tensor_scalar(
            out=itmp, in0=i0, scalar1=s, scalar2=3, op0=OP.logical_shift_right,
            op1=OP.bitwise_and))
        p.drain("vector")
        p.op("vector", lambda q: q.tensor_copy(out=w_u, in_=itmp))
        p.drain("vector")
        wu_tok = p.op("vector", lambda q: q.tensor_scalar(
            out=w_u, in0=w_u, scalar1=-1.0, scalar2=None, op0=OP.add), s_ve)
        p.wait("tensor", s_tr, B["tok_b0_mm"])
        p.wait("tensor", s_ve, wu_tok)
        bank = blk % 4
        for jp in range(8):
            sl = pb[:, bank * 512 + 64 * jp: bank * 512 + 64 * (jp + 1)].bitcast(BF16)
            p.op("tensor", lambda q, o=sl, i_=w_u[:, 128 * jp:128 * (jp + 1)]:
                 q.transpose(o, i_, identb[:]),
                 s_tr if jp == 7 else None)
        tok = p.sems["s_tr"].n
        p.wait("vector", s_tr, tok)
        p.drain("vector")
        for jp in range(8):
            ssl = pb[:, bank * 512 + 64 * jp: bank * 512 + 64 * (jp + 1)].bitcast(BF16)
            p.op("vector", lambda q, o=wqT[jp][:, blk * 128:(blk + 1) * 128], s2=ssl:
                 q.tensor_copy(out=o, in_=s2),
                 s_ve if jp == 7 else None)
        wq_tok = p.sems["s_ve"].n

    p.drain("vector")
    p.op("vector", lambda q: q.tensor_tensor(out=U1[:], in0=xr[0][:], in1=xr[1][:], op=OP.max))
    p.op("vector", lambda q: q.tensor_tensor(out=U2[:], in0=xr[0][:], in1=xr[1][:], op=OP.min))
    for j in range(2, NDT):
        p.drain("vector")
        p.op("vector", lambda q, jj=j: q.tensor_tensor(out=U1[:], in0=U1[:], in1=xr[jj][:], op=OP.max))
        p.op("vector", lambda q, jj=j: q.tensor_tensor(out=U2[:], in0=U2[:], in1=xr[jj][:], op=OP.min))
    p.drain("vector")
    p.op("vector", lambda q: q.tensor_scalar(out=U2[:], in0=U2[:], scalar1=-1.0, scalar2=None, op0=OP.mult))
    p.drain("vector")
    p.op("vector", lambda q: q.tensor_tensor(out=U1[:], in0=U1[:], in1=U2[:], op=OP.max), s_ve)
    am_tok = p.sems["s_ve"].n
    p.wait("tensor", s_ve, am_tok)
    tok_red = [0] * NTT
    for i in range(NTT):
        bank = i % 4
        if i >= 4:
            p.wait("tensor", s_ve, tok_red[i - 4])
        p.op("tensor", lambda q, o=pm[:, bank * 512:bank * 512 + 128],
             i_=U1[:, 128 * i:128 * (i + 1)]:
             q.transpose(o, i_, identf[:]), s_tr)
        tok = p.sems["s_tr"].n
        p.wait("vector", s_tr, tok)
        p.drain("vector")
        p.op("vector", lambda q, o=amaxc[:, i:i + 1].rearrange("p (a o) -> p a o", a=1),
             i_=pm[:, bank * 512:bank * 512 + 128].rearrange("p (a j) -> p a j", a=1):
             q.tensor_reduce(out=o, in_=i_, axis=AX.X, op=OP.max), s_ve)
        tok_red[i] = p.sems["s_ve"].n
    red_tok = p.sems["s_ve"].n
    p.wait("sync", s_ve, red_tok)
    for i in range(NTT):
        p.dma("sync", bass.AP(scr.tensor, 128 * i, [[1, 128], [1, 1]]), amaxc[:, i:i + 1], s_am)
    p.wait("sync", s_am, p.sems["s_am"].n)
    p.dma("sync", rowA, bass.AP(scr.tensor, 0, [[T, 1], [1, T]]), s_am)
    p.wait("vector", s_am, p.sems["s_am"].n)
    p.drain("vector")
    p.op("vector", lambda q: q.tensor_scalar(out=rowB, in0=rowA, scalar1=1e-5,
                                             scalar2=None, op0=OP.max))
    p.drain("vector")
    p.op("vector", lambda q: q.reciprocal(out=rowA, in_=rowB))
    p.drain("vector")
    p.op("vector", lambda q: q.tensor_scalar(out=rowA, in0=rowA, scalar1=127.0,
                                             scalar2=None, op0=OP.mult))
    p.drain("vector")
    rows_tok = p.op("vector", lambda q: q.tensor_scalar(out=rowB, in0=rowB,
                                                        scalar1=wsc[0:1, 0:1], scalar2=1.0 / 127.0,
                                                        op0=OP.mult, op1=OP.mult), s_ve)
    p.wait("tensor", s_ve, rows_tok)
    p.wait("tensor", s_ve, wq_tok)
    for nb in range(NB):
        p.op("tensor", lambda q, o=pm[:, nb * 512:(nb + 1) * 512],
             r=rowA[0:1, nb * 512:(nb + 1) * 512]:
             q.matmul(o, ones_r[:], r, start=True, stop=True))
        p.op("tensor", lambda q, o=pb[:, nb * 512:(nb + 1) * 512],
             r=rowB[0:1, nb * 512:(nb + 1) * 512]:
             q.matmul(o, ones_r[:], r, start=True, stop=True),
             s_tr if nb == NB - 1 else None)
    bc_tok = p.sems["s_tr"].n
    p.wait("vector", s_tr, bc_tok)
    p.drain("vector")
    for j in range(NDT):
        p.op("vector", lambda q, jj=j: q.tensor_tensor(out=U2[:], in0=xr[jj][:],
                                                       in1=pm[:, 0:T], op=OP.mult))
        p.drain("vector")
        p.op("vector", lambda q, jj=j: q.tensor_scalar(
            out=kb[jj][:], in0=U2[:], scalar1=MAGIC, scalar2=MAGIC,
            op0=OP.add, op1=OP.subtract), s_ve)
        p.drain("vector")
    u3_tok = p.op("vector", lambda q: q.tensor_copy(out=U3[:], in_=pb[:, 0:T]), s_ve)
    kb_tok = u3_tok

    p.wait("tensor", s_ve, kb_tok)
    ge_tok = [0] * NDT
    ve_tok = [0] * NDT
    for do in range(NDT):
        ps = pm if do % 2 == 0 else pb
        ebuf = U2 if do % 2 == 0 else U1
        if do >= 2:
            p.wait("tensor", s_ve, ve_tok[do - 2])
        for nb in range(NB):
            for dtl in range(NDT):
                lhsT = wqT[dtl][:, do * 128:(do + 1) * 128]
                rhs = kb[dtl][:, nb * 512:(nb + 1) * 512]
                p.op("tensor", lambda q, o=ps[:, nb * 512:(nb + 1) * 512], l=lhsT, r=rhs,
                     st=(dtl == 0), sp=(dtl == NDT - 1):
                     q.matmul(o, l, r, start=st, stop=sp),
                     s_tr if (dtl == NDT - 1 and nb == NB - 1) else None)
        mmtok = p.sems["s_tr"].n
        p.wait("vector", s_tr, mmtok)
        p.drain("vector")
        p.op("vector", lambda q, e=ebuf, pss=ps: q.tensor_tensor(out=e[:], in0=pss[:, 0:T],
                                                                 in1=U3[:], op=OP.mult), s_ve)
        ve_tok[do] = p.sems["s_ve"].n
        p.drain("vector")
        p.op("vector", lambda q, dd=do, e=ebuf: q.scalar_tensor_tensor(
            out=xr[dd][:], in0=e[:], scalar=fbc[:, dd:dd + 1], in1=xr[dd][:],
            op0=OP.add, op1=OP.add), s_ve)
        ge_tok[do] = p.sems["s_ve"].n
        p.drain("vector")

    v_tok, g_tok = build_ln(p, B, xr, xr, xb, wait_tok=ge_tok[-1])
    B["tok_ln1_v"] = p.sems["s_ve"].n
    B["tok_ln1_g"] = p.sems["s_ge"].n


NH = 16
HD = 64
QB = 512
NQB = T // QB


def build_block2(p, es, B):
    xr, xb, hb = B["xr"], B["xb"], B["hb"]
    U1, U2, U3 = B["U1"], B["U2"], B["U3"]
    pm, pb = B["pm"], B["pb"]
    identf = B["identf"]
    dram = B["dram"]
    wj, wm = B["wj"], B["wm"]
    ones_r = B["ones_r"]

    s_tr = p.sem("s_tr")
    s_ve = p.sem("s_ve")
    s_se = p.sem("s_se")
    s_ge = p.sem("s_ge")
    s_ld2 = p.sem("s_ld2")
    s_xsp = p.sem("s_xsp")
    s_xre = p.sem("s_xre")
    s_wq = [p.sem("s_wq0"), p.sem("s_wq1")]
    s_wv = [p.sem("s_wv0"), p.sem("s_wv1")]
    s_wo = p.sem("s_wo")
    s_qw = [p.sem("s_qw0"), p.sem("s_qw1")]
    s_qr = [p.sem("s_qr0"), p.sem("s_qr1")]
    s_qrg = [p.sem("s_qrg0"), p.sem("s_qrg1")]
    s_bt = p.sem("s_bt")
    s_ac = p.sem("s_ac")
    s_rbc = p.sem("s_rbc")
    s_bf = p.sem("s_bf")
    s_ew = [p.sem("s_ew0"), p.sem("s_ew1")]
    s_er = [p.sem("s_er0"), p.sem("s_er1")]
    s_qe = [p.sem("s_qe0"), p.sem("s_qe1")]

    xrb = [xr[j][:, 0:T].bitcast(BF16) for j in range(NDT)]
    qkT = []
    for j in range(NDT):
        qkT.append(xrb[j][:, 0:T])
        qkT.append(xrb[j][:, T:2 * T])

    def a_sl(kt, c0, c1):
        base = (kt % 4) * 512
        return xb[kt // 4][:, base + c0: base + c1]

    u1b = U1[:, 0:T].bitcast(BF16)
    u2b = U2[:, 0:T].bitcast(BF16)
    u3b = U3[:, 0:T].bitcast(BF16)
    VE = []
    for ub in (u1b, u2b, u3b):
        for k in range(3):
            VE.append(ub[:, k * 1040:(k + 1) * 1040])
    for k in range(4, 8):
        VE.append(xb[k][:, 0:1040])
    vex = B["vex"]
    VE.extend(v[:, 0:1040] for v in vex)
    assert len(VE) == 16

    KEBt = wj[0]
    QELt = wj[1]
    qer = B["qer_t"]
    xin = B["xin"]
    MQS = xin[0][:, 0:D].bitcast(BF16)[0:16, 0:T]
    scm = dram_scm = B["dram"]["sc_mqs"]


    pekT = p.sbuf(es, "pekT", (64, 132), BF16)
    pekE = wj[0][0:64, 2048:2560]
    skq = wj[1][:, 2048:3072].bitcast(F32)
    pek_f = skq[:, 0:64]
    pkrow_f = skq[0:1, 64:128]
    pv_f = skq[:, 128:192]
    pvrow_f = skq[0:1, 192:256]
    pvb = p.sbuf(es, "pvb", (128, HD), BF16)
    pvb2 = p.sbuf(es, "pvb2", (1, HD), BF16)
    x1f = xin[1]
    x1bf = xin[1][:, 0:D].bitcast(BF16)
    pvE = [x1bf[0:128, 64 * k:64 * (k + 1)] for k in range(4)]
    onesbc = B["onesbc"]
    onesbr = B["onesbr"]
    selq = x1bf[0:128, 256:384]
    pksel0 = [x1bf[0:128, 384 + 16 * d:400 + 16 * d] for d in range(NDT)]
    pksel1 = [x1bf[0:128, 512 + 16 * d:528 + 16 * d] for d in range(NDT)]
    bqkvc = x1f[0:128, 320:344]
    kmx = x1f[0:16, 344:345]
    kpc = x1f[0:16, 345:346]

    boc = B["bmc"]
    zbuf = B["zbuf"]
    bfull = B["bfull"]
    btT = bfull
    srow = wm[1][:, 0:D].bitcast(F32)[0:1, 0:512]
    pkms = x1f[0:1, 346:347]

    wm5f = wm[5][:, 0:D].bitcast(F32)
    ccf = wm[2][:, 0:D].bitcast(F32)[0:64, 0:512]
    sab = wm[3][0:2, 0:1024]
    rbrow = wm[4][0:1, 0:512]
    qrst = [wm[5][:, 0:D].bitcast(F32)[:, 0:512], wm[0][:, 0:D].bitcast(F32)[:, 0:512]]
    wqb = [wm[6], wm[7]]
    wvb = wqb
    bvrow = wm[1][0:1, 0:D]

    W_qkv = dram["W_qkv"]
    b_qkv = dram["b_qkv"]
    W_out = dram["W_out"]
    rel_k = dram["rel_emb_k"]
    rel_v = dram["rel_emb_v"]
    xsp = dram["sc_xspill"]
    QRE = [dram["sc_qre0"], dram["sc_qre1"]]
    E2 = [dram["sc_e0"], dram["sc_e1"]]

    tok_prev_v = B["tok_ln1_v"]
    tok_prev_g = B["tok_ln1_g"]

    p.wait("sync", s_ve, tok_prev_v)
    p.wait("sync", s_ge, tok_prev_g)
    for j in range(NDT):
        p.dma("sync", bass.AP(xsp.tensor, j * 128 * T, [[T, 128], [1, T]]), xr[j][:, 0:T], s_xsp)

    for c in range(24):
        p.dma("sync", bqkvc[:, c:c + 1], col_ap(b_qkv, 128 * c), s_ld2)
    for j in range(NDT):
        p.dma("sync", boc[:, j:j + 1], col_ap(dram["b_out"], 128 * j), s_ld2)
    s_gd = p.sem("s_gd")
    p.dma("gpsimd", bvrow, bass.AP(b_qkv.tensor, 2048, [[1024, 1], [1, 1024]]), s_gd)
    p.dma("sync", selq, dram["c_selq"], s_ld2)
    p.dma("sync", onesbc[:], dram["c_ones_col_bf"], s_ld2)
    p.dma("sync", onesbr[:], dram["c_ones_row_bf"], s_ld2)
    p.dma("sync", pek_f, rel_k[0:128, :], s_ld2)
    p.dma("sync", pkrow_f, bass.AP(rel_k.tensor, 128 * 64, [[64, 1], [1, 64]]), s_ld2)
    p.dma("sync", pv_f, rel_v[0:128, :], s_ld2)
    p.dma("sync", pvrow_f, bass.AP(rel_v.tensor, 128 * 64, [[64, 1], [1, 64]]), s_ld2)
    ld2_tok = s_ld2.n

    p.wait("vector", s_ve, tok_prev_v)
    p.wait("vector", s_ld2, ld2_tok)
    p.drain("vector")
    zv_tok = p.op("vector", lambda q: q.memset(zbuf[:], 0.0), s_ve)
    p.drain("vector")
    bq_tok = p.op("vector", lambda q: q.tensor_scalar(out=bqkvc[:, 0:8], in0=bqkvc[:, 0:8],
                                                      scalar1=0.125, scalar2=None, op0=OP.mult), s_ve)
    p.wait("sync", s_ve, zv_tok)
    for par in range(2):
        for r in range(16):
            p.dma("sync", bass.AP(E2[par].tensor, r * 128 * 640, [[640, 128], [1, 128]]),
                  zbuf[:, 0:128], s_ew[par])
            p.dma("sync", bass.AP(E2[par].tensor, r * 128 * 640 + 512, [[640, 128], [1, 128]]),
                  zbuf[:, 0:128], s_ew[par])
        p.dma("sync", bass.AP(E2[par].tensor, 128, [[640, 128], [1, 128]]),
              zbuf[:, 0:128], s_ew[par])
        p.dma("sync", bass.AP(E2[par].tensor, 128 * 15 * 640 + 384, [[640, 128], [1, 128]]),
              zbuf[:, 0:128], s_ew[par])
    ez_tok = [s_ew[0].n, s_ew[1].n]

    p.wait("tensor", s_ve, tok_prev_v)
    p.wait("tensor", s_ld2, ld2_tok)
    p.op("tensor", lambda q: q.transpose(pm[0:64, 0:128], pek_f, identf[:]), None)
    p.op("tensor", lambda q: q.transpose(pm[0:64, 128:129], pkrow_f, identf[0:1, 0:1]), s_tr)
    tok_pek = p.sems["s_tr"].n
    p.wait("vector", s_tr, tok_pek)
    p.drain("vector")
    p.op("vector", lambda q: q.tensor_copy(out=pekT[:, 0:128], in_=pm[0:64, 0:128]))
    p.op("vector", lambda q: q.tensor_copy(out=pekT[:, 128:129], in_=pm[0:64, 128:129]))
    p.op("vector", lambda q: q.tensor_copy(out=pvb[:], in_=pv_f))
    p.op("vector", lambda q: q.tensor_copy(out=pvb2[:], in_=pvrow_f), s_ve)
    tok_pvb = p.sems["s_ve"].n
    p.drain("vector")
    p.op("vector", lambda q: q.tensor_copy(out=pekE[:, 0:192], in_=pekT[:, 0:1].to_broadcast((64, 192))))
    p.op("vector", lambda q: q.tensor_copy(out=pekE[:, 192:321], in_=pekT[:, 0:129]))
    p.op("vector", lambda q: q.tensor_copy(out=pekE[:, 321:512], in_=pekT[:, 128:129].to_broadcast((64, 191))), s_ve)
    tok_peke = p.sems["s_ve"].n
    p.drain("vector")
    p.op("vector", lambda q: q.memset(qer[64:128, 0:T], 0.0))
    p.op("vector", lambda q: q.memset(wj[1][64:128, 0:2048], 0.0))
    p.op("vector", lambda q: q.memset(wj[0][64:128, 2048:2560], 0.0), s_ve)
    tok_zpad = p.sems["s_ve"].n
    p.wait("tensor", s_ve, tok_pvb)
    p.op("tensor", lambda q: q.matmul(pb[0:128, 0:64], onesbr[:], pvb[0:1, 0:64], start=True, stop=True), None)
    p.op("tensor", lambda q: q.matmul(pb[0:128, 64:128], onesbr[:], pvb2[0:1, 0:64], start=True, stop=True), s_tr)
    tok_pvbc = p.sems["s_tr"].n
    p.wait("vector", s_tr, tok_pvbc)
    p.drain("vector")
    p.op("vector", lambda q: q.tensor_copy(out=pvE[0], in_=pb[0:128, 0:64]))
    p.op("vector", lambda q: q.tensor_copy(out=pvE[1][0:64, 0:64], in_=pb[0:64, 0:64]))
    p.op("vector", lambda q: q.tensor_copy(out=pvE[1][64:128, 0:64], in_=pvb[0:64, 0:64]))
    p.op("vector", lambda q: q.tensor_copy(out=pvE[2][0:64, 0:64], in_=pvb[64:128, 0:64]))
    p.op("vector", lambda q: q.tensor_copy(out=pvE[2][64:128, 0:64], in_=pb[64:128, 64:128]))
    p.op("vector", lambda q: q.tensor_copy(out=pvE[3], in_=pb[0:128, 64:128]), s_ve)
    tok_pve = p.sems["s_ve"].n
    p.drain("vector")
    for d in range(NDT):
        p.op("vector", lambda q, dd=d: q.memset(pksel0[dd], 0.0))
        p.op("vector", lambda q, dd=d: q.memset(pksel1[dd], 0.0))
    p.drain("vector")
    for d in range(NDT):
        p.op("vector", lambda q, dd=d: q.tensor_copy(out=pksel0[dd][0:64, 2 * dd:2 * dd + 1], in_=pekT[:, 0:1]))
        p.op("vector", lambda q, dd=d: q.tensor_copy(out=pksel0[dd][64:128, 2 * dd + 1:2 * dd + 2], in_=pekT[:, 0:1]))
        p.op("vector", lambda q, dd=d: q.tensor_copy(out=pksel1[dd][0:64, 2 * dd:2 * dd + 1], in_=pekT[:, 128:129]))
        p.op("vector", lambda q, dd=d: q.tensor_copy(out=pksel1[dd][64:128, 2 * dd + 1:2 * dd + 2], in_=pekT[:, 128:129]),
             s_ve if d == NDT - 1 else None)
    tok_sel = p.sems["s_ve"].n
    p.wait("vector", s_ge, tok_prev_g)
    p.drain("vector")
    tok_keb1 = p.op("vector", lambda q: q.memset(KEBt[64:66, 0:T], 1.0), s_ve)

    def load_wq(dd):
        dst = wqb[dd % 2]
        srcap = bass.AP(W_qkv.tensor, 128 * dd, [[3072, 128], [3072 * 128, 8], [1, 128]])
        p.dma("gpsimd", dst[:, 0:1024], srcap, s_wq[dd % 2])

    p.wait("gpsimd", s_xsp, s_xsp.n)
    p.wait("gpsimd", s_ve, tok_prev_v)
    load_wq(0)
    load_wq(1)

    p.wait("tensor", s_xsp, s_xsp.n)
    p.wait("tensor", s_ve, tok_pvb)
    evq_tok = 0
    for dd in range(16):
        p.wait("tensor", s_wq[dd % 2], 16 * (dd // 2 + 1))
        if evq_tok:
            p.wait("tensor", s_se, evq_tok)
        for nb in range(NB):
            for dt in range(NDT):
                lhsT = wqb[dd % 2][:, dt * 128:(dt + 1) * 128]
                rhs = xb[dt][:, nb * 512:(nb + 1) * 512]
                p.op("tensor", lambda q, o=pm[:, nb * 512:(nb + 1) * 512], l=lhsT, r=rhs,
                     st=(dt == 0), sp=(dt == NDT - 1):
                     q.matmul(o, l, r, start=st, stop=sp),
                     s_tr if (dt == NDT - 1 and nb == NB - 1) else None)
        mmtok = p.sems["s_tr"].n
        p.wait("scalar", s_tr, mmtok)
        if dd == 0:
            p.wait("scalar", s_xsp, s_xsp.n)
        sc = 0.125 if dd < 8 else 1.0
        for nb in range(NB):
            p.op("scalar", lambda q, o=qkT[dd][:, nb * 512:(nb + 1) * 512],
                 i_=pm[:, nb * 512:(nb + 1) * 512], b=bqkvc[:, dd:dd + 1], s=sc:
                 q.activation(out=o, in_=i_, func=AF.Identity, bias=b, scale=s),
                 s_se if nb == NB - 1 else None)
        evq_tok = p.sems["s_se"].n
        if dd + 2 < 16:
            p.wait("gpsimd", s_tr, mmtok)
            load_wq(dd + 2)

    def load_wv(dt):
        p.dma("gpsimd", wvb[dt % 2][:], W_qkv[128 * dt:128 * (dt + 1), 2048:3072], s_wv[dt % 2])

    p.wait("tensor", s_se, evq_tok)
    p.wait("tensor", s_ve, tok_pve)
    p.wait("tensor", s_gd, p.sems["s_gd"].n)
    ev_tok = 0
    dt_tok = {}
    nwv = [0, 0]
    last_mm_tok = p.sems["s_tr"].n
    for g in range(4):
        p.wait("gpsimd", s_tr, last_mm_tok)
        load_wv(0)
        load_wv(1)
        nwv[0] += 16
        nwv[1] += 16
        if ev_tok:
            p.wait("tensor", s_ve, ev_tok)
        for dt in range(NDT):
            p.wait("tensor", s_wv[dt % 2], nwv[dt % 2])
            for ti in range(4):
                tt = g * 4 + ti
                lhsT = xb[dt][:, tt * 128:(tt + 1) * 128]
                psA = pm if ti < 2 else pb
                off = (ti % 2) * 1024
                for half in range(2):
                    p.op("tensor", lambda q, o=psA[:, off + half * 512:off + (half + 1) * 512],
                         l=lhsT, r=wvb[dt % 2][:, half * 512:(half + 1) * 512], st=(dt == 0):
                         q.matmul(o, l, r, start=st, stop=False),
                         s_tr if (ti == 3 and half == 1) else None)
            dt_tok[(g, dt)] = p.sems["s_tr"].n
            if dt + 2 < NDT:
                p.wait("gpsimd", s_tr, dt_tok[(g, dt)])
                load_wv(dt + 2)
                nwv[dt % 2] += 16
        for ti in range(4):
            tt = g * 4 + ti
            psA = pm if ti < 2 else pb
            off = (ti % 2) * 1024
            for half in range(2):
                p.op("tensor", lambda q, o=psA[:, off + half * 512:off + (half + 1) * 512],
                     r=bvrow[0:1, half * 512:(half + 1) * 512]:
                     q.matmul(o, onesbr[:], r, start=False, stop=True),
                     s_tr if (ti == 3 and half == 1) else None)
        last_mm_tok = p.sems["s_tr"].n
        vtok = last_mm_tok
        p.wait("vector", s_tr, vtok)
        p.drain("vector")
        for ti in range(4):
            tt = g * 4 + ti
            psA = pm if ti < 2 else pb
            off = (ti % 2) * 1024
            for h in range(NH):
                p.op("vector", lambda q, t=tt, hh=h, ps2=psA, o2=off: q.tensor_copy(
                    out=VE[t][:, 65 * hh:65 * hh + 64], in_=ps2[:, o2 + hh * 64:o2 + (hh + 1) * 64]),
                    s_ve if (ti == 3 and h == NH - 1) else None)
        ev_tok = p.sems["s_ve"].n
    for tt in range(NTT):
        for h in range(NH):
            p.op("vector", lambda q, t=tt, hh=h: q.tensor_copy(
                out=VE[t][:, 65 * hh + 64:65 * hh + 65], in_=onesbc[:]),
                s_ve if (tt == NTT - 1 and h == NH - 1) else None)
    tok_ve = p.sems["s_ve"].n

    for (base, ps) in ((0, pm), (8, pb)):
        for nb in range(NB):
            for dt in range(NDT):
                p.op("vector", lambda q, d=dt, b=base, n=nb: q.tensor_tensor(
                    out=bfull[:, 0:512], in0=qkT[b + d][:, n * 512:(n + 1) * 512],
                    in1=qkT[b + d][:, n * 512:(n + 1) * 512], op=OP.mult), s_ve)
                sq_tok = p.sems["s_ve"].n
                p.wait("tensor", s_ve, sq_tok)
                p.op("tensor", lambda q, o=ps[0:16, nb * 512:(nb + 1) * 512], d=dt:
                     q.matmul(o, selq[:, 16 * d:16 * (d + 1)], bfull[:, 0:512],
                              start=(d == 0), stop=(d == NDT - 1)), s_tr)
                p.wait("vector", s_tr, p.sems["s_tr"].n)
    nrm_tok = p.sems["s_tr"].n
    p.wait("scalar", s_tr, nrm_tok)
    p.op("scalar", lambda q: q.activation(out=pm[32:48, 0:T], in_=pm[0:16, 0:T], func=AF.Sqrt), None)
    p.op("scalar", lambda q: q.activation(out=pb[32:48, 0:T], in_=pb[0:16, 0:T], func=AF.Sqrt), s_se)
    sq2_tok = p.sems["s_se"].n
    p.wait("vector", s_se, sq2_tok)
    p.drain("vector")
    p.op("vector", lambda q: q.tensor_reduce(out=kmx.rearrange("p (a o) -> p a o", a=1),
                                             in_=pb[32:48, 0:T].rearrange("p (a j) -> p a j", a=1),
                                             axis=AX.X, op=OP.max), s_ve)
    kmx_tok = p.sems["s_ve"].n
    p.drain("vector")
    p.op("vector", lambda q: q.tensor_tensor(out=btT[0:64, 0:129], in0=pekT[:, 0:129],
                                             in1=pekT[:, 0:129], op=OP.mult), s_ve)
    pks_tok = p.sems["s_ve"].n
    p.wait("tensor", s_ve, pks_tok)
    p.op("tensor", lambda q: q.matmul(pb[0:1, 1536:1665], onesbc[0:64, 0:1], btT[0:64, 0:129],
                                      start=True, stop=True), s_tr)
    pkn_tok = p.sems["s_tr"].n
    p.wait("scalar", s_tr, pkn_tok)
    p.op("scalar", lambda q: q.activation(out=srow[0:1, 0:129], in_=pb[0:1, 1536:1665], func=AF.Sqrt), s_se)
    pkn2_tok = p.sems["s_se"].n
    p.wait("vector", s_se, pkn2_tok)
    p.drain("vector")
    p.op("vector", lambda q: q.tensor_reduce(out=pkms.rearrange("p (a o) -> p a o", a=1),
                                             in_=srow[0:1, 0:129].rearrange("p (a j) -> p a j", a=1),
                                             axis=AX.X, op=OP.max), s_ve)
    pkm_tok = p.sems["s_ve"].n
    p.wait("tensor", s_ve, pkm_tok)
    p.op("tensor", lambda q: q.matmul(pb[0:16, 1792:1793], ones_r[0:1, 0:16], pkms,
                                      start=True, stop=True), s_tr)
    pkb_tok = p.sems["s_tr"].n
    p.wait("vector", s_tr, pkb_tok)
    p.drain("vector")
    p.op("vector", lambda q: q.tensor_tensor(out=kpc, in0=kmx, in1=pb[0:16, 1792:1793],
                                             op=OP.add), s_ve)
    p.drain("vector")
    m16_tok = p.op("vector", lambda q: q.tensor_scalar(out=MQS, in0=pm[32:48, 0:T], scalar1=kpc,
                                                       scalar2=-1.0, op0=OP.mult, op1=OP.mult), s_ve)
    p.wait("sync", s_ve, m16_tok)
    p.dma("sync", bass.AP(scm.tensor, 0, [[T, 16], [1, T]]), MQS, s_ld2)
    p.wait("tensor", s_ve, m16_tok)
    p.wait("tensor", s_ve, tok_sel)
    for (sel, ps) in ((pksel0, pm), (pksel1, pb)):
        for nb in range(NB):
            for d in range(NDT):
                p.op("tensor", lambda q, o=ps[0:16, nb * 512:(nb + 1) * 512], dd=d, s=sel:
                     q.matmul(o, s[dd][:], qkT[dd][:, nb * 512:(nb + 1) * 512],
                              start=(dd == 0), stop=(dd == NDT - 1)),
                     s_tr if d == NDT - 1 else None)
    qs_tok = p.sems["s_tr"].n
    p.wait("scalar", s_tr, qs_tok)
    p.wait("scalar", s_ld2, s_ld2.n)
    qs0_tok = p.op("scalar", lambda q: q.activation(out=MQS, in_=pm[0:16, 0:T], func=AF.Copy), s_se)
    p.wait("sync", s_se, qs0_tok)
    p.dma("sync", bass.AP(scm.tensor, 16 * T, [[T, 16], [1, T]]), MQS, s_ld2)
    p.wait("scalar", s_ld2, s_ld2.n)
    qs1_tok = p.op("scalar", lambda q: q.activation(out=MQS, in_=pb[0:16, 0:T], func=AF.Copy), s_se)
    p.wait("sync", s_se, qs1_tok)
    p.dma("sync", bass.AP(scm.tensor, 32 * T, [[T, 16], [1, T]]), MQS, s_ld2)
    qs2_tok = p.sems["s_se"].n

    p.wait("vector", s_se, qs2_tok)
    p.wait("vector", s_ve, tok_ve)
    p.wait("vector", s_ve, tok_keb1)
    prev_pe_done = 0
    last_sc = 0
    er_need = [0, 0]
    last_skq_use = 0
    prev_exp_tok = [0, 0]
    hb_tok = 0
    stA = [xb[4 + q][:, 1040:1424] for q in range(4)]
    stB = [xb[4 + q][:, 1424:1936] for q in range(4)]
    stC = [u1b[:, 3120:3632], u2b[:, 3120:3632], u3b[:, 3120:3632], x1bf[0:128, 694:1206]]
    pchunk = [pb[:, 512:1024].bitcast(BF16), pb[:, 1024:1536].bitcast(BF16)]

    def bchunk(c):
        return pchunk[c // 8][:, (c % 8) * 128:(c % 8 + 1) * 128]

    st2 = {"pend": None, "be": 0, "hb": 0, "ctxfree": [0, 0]}

    def b2_tail():
        pd = st2["pend"]
        if pd is None:
            return
        qbp, ctxp, rbcp, hp, parp, ern, fep = pd
        p.wait("tensor", s_er[parp], ern)
        p.wait("tensor", s_se, fep)
        for qtl in range(4):
            for ws in range(4):
                p.op("tensor", lambda q, o=bchunk(qtl * 4 + ws),
                     i_=stB[qtl][:, ws * 128:(ws + 1) * 128], idb=B["identb"]:
                     q.matmul(o, i_, idb[:], is_transpose=True, start=True, stop=True),
                     s_bt if (qtl == 3 and ws == 3) else None)
        bt2 = p.sems["s_bt"].n
        p.wait("scalar", s_bt, bt2)
        for qtl in range(4):
            p.op("scalar", lambda q, o=stC[qtl],
                 i_=pchunk[qtl // 2][:, (qtl % 2) * 512:(qtl % 2 + 1) * 512]:
                 q.activation(out=o, in_=i_, func=AF.Copy),
                 s_se if qtl == 3 else None)
        st2["be"] = p.sems["s_se"].n
        p.op("scalar", lambda q: q.activation(out=sab[0:1, 0:512], in_=pb[0:1, 0:512], func=AF.Copy), None)
        p.op("scalar", lambda q: q.activation(out=sab[0:1, 512:1024], in_=pb[0:1, 1536:2048], func=AF.Copy), s_se)
        sat_tok = p.sems["s_se"].n
        p.wait("tensor", s_se, st2["be"])
        for qtl in range(4):
            qtoff = qtl * 128
            for ws in range(4):
                p.op("tensor", lambda q, o=ctxp[0:64, qtoff:qtoff + 128], w=ws, s2=stC[qtl]:
                     q.matmul(o, pvE[w], s2[:, w * 128:(w + 1) * 128], start=False, stop=False), None)
        p.wait("tensor", s_se, sat_tok)
        p.op("tensor", lambda q, o=ctxp: q.matmul(o[0:64, 0:512], pvb[0:1, 0:64], sab[0:1, 0:512], start=False, stop=False), None)
        p.op("tensor", lambda q, o=ctxp: q.matmul(o[0:64, 0:512], pvb2[0:1, 0:64], sab[0:1, 512:1024], start=False, stop=False), None)
        p.op("tensor", lambda q, o=ctxp: q.matmul(o[0:65, 0:512], zbuf[0:1, 0:65], MQS[0:1, 0:512], start=False, stop=True), s_ac)
        a_close = p.sems["s_ac"].n
        p.wait("vector", s_ac, a_close)
        p.drain("vector")
        p.op("vector", lambda q, c2=ctxp: q.reciprocal(out=srow[0:1, 0:512], in_=c2[64:65, 0:512]))
        p.drain("vector")
        rb_tok = p.op("vector", lambda q: q.tensor_copy(out=rbrow[:], in_=srow[0:1, 0:512]), s_ve)
        p.wait("tensor", s_ve, rb_tok)
        p.op("tensor", lambda q, o=rbcp: q.matmul(o, onesbr[0:1, 0:64], rbrow[:], start=True, stop=True), s_rbc)
        rbc_tok = p.sems["s_rbc"].n
        p.drain("vector")
        p.op("vector", lambda q, c2=ctxp: q.tensor_copy(out=ccf[:, 0:512], in_=c2[0:64, 0:512]))
        p.wait("vector", s_rbc, rbc_tok)
        p.drain("vector")
        st2["hb"] = p.op("vector", lambda q, hh=hp, b0=qbp * 512, r2=rbcp: q.tensor_tensor(
            out=hb[hh // 2][64 * (hh % 2):64 * (hh % 2) + 64, b0:b0 + 512],
            in0=ccf[:, 0:512], in1=r2, op=OP.mult), s_ve)
        st2["ctxfree"][qbp % 2] = st2["hb"]
        st2["pend"] = None

    for h in range(NH):
        par = h % 2
        if last_sc:
            p.wait("vector", s_tr, last_sc)
        p.drain("vector")
        p.op("vector", lambda q, hh=h: q.tensor_copy(
            out=KEBt[0:64, 0:T], in_=qkT[8 + hh // 2][64 * (hh % 2):64 * (hh % 2) + 64, 0:T]))
        p.op("vector", lambda q, hh=h: q.tensor_copy(
            out=QELt[0:64, 0:T], in_=qkT[hh // 2][64 * (hh % 2):64 * (hh % 2) + 64, 0:T]))
        cp_tok = p.op("vector", lambda q, hh=h: q.tensor_copy(
            out=qer[0:64, 0:T], in_=qkT[hh // 2][64 * (hh % 2):64 * (hh % 2) + 64, 0:T]), s_ve)
        p.wait("tensor", s_ve, cp_tok)
        if h >= 2:
            p.wait("tensor", s_qr[par], s_qr[par].n)
        p.wait("scalar", s_tr, last_skq_use)
        p.wait("tensor", s_se, p.sems["s_se"].n)
        qr_ev = [0, 0]
        qw_done = [0, 0]
        for qt in range(NTT):
            pp = qt % 2
            bank = pb[:, 512 + pp * 512:1024 + pp * 512]
            if qr_ev[pp]:
                p.wait("tensor", s_se, qr_ev[pp])
            p.op("tensor", lambda q, o=bank, t=qt:
                 q.matmul(o, QELt[0:128, 128 * t:128 * (t + 1)],
                          wj[0][0:128, 2048:2560], start=True, stop=True), s_tr)
            t_tok = p.sems["s_tr"].n
            p.wait("scalar", s_tr, t_tok)
            if qw_done[pp]:
                p.wait("scalar", s_qw[par], s_qw[par].n)
            p.op("scalar", lambda q, o=qrst[pp], i_=bank: q.activation(out=o, in_=i_, func=AF.Copy), s_se)
            qr_ev[pp] = p.sems["s_se"].n
            p.wait("scalar", s_se, qr_ev[pp])
            p.dma("scalar", bass.AP(QRE[par].tensor, 128 * qt * 512, [[512, 128], [1, 512]]),
                  qrst[pp], s_qw[par])
            qw_done[pp] = s_qw[par].n
        qw_tok = s_qw[par].n
        p.wait("tensor", s_se, p.sems["s_se"].n)
        p.wait("sync", s_ve, cp_tok)
        p.wait("sync", s_ld2, s_ld2.n)
        p.wait("sync", s_tr, p.sems["s_tr"].n)
        p.dma("sync", QELt[64:65, 0:T], bass.AP(scm.tensor, h * T, [[T, 1], [1, T]]), s_qe[par])
        p.dma("sync", QELt[65:66, 0:T], bass.AP(scm.tensor, (16 + h) * T, [[T, 1], [1, T]]), s_qe[par])
        p.dma("sync", qer[64:65, 0:T], bass.AP(scm.tensor, h * T, [[T, 1], [1, T]]), s_qe[par])
        p.dma("sync", qer[65:66, 0:T], bass.AP(scm.tensor, (32 + h) * T, [[T, 1], [1, T]]), s_qe[par])
        qe_tok = s_qe[par].n

        tstg = [skq[:, 0:128], skq[:, 128:256], skq[:, 256:384], skq[:, 384:512],
                qrst[0][:, 0:128], qrst[0][:, 128:256], qrst[0][:, 256:384], qrst[0][:, 384:512],
                qrst[1][:, 0:128], qrst[1][:, 128:256], qrst[1][:, 256:384], qrst[1][:, 384:512]]
        for qb in range(NQB):
            qb0 = qb * 512
            cbank = 1024 + 512 * (qb % 2)
            ctxA = pm[0:66, cbank:cbank + 512]
            rbcA = pm[64:128, cbank:cbank + 512]
            tchunks = {}
            tlist = []
            for kt in range(NTT):
                k0 = 128 * kt
                b0 = max(qb0, k0 - 64)
                b1 = min(qb0 + 512, k0 + 192)
                if b0 < b1:
                    c0 = b0
                    while c0 < b1:
                        cw = min(128, b1 - c0)
                        tchunks[(kt, c0)] = (len(tlist), cw)
                        tlist.append((kt, c0, cw))
                        c0 += cw
            assert len(tlist) <= 12
            p.wait("sync", s_qw[par], qw_tok)
            p.wait("sync", s_tr, last_skq_use)
            p.wait("gpsimd", s_qw[par], qw_tok)
            p.wait("gpsimd", s_tr, last_skq_use)
            for idx, (kt, c0, cw) in enumerate(tlist):
                p.dma("sync" if idx % 2 == 0 else "gpsimd", tstg[idx][0:cw, 0:128],
                      bass.AP(QRE[par].tensor, c0 * 511 + 128 * kt + 256, [[511, cw], [1, 128]]),
                      s_qr[par] if idx % 2 == 0 else s_qrg[par])
            tload_n = s_qr[par].n
            tload_g = s_qrg[par].n
            first_t = True
            p.wait("tensor", s_qe[par], qe_tok)
            if st2["ctxfree"][qb % 2]:
                p.wait("tensor", s_ve, st2["ctxfree"][qb % 2])
            for kt in range(NTT):
                if kt == 3:
                    b2_tail()
                k0 = 128 * kt
                b0 = max(qb0, k0 - 64)
                b1 = min(qb0 + 512, k0 + 192)
                Ls = max(qb0, k0 + 192)
                Rs = min(qb0 + 512, k0 - 64)
                pp = kt % 2
                sc_ps = pm[:, pp * 512:(pp + 1) * 512]
                if prev_exp_tok[pp]:
                    p.wait("tensor", s_se, prev_exp_tok[pp])
                ops = []
                if b0 < b1:
                    c0 = b0
                    while c0 < b1:
                        cw = min(128, b1 - c0)
                        ops.append(("T", c0, cw))
                        c0 += cw
                if Rs > qb0:
                    ops.append(("R", qb0, Rs))
                if b0 < b1:
                    ops.append(("B", b0, b1))
                if Ls < qb0 + 512:
                    ops.append(("L", Ls, qb0 + 512))
                nops = len(ops)
                for oi, op_ in enumerate(ops):
                    st = (oi == 0)
                    sp = (oi == nops - 1)
                    if op_[0] == "T":
                        _, c0, cw = op_
                        if first_t:
                            p.wait("tensor", s_qr[par], tload_n)
                            p.wait("tensor", s_qrg[par], tload_g)
                            first_t = False
                        idx, _cw = tchunks[(kt, c0)]
                        p.op("tensor", lambda q, o=sc_ps[:, c0 - qb0:c0 - qb0 + cw], i_=tstg[idx][0:cw, 0:128],
                             idf=identf[0:cw, 0:cw], s1=st, s2=sp:
                             q.matmul(o, i_, idf, is_transpose=True, start=s1, stop=s2), s_tr)
                    elif op_[0] == "R":
                        _, a0, a1 = op_
                        p.op("tensor", lambda q, o=sc_ps[:, a0 - qb0:a1 - qb0], l=KEBt[0:128, k0:k0 + 128],
                             r=qer[0:128, a0:a1], s1=st, s2=sp:
                             q.matmul(o, l, r, start=s1, stop=s2), s_tr)
                    elif op_[0] == "B":
                        _, a0, a1 = op_
                        p.op("tensor", lambda q, o=sc_ps[:, a0 - qb0:a1 - qb0], l=KEBt[0:65, k0:k0 + 128],
                             r=QELt[0:65, a0:a1], s1=st, s2=sp:
                             q.matmul(o, l, r, start=s1, stop=s2), s_tr)
                    else:
                        _, a0, a1 = op_
                        p.op("tensor", lambda q, o=sc_ps[:, a0 - qb0:a1 - qb0], l=KEBt[0:128, k0:k0 + 128],
                             r=QELt[0:128, a0:a1], s1=st, s2=sp:
                             q.matmul(o, l, r, start=s1, stop=s2), s_tr)
                sc_tok = p.sems["s_tr"].n
                last_sc = sc_tok
                if any(o[0] == "T" for o in ops):
                    last_skq_use = sc_tok
                p.wait("scalar", s_tr, sc_tok)
                p.op("scalar", lambda q, o=a_sl(kt, 0, 512), i_=sc_ps:
                     q.activation(out=o, in_=i_, func=AF.Exp), s_se)
                prev_exp_tok[pp] = p.sems["s_se"].n
                p.wait("tensor", s_se, prev_exp_tok[pp])
                p.op("tensor", lambda q, o=ctxA[0:65, 0:512], l=VE[kt][:, 65 * h:65 * h + 65],
                     r=a_sl(kt, 0, 512), st=(kt == 0):
                     q.matmul(o, l, r, start=st, stop=False), None)
            p.op("tensor", lambda q: q.matmul(pb[0:1, 0:512], zbuf[0:1, 0:1], MQS[0:1, 0:512],
                                              start=True, stop=False), None)
            p.op("tensor", lambda q: q.matmul(pb[0:1, 1536:2048], zbuf[0:1, 0:1], MQS[0:1, 0:512],
                                              start=True, stop=False), None)
            for kt in range(NTT):
                k0 = 128 * kt
                Ls = max(qb0, k0 + 192)
                Rs = min(qb0 + 512, k0 - 64)
                if Ls < qb0 + 512:
                    p.op("tensor", lambda q, o=pb[0:1, Ls - qb0:512], r=a_sl(kt, Ls - qb0, 512):
                         q.matmul(o, onesbc[:], r, start=False, stop=False), None)
                if Rs > qb0:
                    p.op("tensor", lambda q, o=pb[0:1, 1536:1536 + Rs - qb0], r=a_sl(kt, 0, Rs - qb0):
                         q.matmul(o, onesbc[:], r, start=False, stop=False), None)
            p.op("tensor", lambda q: q.matmul(pb[0:1, 0:512], zbuf[0:1, 0:1], MQS[0:1, 0:512],
                                              start=False, stop=True), None)
            p.op("tensor", lambda q: q.matmul(pb[0:1, 1536:2048], zbuf[0:1, 0:1], MQS[0:1, 0:512],
                                              start=False, stop=True), s_tr)
            if st2["be"]:
                p.wait("tensor", s_se, st2["be"])
            fops = [(qtl, i, qb * 4 + qtl + i - 1) for qtl in range(4) for i in range(3)
                    if 0 <= qb * 4 + qtl + i - 1 < NTT]
            for oi, (qtl, i, kt2) in enumerate(fops):
                p.op("tensor", lambda q, o=bchunk(qtl * 4 + i),
                     i_=a_sl(kt2, qtl * 128, qtl * 128 + 128), idb=B["identb"]:
                     q.matmul(o, i_, idb[:], is_transpose=True, start=True, stop=True),
                     s_bf if oi == len(fops) - 1 else None)
            ft = p.sems["s_bf"].n
            vi = [(1 if qb * 4 + qtl == 0 else 0, 2 if qb * 4 + qtl == NTT - 1 else 3)
                  for qtl in range(4)]
            p.wait("scalar", s_bf, ft)
            p.wait("scalar", s_ew[par], s_ew[par].n)
            for qtl in range(4):
                v0, v1 = vi[qtl]
                p.op("scalar", lambda q, o=stA[qtl][:, 0:(v1 - v0) * 128],
                     i_=pchunk[qtl // 2][:, (qtl % 2 * 4 + v0) * 128:(qtl % 2 * 4 + v1) * 128]:
                     q.activation(out=o, in_=i_, func=AF.Copy),
                     s_se if qtl == 3 else None)
            fe = p.sems["s_se"].n
            p.wait("sync", s_se, fe)
            p.wait("scalar", s_se, fe)
            if h >= 2:
                p.wait("sync", s_er[par], er_need[par])
                p.wait("scalar", s_er[par], er_need[par])
            for qtl in range(4):
                qt = qb * 4 + qtl
                v0, v1 = vi[qtl]
                p.dma("sync" if qtl % 2 == 0 else "scalar",
                      bass.AP(E2[par].tensor, 128 * qt * 640 + 128 + v0 * 128,
                              [[640, 128], [1, (v1 - v0) * 128]]),
                      stA[qtl][:, 0:(v1 - v0) * 128], s_ew[par])
            p.wait("sync", s_ew[par], s_ew[par].n)
            for qtl in range(4):
                qt = qb * 4 + qtl
                p.dma("sync", stB[qtl], bass.AP(E2[par].tensor, 128 * qt * 640,
                                                [[641, 128], [1, 512]]), s_er[par])
            er_need[par] = s_er[par].n
            st2["pend"] = (qb, ctxA, rbcA, h, par, s_er[par].n, fe)
        p.op("tensor", lambda q: q.nop(), s_tr)
        prev_pe_done = p.sems["s_tr"].n

    b2_tail()
    hb_tok = st2["hb"]
    p.op("tensor", lambda q: q.nop(), s_tr)
    prev_pe_done = p.sems["s_tr"].n

    p.wait("sync", s_tr, prev_pe_done)
    for j in range(NDT):
        p.dma("sync", xr[j][:, 0:T], bass.AP(xsp.tensor, j * 128 * T, [[T, 128], [1, T]]), s_xre)
    p.wait("gpsimd", s_tr, prev_pe_done)
    for dt in range(NDT):
        p.dma("gpsimd", VE[dt][:, 0:1024], W_out[128 * dt:128 * (dt + 1), :], s_wo)
    p.wait("tensor", s_wo, s_wo.n)
    p.wait("tensor", s_xre, s_xre.n)
    p.wait("tensor", s_ve, hb_tok)
    r_tok = [0] * NDT
    for do in range(NDT):
        ps = pm if do % 2 == 0 else pb
        if do >= 2:
            p.wait("tensor", s_ve, r_tok[do - 2])
        for nb in range(NB):
            for dt in range(NDT):
                p.op("tensor", lambda q, o=ps[:, nb * 512:(nb + 1) * 512], d=dt, dd=do, n=nb:
                     q.matmul(o, VE[d][:, dd * 128:(dd + 1) * 128], hb[d][:, n * 512:(n + 1) * 512],
                              start=(d == 0), stop=(d == NDT - 1)),
                     s_tr if (dt == NDT - 1 and nb == NB - 1) else None)
        mmtok = p.sems["s_tr"].n
        p.wait("vector", s_tr, mmtok)
        p.drain("vector")
        for nb in range(NB):
            p.op("vector", lambda q, o=xr[do][:, nb * 512:(nb + 1) * 512],
                 i0=ps[:, nb * 512:(nb + 1) * 512], sc=boc[:, do:do + 1],
                 i1=xr[do][:, nb * 512:(nb + 1) * 512]:
                 q.scalar_tensor_tensor(out=o, in0=i0, scalar=sc, in1=i1,
                                        op0=OP.add, op1=OP.add),
                 s_ve if nb == NB - 1 else None)
        r_tok[do] = p.sems["s_ve"].n

    build_ln(p, B, xr, xr, xb, wait_tok=r_tok[-1])
    B["tok_ln2_v"] = p.sems["s_ve"].n
    B["tok_ln2_g"] = p.sems["s_ge"].n


LH3 = 8
NW3 = 4
NHT3 = 22


def build_block3(p, es, B):
    xr, xb, hb = B["xr"], B["xb"], B["hb"]
    U1, U2, U3 = B["U1"], B["U2"], B["U3"]
    pm, pb = B["pm"], B["pb"]
    dram = B["dram"]
    wj, wm = B["wj"], B["wm"]
    xin = B["xin"]
    vex = B["vex"]
    identb = B["identb"]
    onesbc, onesbr = B["onesbc"], B["onesbr"]
    zbuf, bfull = B["zbuf"], B["bfull"]
    qert = B["qer_t"]
    gext = B["gext"]
    tbk = gext
    cnL, cnR = B["cnL"], B["cnR"]
    sel8h, selfq = B["sel8h"], B["selfq"]
    cm = B["cmisc"]
    b1c = B["b1c"]
    amx = B["amaxc"]

    s_tr = p.sem("s_tr")
    s_ve = p.sem("s_ve")
    s_se = p.sem("s_se")
    s_ge = p.sem("s_ge")
    s_l3 = p.sem("s_l3")
    s_w3 = [p.sem("s_w30"), p.sem("s_w31")]
    s_wo3 = p.sem("s_wo3")
    s_mb = p.sem("s_mb")
    s_mg = p.sem("s_mg")
    s_mb2 = p.sem("s_mb2")
    s_mg2 = p.sem("s_mg2")
    s_tg = p.sem("s_tg")
    s_w2 = [p.sem("s_w20"), p.sem("s_w21")]
    s_x3 = p.sem("s_x3")
    s_xr3 = [p.sem("s_xr3a"), p.sem("s_xr3b"), p.sem("s_xr3c")]

    u1b = U1[:, 0:T].bitcast(BF16)
    u2b = U2[:, 0:T].bitcast(BF16)
    u3b = U3[:, 0:T].bitcast(BF16)
    x0b = xin[0][:, 0:D].bitcast(BF16)
    x1b = xin[1][:, 0:D].bitcast(BF16)

    QKV = [hb[j][:, 0:T] for j in range(8)] + \
          [u1b[:, 0:T], u1b[:, T:2 * T], u2b[:, 0:T], u2b[:, T:2 * T]]
    hsh = [xb[j][:, 0:T] for j in range(4)] + \
          [u3b[:, 0:T], u3b[:, T:2 * T], x0b[0:128, 0:T], x1b[0:128, 0:T]]
    T1q = wj[0][:, 0:T]
    T2q = wj[1][:, 0:T]
    T1k = qert[:, 0:T]
    T2k = tbk[:, 0:T]
    VEL = [wj[0][:, c * 520:(c + 1) * 520] for c in range(3)] + \
          [wj[1][:, c * 520:(c + 1) * 520] for c in range(3)] + \
          [qert[:, c * 520:(c + 1) * 520] for c in range(3)] + \
          [gext[:, c * 520:(c + 1) * 520] for c in range(3)] + \
          [vex[0][:, 0:520], vex[0][:, 520:1040], vex[1][:, 0:520], vex[1][:, 520:1040]]
    assert len(VEL) == 16
    gg = [u1b[:, 0:T], u1b[:, T:2 * T], u2b[:, 0:T], u2b[:, T:2 * T]] + \
         [hb[j][:, 0:T] for j in range(8)] + \
         [xb[j][:, 0:T] for j in range(4, 8)] + \
         [wj[0][:, 0:T], wj[1][:, 0:T], qert[:, 0:T], gext[:, 0:T]]
    assert len(gg) == NHT3 - 2
    gg20c = [vex[0][:, 0:512], vex[0][:, 512:1024], vex[1][:, 0:512], vex[1][:, 512:1024]]
    gg21c = [wj[0][0:42, 2048:2560], wj[0][0:42, 2560:3072],
             wj[1][0:42, 2048:2560], wj[1][0:42, 2560:3072]]

    def ggsl(k, nb):
        if k < 20:
            return gg[k][0:(128 if k < 21 else 42), nb * 512:(nb + 1) * 512]
        if k == 20:
            return gg20c[nb]
        return gg21c[nb]

    HLq = (u3b[0:8, 0:T], u3b[0:8, T:2 * T])
    HLk = (x0b[0:8, 0:T], x1b[0:8, 0:T])
    wqb = [wm[6], wm[7]]
    wodst = [wm[0], wm[5], wm[1], wm[3]]
    w2slabs = [(wm[0], wm[5], wm[3]), (wm[6], wm[7], wm[1])]
    rbrow3 = wm[4][0:1, 0:512]
    ccrow = wm[2][:, 0:D].bitcast(F32)[0:1, 0:512]
    ctxs = wm[2][:, 0:D].bitcast(F32)[0:64, 0:512]
    mu_bf = u2b[0:1, 0:T]
    rstd_bf = u2b[0:1, T:2 * T]
    gbc, bbc, b2c = B["dbc"], B["bmc"], B["fbc"]
    qsc = amx
    scb = dram["sc_amax"]

    def asl(kt):
        return xb[kt // 4][:, (kt % 4) * 512:((kt % 4) + 1) * 512]

    tok_v = B["tok_ln2_v"]
    tok_g = B["tok_ln2_g"]

    p.wait("sync", s_ve, tok_v)
    p.wait("sync", s_ge, tok_g)
    p.wait("vector", s_tr, p.sems["s_tr"].n)
    p.wait("gpsimd", s_tr, p.sems["s_tr"].n)
    p.wait("gpsimd", s_ve, tok_v)
    p.wait("scalar", s_ge, tok_g)
    p.wait("tensor", s_ve, tok_v)
    p.wait("tensor", s_ge, tok_g)

    xsp3 = dram["sc_xspill"]
    for j in range(NDT):
        p.dma("sync", bass.AP(xsp3.tensor, j * 128 * T, [[T, 128], [1, T]]),
              xr[j][:, 0:T], s_x3)
    x3_spilled = s_x3.n

    carry = {"w2mm": 0, "w2ve": 0}

    for L in range(2):
        if L == 1:
            p.wait("sync", s_tr, carry["w2mm"])
            p.wait("sync", s_ve, carry["w2ve"])
            p.wait("vector", s_tr, carry["w2mm"])
            p.wait("gpsimd", s_tr, carry["w2mm"])
            p.wait("gpsimd", s_ve, carry["w2ve"])
            p.wait("scalar", s_ve, carry["w2ve"])
            p.wait("tensor", s_ve, carry["w2ve"])
        p.dma("sync", qsc[:, 0:1], col_ap(dram["l1f_qs"], L * 128), s_l3)
        p.dma("sync", qsc[:, 2:3], col_ap(dram["l1f_ks"], L * 128), s_l3)
        for j in range(NDT):
            p.dma("sync", gbc[:, j:j + 1], col_ap(dram["l1f_lng"], L * D + 128 * j), s_l3)
            p.dma("sync", bbc[:, j:j + 1], col_ap(dram["l1f_lnb"], L * D + 128 * j), s_l3)
            p.dma("sync", b2c[:, j:j + 1], col_ap(dram["l1f_b2"], L * D + 128 * j), s_l3)
        for half in range(2):
            for c in range(22):
                n = 128 if c < 21 else 42
                p.dma("sync", b1c[0:n, 22 * half + c:22 * half + c + 1],
                      col_ap(dram["l1f_b1"], L * 5460 + half * 2730 + 128 * c, n=n), s_l3)
        p.dma("sync", T1q, dram["c_cosq"][0:128, :], s_tg)
        p.dma("sync", T2q, dram["c_sinqS"][0:128, :], s_tg)
        p.dma("sync", T1k, dram["c_cosk"][0:128, :], s_tg)
        p.dma("sync", T2k, dram["c_sinkS"][0:128, :], s_tg)
        ld3 = s_l3.n
        tg3 = s_tg.n

        p.wait("vector", s_l3, ld3)
        p.wait("vector", s_tg, tg3)
        p.drain("vector")
        p.op("vector", lambda q: q.tensor_scalar(out=T1q, in0=T1q, scalar1=qsc[:, 0:1],
                                                 scalar2=8.0, op0=OP.mult, op1=OP.mult))
        p.op("vector", lambda q: q.tensor_scalar(out=T2q, in0=T2q, scalar1=qsc[:, 0:1],
                                                 scalar2=8.0, op0=OP.mult, op1=OP.mult))
        p.op("vector", lambda q: q.tensor_scalar(out=T1k, in0=T1k, scalar1=qsc[:, 2:3],
                                                 scalar2=None, op0=OP.mult))
        tab_tok = p.op("vector", lambda q: q.tensor_scalar(out=T2k, in0=T2k, scalar1=qsc[:, 2:3],
                                                           scalar2=None, op0=OP.mult), s_ve)

        p.drain("vector")
        if L == 1:
            for j in range(4):
                p.op("vector", lambda q, jj=j: q.tensor_copy(out=hsh[jj], in_=xr[jj][:, 0:T]))
        for j in range(4):
            p.op("vector", lambda q, jj=j: q.memset(hsh[4 + jj][:, 0:1], 0.0))
            p.op("vector", lambda q, jj=j: q.tensor_copy(out=hsh[4 + jj][:, 1:T],
                                                         in_=xr[4 + jj][:, 0:T - 1]),
                 s_ve if j == 3 else None)
        hsh_tok = p.sems["s_ve"].n

        wqkv_d = dram["l1f_wqkv"]

        def load_wq3(dd):
            srcap = bass.AP(wqkv_d.tensor, L * D * 1536 + 128 * dd,
                            [[1536, 128], [1536 * 128, 8], [1, 128]])
            p.dma("gpsimd", wqb[dd % 2][:, 0:1024], srcap, s_w3[dd % 2])

        bq0, bq1 = s_w3[0].n, s_w3[1].n
        load_wq3(0)
        load_wq3(1)
        p.wait("tensor", s_ve, hsh_tok)
        evq = 0
        qkv_mm_last = 0
        for dd in range(12):
            p.wait("tensor", s_w3[dd % 2], (bq0 if dd % 2 == 0 else bq1) + 16 * (dd // 2 + 1))
            if evq:
                p.wait("tensor", s_se, evq)
            for nb in range(NB):
                for dt in range(NDT):
                    p.op("tensor", lambda q, o=pm[:, nb * 512:(nb + 1) * 512],
                         l=wqb[dd % 2][:, dt * 128:(dt + 1) * 128],
                         r=hsh[dt][:, nb * 512:(nb + 1) * 512],
                         st=(dt == 0), sp=(dt == NDT - 1):
                         q.matmul(o, l, r, start=st, stop=sp),
                         s_tr if (dt == NDT - 1 and nb == NB - 1) else None)
            mmtok = p.sems["s_tr"].n
            qkv_mm_last = mmtok
            p.wait("scalar", s_tr, mmtok)
            for nb in range(NB):
                p.op("scalar", lambda q, o=QKV[dd][:, nb * 512:(nb + 1) * 512],
                     i_=pm[:, nb * 512:(nb + 1) * 512]:
                     q.activation(out=o, in_=i_, func=AF.Copy),
                     s_se if nb == NB - 1 else None)
            evq = p.sems["s_se"].n
            if dd + 2 < 12:
                p.wait("gpsimd", s_tr, mmtok)
                load_wq3(dd + 2)
        qkv_tok = evq

        p.wait("gpsimd", s_tr, qkv_mm_last)
        for dt in range(4):
            p.dma("gpsimd", wodst[dt][:, 0:1024],
                  bass.AP(dram["l1f_wo"].tensor, L * 512 * 1024 + dt * 128 * 1024,
                          [[1024, 128], [1, 1024]]), s_wo3)
        wo_ld = s_wo3.n
        w1_d = dram["l1f_w1"]
        b30, b31 = s_w3[0].n, s_w3[1].n

        def load_w1(k, half):
            M = 128 if k < 21 else 42
            off = (2730 if half == 0 else 0) + k * 128
            srcw = bass.AP(w1_d.tensor, L * D * 5460 + off,
                           [[5460, 128], [5460 * 128, 8], [1, M]])
            p.dma("gpsimd", wqb[half][:, 0:8 * M], srcw, s_w3[half])

        load_w1(0, 0)
        load_w1(0, 1)

        p.wait("vector", s_se, qkv_tok)
        p.wait("tensor", s_se, qkv_tok)
        for (b0, ps) in ((0, pm), (4, pb)):
            for nb in range(NB):
                for dt in range(4):
                    p.op("vector", lambda q, d=dt, b=b0, n=nb: q.tensor_tensor(
                        out=bfull[:, 0:512], in0=QKV[b + d][:, n * 512:(n + 1) * 512],
                        in1=QKV[b + d][:, n * 512:(n + 1) * 512], op=OP.mult), s_ve)
                    sqtok = p.sems["s_ve"].n
                    p.wait("tensor", s_ve, sqtok)
                    p.op("tensor", lambda q, o=ps[0:8, nb * 512:(nb + 1) * 512], d=dt:
                         q.matmul(o, selfq[:, 8 * d:8 * (d + 1)], bfull[:, 0:512],
                                  start=(d == 0), stop=(d == 3)), s_tr)
                    p.wait("vector", s_tr, p.sems["s_tr"].n)
        nrm_tok = p.sems["s_tr"].n
        p.wait("scalar", s_tr, nrm_tok)
        p.op("scalar", lambda q: q.activation(out=pm[32:40, 0:T], in_=pm[0:8, 0:T], func=AF.Sqrt), None)
        p.op("scalar", lambda q: q.activation(out=pb[32:40, 0:T], in_=pb[0:8, 0:T], func=AF.Sqrt), s_se)
        sq2 = p.sems["s_se"].n
        p.wait("vector", s_se, sq2)
        p.drain("vector")
        p.op("vector", lambda q: q.memset(u3b[0:128, 0:2 * T], 0.0))
        p.op("vector", lambda q: q.memset(x0b[0:128, 0:T], 0.0))
        p.op("vector", lambda q: q.memset(x1b[0:128, 0:T], 0.0))
        p.drain("vector")
        for (ps, HL) in ((pm, HLq), (pb, HLk)):
            p.drain("vector")
            p.op("vector", lambda q, ps2=ps: q.tensor_scalar(out=ps2[64:72, 0:T], in0=ps2[32:40, 0:T],
                                                             scalar1=1e-12, scalar2=None, op0=OP.max))
            p.drain("vector")
            p.op("vector", lambda q, ps2=ps: q.reciprocal(out=ps2[64:72, 0:T], in_=ps2[64:72, 0:T]))
            p.drain("vector")
            p.op("vector", lambda q, ps2=ps, hl=HL: q.tensor_copy(out=hl[0], in_=ps2[64:72, 0:T]))
            p.drain("vector")
            p.op("vector", lambda q, ps2=ps, hl=HL: q.tensor_tensor(out=hl[1], in0=ps2[64:72, 0:T],
                                                                    in1=hl[0], op=OP.subtract), s_ve)
        hl_tok = p.sems["s_ve"].n
        hlz_tok = hl_tok
        HLFq = (u3b[0:128, 0:T], u3b[0:128, T:2 * T])
        HLFk = (x0b[0:128, 0:T], x1b[0:128, 0:T])

        rotscr = xb[0]
        prodscr = xb[1]
        prev_qp = 0
        prev_s = 0
        p.wait("gpsimd", s_ve, tab_tok)
        p.wait("gpsimd", s_se, qkv_tok)
        for side in range(2):
            b0 = 4 * side
            TT1 = T1q if side == 0 else T1k
            TT2 = T2q if side == 0 else T2k
            HL = HLFq if side == 0 else HLFk
            for dt in range(4):
                src = QKV[b0 + dt]
                p.wait("tensor", s_ve, hl_tok)
                p.wait("tensor", s_ve, hlz_tok)
                if prev_qp:
                    p.wait("tensor", s_ve, prev_qp)
                for nb in range(NB):
                    p.op("tensor", lambda q, o=pm[:, nb * 512:(nb + 1) * 512], d=dt, hl=HL, n=nb:
                         q.matmul(o, sel8h[0:128, 128 * d:128 * (d + 1)],
                                  hl[0][0:128, n * 512:(n + 1) * 512], start=True, stop=False), None)
                    p.op("tensor", lambda q, o=pm[:, nb * 512:(nb + 1) * 512], d=dt, hl=HL, n=nb:
                         q.matmul(o, sel8h[0:128, 128 * d:128 * (d + 1)],
                                  hl[1][0:128, n * 512:(n + 1) * 512], start=False, stop=True),
                         s_tr if nb == NB - 1 else None)
                rsb = p.sems["s_tr"].n
                if prev_qp:
                    p.wait("gpsimd", s_ve, prev_qp)
                p.drain("gpsimd")
                p.op("gpsimd", lambda q, s2=src, t1=TT1: q.tensor_tensor(
                    out=prodscr[:, 0:T], in0=s2, in1=t1, op=OP.mult), s_ge)
                m1tok = p.sems["s_ge"].n
                p.wait("vector", s_ve, hl_tok)
                if prev_s:
                    p.wait("vector", s_ge, prev_s)
                for blk in range(4):
                    r0 = 32 * blk
                    rin = r0 + 32 if blk % 2 == 0 else r0 - 32
                    p.op("vector", lambda q, a=r0, b=rin, s2=src, t2=TT2: q.tensor_tensor(
                        out=rotscr[a:a + 32, 0:T], in0=s2[b:b + 32, 0:T],
                        in1=t2[b:b + 32, 0:T], op=OP.mult),
                        s_ve if blk == 3 else None)
                m2tok = p.sems["s_ve"].n
                p.wait("gpsimd", s_ve, m2tok)
                p.drain("gpsimd")
                prev_s = p.op("gpsimd", lambda q: q.tensor_tensor(
                    out=prodscr[:, 0:T], in0=prodscr[:, 0:T], in1=rotscr[:, 0:T], op=OP.add), s_ge)
                p.wait("vector", s_ge, prev_s)
                p.wait("vector", s_tr, rsb)
                p.drain("vector")
                prev_qp = p.op("vector", lambda q, s2=src: q.tensor_tensor(
                    out=s2, in0=prodscr[:, 0:T], in1=pm[0:128, 0:T], op=OP.mult), s_ve)
        rot_tok = prev_qp
        rot_s_tok = prev_s

        p.wait("vector", s_ge, rot_s_tok)
        p.wait("tensor", s_ve, hl_tok)
        evtok = [0, 0, 0, 0]
        for vd in range(4):
            for tt in range(NTT):
                bank = tt % 4
                if evtok[bank]:
                    p.wait("tensor", s_ve, evtok[bank])
                p.op("tensor", lambda q, o=pb[:, bank * 512:(bank + 1) * 512].bitcast(BF16)[:, 0:128],
                     i_=QKV[8 + vd][:, tt * 128:(tt + 1) * 128], idb=identb:
                     q.matmul(o, i_, idb[:], is_transpose=True, start=True, stop=True), s_tr)
                ttok = p.sems["s_tr"].n
                p.wait("vector", s_tr, ttok)
                p.drain("vector")
                for hh in range(2):
                    h = 2 * vd + hh
                    p.op("vector", lambda q, t=tt, b=bank, h2=h, hh2=hh: q.tensor_copy(
                        out=VEL[t][:, 65 * h2:65 * h2 + 64],
                        in_=pb[:, b * 512:(b + 1) * 512].bitcast(BF16)[:, 64 * hh2:64 * (hh2 + 1)]),
                        s_ve if hh == 1 else None)
                evtok[bank] = p.sems["s_ve"].n
        p.drain("vector")
        for tt in range(NTT):
            for h in range(LH3):
                p.op("vector", lambda q, t=tt, h2=h: q.tensor_copy(
                    out=VEL[t][:, 65 * h2 + 64:65 * h2 + 65], in_=onesbc[:]),
                    s_ve if (tt == NTT - 1 and h == LH3 - 1) else None)
        vel_tok = p.sems["s_ve"].n

        def ke_ap(bb, r0, r1, c0, c1):
            if bb == 0:
                return wj[0][r0:r1, 2048 + c0:2048 + c1]
            return vex[2][r0:r1, c0:c1]

        def qe_ap(bb, r0, r1, c0, c1):
            return wj[1][r0:r1, 2048 + 512 * bb + c0:2048 + 512 * bb + c1]

        smb = [s_mb, s_mb2]
        smg = [s_mg, s_mg2]
        mcs = [amx[:, 8:12], amx[:, 12:16]]
        scoff = [0, 1024]
        p.drain("vector")
        p.op("vector", lambda q: q.memset(ke_ap(0, 64, 65, 0, 1024), 1.0))
        p.drain("vector")
        p.op("vector", lambda q: q.memset(ke_ap(1, 64, 65, 0, 1024), 1.0))
        p.drain("vector")
        p.op("vector", lambda q: q.memset(qe_ap(0, 64, 128, 0, 512), 0.0))
        p.drain("vector")
        ke1_tok = p.op("vector", lambda q: q.memset(qe_ap(1, 64, 128, 0, 512), 0.0), s_ve)
        p.wait("tensor", s_ve, vel_tok)
        p.wait("tensor", s_ve, ke1_tok)
        p.wait("scalar", s_ve, rot_tok)
        last_p2b = [0, 0]
        st = {"av": 0, "sel": 0, "ctx": 0, "pend": None}
        prev_exp = [0, 0]

        def attn_tail():
            pd = st["pend"]
            if pd is None:
                return
            h2, w2, fk = pd
            p.wait("tensor", s_ge, st["sel"])
            for kt in range(fk, 8):
                ktt = 4 * w2 + kt - 4
                p.op("tensor", lambda q, t=ktt, hh=h2, a=kt:
                     q.matmul(pb[0:65, 0:512], VEL[t][:, 65 * hh:65 * hh + 65], asl(a),
                              start=False, stop=(a == 7)), s_tr)
            st["av"] = p.sems["s_tr"].n
            p.wait("vector", s_tr, st["av"])
            p.op("vector", lambda q: q.reciprocal(out=ccrow, in_=pb[64:65, 0:512]))
            p.drain("vector")
            rb = p.op("vector", lambda q: q.tensor_copy(out=rbrow3, in_=ccrow), s_ve)
            p.wait("tensor", s_ve, rb)
            p.op("tensor", lambda q: q.matmul(pb[0:64, 1536:2048], onesbr[0:1, 0:64],
                                              rbrow3, start=True, stop=True), s_tr)
            rbc = p.sems["s_tr"].n
            p.wait("scalar", s_ve, rb)
            p.op("scalar", lambda q: q.activation(out=ctxs, in_=pb[0:64, 0:512],
                                                  func=AF.Copy), s_se)
            ctok = p.sems["s_se"].n
            p.wait("vector", s_tr, rbc)
            p.wait("vector", s_se, ctok)
            p.drain("vector")
            st["ctx"] = p.op("vector", lambda q, hh=h2, a=512 * w2: q.tensor_tensor(
                out=xb[4 + hh // 2][64 * (hh % 2):64 * (hh % 2) + 64, a:a + 512],
                in0=ctxs, in1=pb[0:64, 1536:2048], op=OP.mult), s_ve)
            st["pend"] = None

        iters = [(h, w) for h in range(LH3) for w in range(NW3)]
        cpq_tok = [0] * (len(iters) + 1)

        def stage_i(i2):
            if i2 >= len(iters):
                return
            h2, w2 = iters[i2]
            b2 = i2 % 2
            r0 = 64 * (h2 % 2)
            kt2 = QKV[4 + h2 // 2]
            qt2 = QKV[h2 // 2]
            if last_p2b[b2]:
                p.wait("vector", s_tr, last_p2b[b2])
            p.drain("vector")
            if w2 == 0:
                p.op("vector", lambda q, o=ke_ap(b2, 0, 64, 512, 1024), kt=kt2, rr=r0:
                     q.tensor_copy(out=o, in_=kt[rr:rr + 64, 0:512]))
            else:
                p.op("vector", lambda q, o=ke_ap(b2, 0, 64, 0, 1024), kt=kt2, rr=r0,
                     a=(w2 - 1) * 512:
                     q.tensor_copy(out=o, in_=kt[rr:rr + 64, a:a + 1024]))
            p.op("vector", lambda q, o=qe_ap(b2, 0, 64, 0, 512), qt=qt2, rr=r0, a=w2 * 512:
                 q.tensor_copy(out=o, in_=qt[rr:rr + 64, a:a + 512]))
            cpq_tok[i2] = p.op("vector", lambda q, o=qe_ap(b2, 64, 65, 0, 512):
                               q.memset(o, 0.0), s_ve)

        stage_i(0)
        for idx, (h, w) in enumerate(iters):
            b = idx % 2
            hb0 = 64 * (h % 2)
            qtile = QKV[h // 2]
            ktile = QKV[4 + h // 2]
            q0 = w * 512
            p.wait("vector", smb[b], smb[b].n)
            redtok = {}
            p.wait("tensor", s_ve, cpq_tok[idx])
            for qt in range(4):
                base = (qt % 2) * 1024
                if qt >= 2:
                    p.wait("tensor", s_ve, redtok[qt - 2])
                lq = qe_ap(b, 0, 128, qt * 128, (qt + 1) * 128)
                if w >= 1:
                    segs = [(0, 128, cnL), (128, 384, None), (512, 128, cnR)]
                    rlo = 0
                else:
                    segs = ([(512 - qt * 128, qt * 128, None)] if qt > 0 else []) + \
                           [(512, 128, cnR)]
                    rlo = 512 - qt * 128
                for si, (r0, nlen, cn) in enumerate(segs):
                    last = (si == len(segs) - 1) and cn is None
                    p.op("tensor", lambda q, o=pm[:, base + r0:base + r0 + nlen], l=lq,
                         r=ke_ap(b, 0, 128, qt * 128 + r0, qt * 128 + r0 + nlen),
                         sp=(cn is None):
                         q.matmul(o, l, r, start=True, stop=sp),
                         s_tr if last else None)
                    if cn is not None:
                        p.op("tensor", lambda q, o=pm[:, base + r0:base + r0 + nlen],
                             c2=cn, idb=identb:
                             q.matmul(o, idb[:], c2[:], start=False, stop=True),
                             s_tr if si == len(segs) - 1 else None)
                p1tok = p.sems["s_tr"].n
                p.wait("vector", s_tr, p1tok)
                p.op("vector", lambda q, bs=base, r=rlo, qt2=qt, bb=b: q.tensor_reduce(
                    out=mcs[bb][:, qt2:qt2 + 1].rearrange("p (a o) -> p a o", a=1),
                    in_=pm[:, bs + r:bs + 640].rearrange("p (a j) -> p a j", a=1),
                    axis=AX.X, op=OP.max), s_ve)
                redtok[qt] = p.sems["s_ve"].n
            p.drain("vector")
            neg_tok = p.op("vector", lambda q, bb=b: q.tensor_scalar(
                out=mcs[bb], in0=mcs[bb], scalar1=-1.0, scalar2=None, op0=OP.mult), s_ve)
            stage_i(idx + 1)
            p.wait("sync", s_ve, neg_tok)
            p.wait("sync", smg[b], smg[b].n)
            for qt in range(4):
                p.dma("sync", bass.AP(scb.tensor, scoff[b] + qt * 128, [[1, 128], [1, 1]]),
                      mcs[b][:, qt:qt + 1], smb[b])
            p.wait("gpsimd", smb[b], smb[b].n)
            if last_p2b[b]:
                p.wait("gpsimd", s_tr, last_p2b[b])
            p.dma("gpsimd", qe_ap(b, 64, 65, 0, 512),
                  bass.AP(scb.tensor, scoff[b], [[512, 1], [1, 512]]), smg[b])
            attn_tail()
            p.wait("tensor", smg[b], smg[b].n)
            if st["ctx"]:
                p.wait("tensor", s_ve, st["ctx"])
            p.op("tensor", lambda q: q.matmul(pb[0:65, 0:512], zbuf[0:1, 0:65],
                                              rbrow3, start=True, stop=False), None)
            first_kt = 4 if w == 0 else 0
            for kt in range(8):
                if kt < first_kt:
                    continue
                pp = kt % 2
                sc_ps = pb[:, 512 + pp * 512:1024 + pp * 512]
                if prev_exp[pp]:
                    p.wait("tensor", s_se, prev_exp[pp])
                p.op("tensor", lambda q, o=sc_ps, l=ke_ap(b, 0, 128, kt * 128, (kt + 1) * 128),
                     r=qe_ap(b, 0, 128, 0, 512):
                     q.matmul(o, l, r, start=True, stop=True), s_tr)
                sctok = p.sems["s_tr"].n
                if kt == 7:
                    last_p2b[b] = sctok
                p.wait("scalar", s_tr, sctok)
                if kt == first_kt:
                    if st["av"]:
                        p.wait("scalar", s_tr, st["av"])
                    if st["sel"]:
                        p.wait("scalar", s_ge, st["sel"])
                p.op("scalar", lambda q, o=asl(kt), i_=sc_ps:
                     q.activation(out=o, in_=i_, func=AF.Exp), s_se)
                prev_exp[pp] = p.sems["s_se"].n
            p.wait("gpsimd", s_se, p.sems["s_se"].n)
            p.drain("gpsimd")
            if w >= 1:
                p.op("gpsimd", lambda q: q.affine_select(
                    out=xb[0][:, 0:T], in_=xb[0][:, 0:T], pattern=[[128, 4], [-1, 512]],
                    base=0, channel_multiplier=1, compare_op=OP.is_ge, fill=0.0), None)
                p.drain("gpsimd")
            p.op("gpsimd", lambda q: q.affine_select(
                out=xb[1][:, 0:T], in_=xb[1][:, 0:T], pattern=[[-128, 4], [1, 512]],
                base=0, channel_multiplier=-1, compare_op=OP.is_ge, fill=0.0), s_ge)
            st["sel"] = p.sems["s_ge"].n
            st["pend"] = (h, w, first_kt)
        attn_tail()
        ctx_ev = st["ctx"]
        last_av = st["av"]

        p.wait("tensor", s_wo3, wo_ld)
        p.wait("tensor", s_ve, ctx_ev)
        if L == 0:
            p.wait("vector", s_x3, x3_spilled)
        r3tok = [0] * NDT
        womm_last = 0
        for do in range(NDT):
            ps = pm if do % 2 == 0 else pb
            if do >= 2:
                p.wait("tensor", s_ve, r3tok[do - 2])
            for nb in range(NB):
                for dt in range(4):
                    p.op("tensor", lambda q, o=ps[:, nb * 512:(nb + 1) * 512], d=dt, dd=do, n=nb:
                         q.matmul(o, wodst[d][:, dd * 128:(dd + 1) * 128],
                                  xb[4 + d][:, n * 512:(n + 1) * 512],
                                  start=(d == 0), stop=(d == 3)),
                         s_tr if (dt == 3 and nb == NB - 1) else None)
            womm_last = p.sems["s_tr"].n
            p.wait("vector", s_tr, womm_last)
            for nb in range(NB):
                p.op("vector", lambda q, o=xr[do][:, nb * 512:(nb + 1) * 512],
                     i0=ps[:, nb * 512:(nb + 1) * 512], i1=xr[do][:, nb * 512:(nb + 1) * 512]:
                     q.tensor_tensor(out=o, in0=i0, in1=i1, op=OP.add),
                     s_ve if nb == NB - 1 else None)
            r3tok[do] = p.sems["s_ve"].n

        p.wait("gpsimd", s_ve, r3tok[-1])
        p.wait("gpsimd", s_tr, last_av)
        p.drain("gpsimd")
        for j in range(4):
            p.op("gpsimd", lambda q, jj=j: q.tensor_copy(out=hsh[jj], in_=xr[jj][:, 0:T]),
                 s_ge if j == 3 else None)
            p.drain("gpsimd")
        h2g = p.sems["s_ge"].n
        p.wait("vector", s_ve, r3tok[-1])
        p.drain("vector")
        for j in range(4):
            p.op("vector", lambda q, jj=j: q.memset(hsh[4 + jj][:, 0:1], 0.0))
            p.op("vector", lambda q, jj=j: q.tensor_copy(out=hsh[4 + jj][:, 1:T],
                                                         in_=xr[4 + jj][:, 0:T - 1]),
                 s_ve if j == 3 else None)
        h2v = p.sems["s_ve"].n
        p.wait("tensor", s_ge, h2g)
        p.wait("tensor", s_ve, h2v)
        p.wait("vector", s_ge, h2g)
        for nb in range(NB):
            for j in range(8):
                p.op("tensor", lambda q, o=pm[0:1, nb * 512:(nb + 1) * 512], jj=j, n=nb:
                     q.matmul(o, onesbc[:], hsh[jj][:, n * 512:(n + 1) * 512],
                              start=(jj == 0), stop=(jj == 7)),
                     s_tr if j == 7 else None)
        for j in range(8):
            for nb in range(NB):
                p.op("vector", lambda q, jj=j, n=nb: q.tensor_tensor(
                    out=bfull[:, 0:512], in0=hsh[jj][:, n * 512:(n + 1) * 512],
                    in1=hsh[jj][:, n * 512:(n + 1) * 512], op=OP.mult), s_ve)
                sqtok = p.sems["s_ve"].n
                p.wait("tensor", s_ve, sqtok)
                p.op("tensor", lambda q, o=pb[0:1, nb * 512:(nb + 1) * 512], jj=j:
                     q.matmul(o, onesbc[:], bfull[:, 0:512],
                              start=(jj == 0), stop=(jj == 7)), s_tr)
                p.wait("vector", s_tr, p.sems["s_tr"].n)
        st_tok = p.sems["s_tr"].n
        p.wait("vector", s_tr, st_tok)
        p.drain("vector")
        p.op("vector", lambda q: q.tensor_scalar(out=U1[0:1, 0:T], in0=pm[0:1, 0:T],
                                                 scalar1=1.0 / D, scalar2=None, op0=OP.mult))
        p.drain("vector")
        p.op("vector", lambda q: q.tensor_scalar(out=pb[0:1, 0:T], in0=pb[0:1, 0:T],
                                                 scalar1=1.0 / D, scalar2=None, op0=OP.mult))
        p.drain("vector")
        p.op("vector", lambda q: q.tensor_tensor(out=U2[0:1, 0:T], in0=U1[0:1, 0:T],
                                                 in1=U1[0:1, 0:T], op=OP.mult))
        p.drain("vector")
        var_tok = p.op("vector", lambda q: q.tensor_tensor(out=pb[0:1, 0:T], in0=pb[0:1, 0:T],
                                                           in1=U2[0:1, 0:T], op=OP.subtract), s_ve)
        p.wait("scalar", s_ve, var_tok)
        srt = p.op("scalar", lambda q, b=cm[0:1, 1:2]: q.activation(out=pb[0:1, 0:T], in_=pb[0:1, 0:T],
                                                                    func=AF.Sqrt, bias=b), s_se)
        p.wait("vector", s_se, srt)
        p.drain("vector")
        p.op("vector", lambda q: q.reciprocal(out=pb[0:1, 0:T], in_=pb[0:1, 0:T]))
        p.drain("vector")
        p.op("vector", lambda q: q.tensor_copy(out=mu_bf, in_=U1[0:1, 0:T]))
        p.drain("vector")
        rows_tok = p.op("vector", lambda q: q.tensor_copy(out=rstd_bf, in_=pb[0:1, 0:T]), s_ve)
        p.wait("tensor", s_ve, rows_tok)
        for nb in range(NB):
            p.op("tensor", lambda q, o=pb[:, nb * 512:(nb + 1) * 512],
                 r=mu_bf[0:1, nb * 512:(nb + 1) * 512]:
                 q.matmul(o, onesbr[:], r, start=True, stop=True))
            p.op("tensor", lambda q, o=pm[:, nb * 512:(nb + 1) * 512],
                 r=rstd_bf[0:1, nb * 512:(nb + 1) * 512]:
                 q.matmul(o, onesbr[:], r, start=True, stop=True),
                 s_tr if nb == NB - 1 else None)
        bc_tok = p.sems["s_tr"].n
        p.wait("vector", s_tr, bc_tok)
        p.drain("vector")
        for j in range(8):
            p.op("vector", lambda q, jj=j: q.tensor_tensor(out=U1[:], in0=hsh[jj],
                                                           in1=pb[:, 0:T], op=OP.subtract))
            p.drain("vector")
            p.op("vector", lambda q: q.tensor_tensor(out=U1[:], in0=U1[:],
                                                     in1=pm[:, 0:T], op=OP.mult))
            p.drain("vector")
            p.op("vector", lambda q, jj=j: q.tensor_scalar(
                out=hsh[jj], in0=U1[:], scalar1=gbc[:, jj:jj + 1], scalar2=bbc[:, jj:jj + 1],
                op0=OP.mult, op1=OP.add), s_ve)
            p.drain("vector")
        ln_tok = p.sems["s_ve"].n

        w2_d = dram["l1f_w2"]
        w2cnt = {}

        def load_w2(do):
            sl = w2slabs[do % 2]
            base = L * 2730 * 1024 + do * 128
            p.dma("gpsimd", sl[0][:, 0:1024],
                  bass.AP(w2_d.tensor, base, [[1024, 128], [128 * 1024, 8], [1, 128]]), s_w2[do % 2])
            p.dma("gpsimd", sl[1][:, 0:1024],
                  bass.AP(w2_d.tensor, base + 8 * 128 * 1024,
                          [[1024, 128], [128 * 1024, 8], [1, 128]]), s_w2[do % 2])
            p.dma("gpsimd", sl[2][:, 0:640],
                  bass.AP(w2_d.tensor, base + 16 * 128 * 1024,
                          [[1024, 128], [128 * 1024, 5], [1, 128]]), s_w2[do % 2])
            p.dma("gpsimd", sl[2][0:42, 640:768],
                  bass.AP(w2_d.tensor, base + 21 * 128 * 1024, [[1024, 42], [1, 128]]), s_w2[do % 2])
            w2cnt[do] = s_w2[do % 2].n

        p.wait("gpsimd", s_tr, womm_last)
        load_w2(0)
        prevst = 0
        prevge = 0
        amm = 0
        for k in range(NHT3):
            M = 128 if k < 21 else 42
            p.wait("tensor", s_w3[0], b30 + 16 * (k + 1))
            if prevge:
                p.wait("tensor", s_se, prevge)
            if k == 0:
                p.wait("tensor", s_ve, ln_tok)
            for nb in range(NB):
                for dt in range(8):
                    p.op("tensor", lambda q, o=pm[0:M, nb * 512:(nb + 1) * 512],
                         l=wqb[0][:, dt * M:(dt + 1) * M], r=hsh[dt][:, nb * 512:(nb + 1) * 512],
                         st=(dt == 0), sp=(dt == 7):
                         q.matmul(o, l, r, start=st, stop=sp),
                         s_tr if (dt == 7 and nb == NB - 1) else None)
            gmm = p.sems["s_tr"].n
            p.wait("scalar", s_tr, gmm)
            for nb in range(NB):
                p.op("scalar", lambda q, o=ggsl(k, nb),
                     i_=pm[0:M, nb * 512:(nb + 1) * 512], b=b1c[0:M, 22 + k:23 + k]:
                     q.activation(out=o, in_=i_, func=AF.Gelu, bias=b),
                     s_se if nb == NB - 1 else None)
            prevge = p.sems["s_se"].n
            if k + 1 < NHT3:
                p.wait("gpsimd", s_tr, gmm)
                load_w1(k + 1, 0)
            p.wait("tensor", s_w3[1], b31 + 16 * (k + 1))
            if prevst:
                p.wait("tensor", s_ve, prevst)
            for nb in range(NB):
                for dt in range(8):
                    p.op("tensor", lambda q, o=pb[0:M, nb * 512:(nb + 1) * 512],
                         l=wqb[1][:, dt * M:(dt + 1) * M], r=hsh[dt][:, nb * 512:(nb + 1) * 512],
                         st=(dt == 0), sp=(dt == 7):
                         q.matmul(o, l, r, start=st, stop=sp),
                         s_tr if (dt == 7 and nb == NB - 1) else None)
            amm = p.sems["s_tr"].n
            p.wait("vector", s_tr, amm)
            p.wait("vector", s_se, prevge)
            if k < 20:
                p.op("vector", lambda q, kk=k, M2=M: q.scalar_tensor_tensor(
                    out=gg[kk][0:M2, 0:T], in0=pb[0:M2, 0:T], scalar=b1c[0:M2, kk:kk + 1],
                    in1=gg[kk][0:M2, 0:T], op0=OP.add, op1=OP.mult), s_ve)
            else:
                for nb in range(NB):
                    p.op("vector", lambda q, o=ggsl(k, nb), n=nb, M2=M, kk=k:
                         q.scalar_tensor_tensor(
                             out=o, in0=pb[0:M2, n * 512:(n + 1) * 512],
                             scalar=b1c[0:M2, kk:kk + 1], in1=o, op0=OP.add, op1=OP.mult),
                         s_ve if nb == NB - 1 else None)
            prevst = p.sems["s_ve"].n
            if k + 1 < NHT3:
                p.wait("gpsimd", s_tr, amm)
                load_w1(k + 1, 1)
        gg_done = prevst

        p.wait("gpsimd", s_tr, amm)
        load_w2(1)
        w2ve = [0] * NDT
        w2mm = [0] * NDT
        for do in range(NDT):
            p.wait("tensor", s_w2[do % 2], w2cnt[do])
            ps = pm if do % 2 == 0 else pb
            if do == 0:
                p.wait("tensor", s_se, prevge)
                p.wait("tensor", s_ve, gg_done)
            if do >= 2:
                p.wait("tensor", s_ve, w2ve[do - 2])
            sl = w2slabs[do % 2]
            for nb in range(NB):
                for k in range(NHT3):
                    if k < 16:
                        lhsT = sl[k // 8][:, (k % 8) * 128:(k % 8 + 1) * 128]
                    elif k < 21:
                        lhsT = sl[2][:, (k - 16) * 128:(k - 15) * 128]
                    else:
                        lhsT = sl[2][0:42, 640:768]
                    p.op("tensor", lambda q, o=ps[:, nb * 512:(nb + 1) * 512], l=lhsT,
                         r=ggsl(k, nb), st=(k == 0), sp=(k == NHT3 - 1):
                         q.matmul(o, l, r, start=st, stop=sp),
                         s_tr if (k == NHT3 - 1 and nb == NB - 1) else None)
            w2mm[do] = p.sems["s_tr"].n
            p.wait("vector", s_tr, w2mm[do])
            for nb in range(NB):
                p.op("vector", lambda q, o=xr[do][:, nb * 512:(nb + 1) * 512],
                     i0=ps[:, nb * 512:(nb + 1) * 512], sc=b2c[:, do:do + 1],
                     i1=xr[do][:, nb * 512:(nb + 1) * 512]:
                     q.scalar_tensor_tensor(out=o, in0=i0, scalar=sc, in1=i1,
                                            op0=OP.add, op1=OP.add),
                     s_ve if nb == NB - 1 else None)
            w2ve[do] = p.sems["s_ve"].n
            if do + 2 < NDT:
                p.wait("gpsimd", s_tr, w2mm[do])
                load_w2(do + 2)
        carry["w2mm"] = w2mm[-1]
        carry["w2ve"] = w2ve[-1]

    xbufs = [U1[:, 0:T], U2[:, 0:T], U3[:, 0:T]]
    p.wait("sync", s_tr, carry["w2mm"])
    p.wait("sync", s_ve, carry["w2ve"])
    addtok = [0] * NDT
    for j in range(NDT):
        if j >= 3:
            p.wait("sync", s_ve, addtok[j - 3])
        p.dma("sync", xbufs[j % 3], bass.AP(xsp3.tensor, j * 128 * T, [[T, 128], [1, T]]),
              s_xr3[j % 3])
        p.wait("vector", s_xr3[j % 3], s_xr3[j % 3].n)
        p.drain("vector")
        addtok[j] = p.op("vector", lambda q, jj=j, b=xbufs[j % 3]: q.tensor_tensor(
            out=xr[jj][:, 0:T], in0=xr[jj][:, 0:T], in1=b, op=OP.add), s_ve)

    build_ln(p, B, xr, xr, xb, wait_tok=addtok[-1])
    B["tok_ln3_v"] = p.sems["s_ve"].n
    B["tok_ln3_g"] = p.sems["s_ge"].n



def build_ln(p, B, rin, yf, ybf, wait_tok, gain=None, bias=None, wait_ge_tok=0):
    U1, U2 = B["U1"], B["U2"]
    pm, pb = B["pm"], B["pb"]
    rowA, rowB = B["rowA"], B["rowB"]
    invd_c, ones_r = B["invd_c"], B["ones_r"]
    s_tr, s_ve, s_se, s_ge = p.sem("s_tr"), p.sem("s_ve"), p.sem("s_se"), p.sem("s_ge")
    n = len(rin)

    p.wait("tensor", s_ve, wait_tok)
    if wait_ge_tok:
        p.wait("tensor", s_ge, wait_ge_tok)
        p.wait("vector", s_ge, wait_ge_tok)
    for nb in range(NB):
        for j in range(n):
            p.op("tensor", lambda q, o=pb[0:1, nb * 512:(nb + 1) * 512],
                 r=rin[j][:, nb * 512:(nb + 1) * 512], st=(j == 0), sp=(j == n - 1):
                 q.matmul(o, invd_c[:], r, start=st, stop=sp),
                 s_tr if j == n - 1 else None)
    p.wait("vector", s_ve, wait_tok)
    p.drain("vector")
    for j in range(n):
        sq_tok = p.op("vector", lambda q, jj=j: q.tensor_tensor(
            out=U1[:], in0=rin[jj][:], in1=rin[jj][:], op=OP.mult), s_ve)
        p.wait("tensor", s_ve, sq_tok)
        for nb in range(NB):
            p.op("tensor", lambda q, o=pm[0:1, nb * 512:(nb + 1) * 512],
                 r=U1[:, nb * 512:(nb + 1) * 512], st=(j == 0), sp=(j == n - 1):
                 q.matmul(o, invd_c[:], r, start=st, stop=sp),
                 s_tr if nb == NB - 1 else None)
        if j < n - 1:
            p.wait("vector", s_tr, s_tr.n)
            p.drain("vector")
    m2_tok = p.sems["s_tr"].n
    p.wait("vector", s_tr, m2_tok)
    p.drain("vector")
    p.op("vector", lambda q: q.tensor_copy(out=rowA, in_=pb[0:1, :]))
    p.drain("vector")
    p.op("vector", lambda q: q.tensor_tensor(out=rowB, in0=rowA, in1=rowA, op=OP.mult))
    p.drain("vector")
    var_tok = p.op("vector", lambda q: q.tensor_tensor(out=rowB, in0=pm[0:1, :], in1=rowB,
                                                       op=OP.subtract), s_ve)
    p.wait("scalar", s_ve, var_tok)
    cm = B["cmisc"]
    sr_tok = p.op("scalar", lambda q, b=cm[0:1, 1:2]: q.activation(out=rowB, in_=rowB, func=AF.Sqrt,
                                                   bias=b), s_se)
    p.wait("vector", s_se, sr_tok)
    p.drain("vector")
    p.op("vector", lambda q: q.reciprocal(out=rowB, in_=rowB))
    p.drain("vector")
    rows_tok = p.op("vector", lambda q: q.engine_nop(), s_ve) if False else p.sems["s_ve"].n
    rows_tok = p.op("vector", lambda q: q.tensor_copy(out=rowA, in_=rowA), s_ve)
    p.wait("tensor", s_ve, rows_tok)
    for nb in range(NB):
        p.op("tensor", lambda q, o=pm[:, nb * 512:(nb + 1) * 512],
             r=rowB[0:1, nb * 512:(nb + 1) * 512]:
             q.matmul(o, ones_r[:], r, start=True, stop=True))
        p.op("tensor", lambda q, o=pb[:, nb * 512:(nb + 1) * 512],
             r=rowA[0:1, nb * 512:(nb + 1) * 512]:
             q.matmul(o, ones_r[:], r, start=True, stop=True),
             s_tr if nb == NB - 1 else None)
    bc_tok = p.sems["s_tr"].n
    p.wait("vector", s_tr, bc_tok)
    p.drain("vector")
    last_ge = 0
    for j in range(len(rin)):
        p.op("vector", lambda q, jj=j: q.tensor_tensor(out=U2[:], in0=rin[jj][:],
                                                       in1=pb[:, 0:T], op=OP.subtract))
        p.drain("vector")
        if gain is not None:
            p.op("vector", lambda q: q.tensor_tensor(out=U2[:], in0=U2[:], in1=pm[:, 0:T],
                                                     op=OP.mult))
            p.drain("vector")
            ytok = p.op("vector", lambda q, jj=j, g=gain, b=bias: q.tensor_scalar(
                out=yf[jj][:], in0=U2[:], scalar1=g[jj], scalar2=b[jj],
                op0=OP.mult, op1=OP.add), s_ve)
        else:
            ytok = p.op("vector", lambda q, jj=j: q.tensor_tensor(out=yf[jj][:], in0=U2[:],
                                                                  in1=pm[:, 0:T], op=OP.mult), s_ve)
        if ybf is not None:
            p.wait("gpsimd", s_ve, ytok)
            if last_ge:
                p.drain("gpsimd")
            last_ge = p.op("gpsimd", lambda q, jj=j: q.tensor_copy(out=ybf[jj][:], in_=yf[jj][:]), s_ge)
        p.drain("vector")
    return p.sems["s_ve"].n, last_ge


def build_output_store(p, B, src_tiles, out_y):
    pm = B["pm"]
    identf = B["identf"]
    yout = B["yout"]
    s_tr, s_ve = p.sem("s_tr"), p.sem("s_ve")
    s_st2 = [p.sem("s_st0"), p.sem("s_st1")]
    st_done = [0, 0]
    p.wait("tensor", s_ve, p.sems["s_ve"].n)
    for i in range(NTT):
        buf = i % 2
        for half in range(2):
            bank = buf * 2 + half
            for k in range(4):
                j = half * 4 + k
                p.op("tensor", lambda q, o=pm[:, bank * 512 + 128 * k: bank * 512 + 128 * (k + 1)],
                     i_=src_tiles[j][:, 128 * i:128 * (i + 1)]:
                     q.transpose(o, i_, identf[:]),
                     s_tr if k == 3 else None)
            p.wait("vector", s_tr, p.sems["s_tr"].n)
            if st_done[buf]:
                p.wait("vector", s_st2[buf], st_done[buf])
            if i >= 1:
                p.drain("vector")
            for k in range(4):
                j = half * 4 + k
                p.op("vector", lambda q, o=yout[buf][:, 128 * j:128 * (j + 1)],
                     s=pm[:, bank * 512 + 128 * k: bank * 512 + 128 * (k + 1)]:
                     q.tensor_copy(out=o, in_=s),
                     s_ve if (half == 1 and k == 3) else None)
        evtok = p.sems["s_ve"].n
        p.wait("sync", s_ve, evtok)
        p.dma("sync", out_y[128 * i:128 * (i + 1), :], yout[buf][:], s_st2[buf])
        st_done[buf] = s_st2[buf].n
        p.wait("tensor", s_ve, evtok)
    p.wait("sync", s_st2[0], s_st2[0].n)
    p.wait("sync", s_st2[1], s_st2[1].n)


class ProgP(Prog):
    def sbuf(self, es, name, shape, dtype):
        return es.enter_context(self.nc.sbuf_tensor(name, list(shape), dtype))

    def psum(self, es, name, shape, dtype=F32):
        return es.enter_context(self.nc.psum_tensor(name, list(shape), dtype))


def _mk_selq():
    import ml_dtypes
    s = np.zeros((128, 128), dtype=np.float32)
    for d in range(8):
        for h in range(16):
            if h // 2 == d:
                r0 = 64 * (h % 2)
                s[r0:r0 + 64, 16 * d + h] = 1.0
    return s.astype(ml_dtypes.bfloat16)


def _mk_b3_consts():
    import ml_dtypes
    bf = ml_dtypes.bfloat16
    t = np.arange(T, dtype=np.float64)
    half_idx = np.arange(0, 64, 2, dtype=np.float64)
    inv_freq = 1.0 / 10000.0 ** (half_idx / 64.0)
    fh = t[:, None] * inv_freq[None, :]
    freqs = np.concatenate([fh, fh], axis=1)
    base = (half_idx + 0.4 * 64) / (1.4 * 64)
    power = (t - T // 2) / 256.0
    sh = base[None, :] ** power[:, None]
    scale = np.concatenate([sh, sh], axis=1)
    cosq = (np.cos(freqs) * scale).T.astype(np.float32)
    sinq = (np.sin(freqs) * scale).T.astype(np.float32)
    cosk = (np.cos(freqs) / scale).T.astype(np.float32)
    sink = (np.sin(freqs) / scale).T.astype(np.float32)
    sw = (np.arange(64) + 32) % 64
    sgn2 = np.where(np.arange(64) < 32, 1.0, -1.0).astype(np.float32)[:, None]
    out = {
        "c_cosq": np.tile(cosq, (2, 1)).astype(bf),
        "c_sinqS": np.tile(sinq[sw, :] * sgn2, (2, 1)).astype(bf),
        "c_cosk": np.tile(cosk, (2, 1)).astype(bf),
        "c_sinkS": np.tile(sink[sw, :] * sgn2, (2, 1)).astype(bf),
    }
    s8 = np.zeros((128, 512), dtype=np.float32)
    for d in range(4):
        for r in range(128):
            s8[2 * d + (r >= 64), 128 * d + r] = 1.0
    out["c_sel8"] = s8.astype(bf)
    sq = np.zeros((128, 32), dtype=np.float32)
    for d in range(4):
        for r in range(128):
            sq[r, 8 * d + 2 * d + (r >= 64)] = 1.0
    out["c_selfq"] = sq.astype(bf)
    pp = np.arange(128)[:, None]
    uu = np.arange(128)[None, :]
    out["c_cnL"] = np.where(uu < pp, -100000.0, 0.0).astype(bf)
    out["c_cnR"] = np.where(uu > pp, -100000.0, 0.0).astype(bf)
    return out


def build_program(n_blocks=1):
    import ml_dtypes
    nc = bass.Bass("TRN2", target_bir_lowering=False)
    p = ProgP(nc)

    dram = {}
    def din(name, shape, dt=F32):
        dram[name] = nc.dram_tensor(name, list(shape), dt, kind="ExternalInput").ap()
    def dout(name, shape, dt=F32):
        dram[name] = nc.dram_tensor(name, list(shape), dt, kind="ExternalOutput").ap()

    din("x", [T, D])
    din("W_aiv", [D, 3 * D])
    din("decay_bias", [D])
    din("W_mix", [D, D])
    din("b_mix", [D])
    din("c_id_bf", [128, 128], BF16)
    din("c_id_f32", [128, 128], F32)
    din("c_ones_row", [1, 128], F32)
    din("c_invd_col", [128, 1], F32)
    din("c_misc", [128, 4], F32)
    din("pk", [256 * 1024], I32)
    din("wscale", [1, 1], F32)
    din("fbias", [D], F32)
    din("W_qkv", [D, 3 * D])
    din("b_qkv", [3 * D])
    din("W_out", [D, D])
    din("b_out", [D])
    din("rel_emb_k", [129, 64])
    din("rel_emb_v", [129, 64])
    din("c_selq", [128, 128], BF16)
    din("c_ones_col_bf", [128, 1], BF16)
    din("c_ones_row_bf", [1, 128], BF16)
    din("l1f_wqkv", [2 * D * 1536])
    din("l1f_qs", [256])
    din("l1f_ks", [256])
    din("l1f_wo", [2 * 512 * D])
    din("l1f_lng", [2 * D])
    din("l1f_lnb", [2 * D])
    din("l1f_w1", [2 * D * 5460])
    din("l1f_b1", [2 * 5460])
    din("l1f_w2", [2 * 2730 * D])
    din("l1f_b2", [2 * D])
    din("c_cosq", [128, T], BF16)
    din("c_sinqS", [128, T], BF16)
    din("c_cosk", [128, T], BF16)
    din("c_sinkS", [128, T], BF16)
    din("c_sel8", [128, 512], BF16)
    din("c_selfq", [128, 32], BF16)
    din("c_cnL", [128, 128], BF16)
    din("c_cnR", [128, 128], BF16)
    dram["sc_amax"] = nc.dram_tensor("sc_amax", [T], F32, kind="Internal").ap()
    dram["sc_xspill"] = nc.dram_tensor("sc_xspill", [D * T], F32, kind="Internal").ap()
    dram["sc_qre0"] = nc.dram_tensor("sc_qre0", [T * 512], F32, kind="Internal").ap()
    dram["sc_qre1"] = nc.dram_tensor("sc_qre1", [T * 512], F32, kind="Internal").ap()
    dram["sc_e0"] = nc.dram_tensor("sc_e0", [T * 640], BF16, kind="Internal").ap()
    dram["sc_e1"] = nc.dram_tensor("sc_e1", [T * 640], BF16, kind="Internal").ap()
    dram["sc_mqs"] = nc.dram_tensor("sc_mqs", [48 * T], BF16, kind="Internal").ap()
    dout("y", [T, D])

    consts = {
        "c_id_bf": np.eye(128).astype(ml_dtypes.bfloat16),
        "c_id_f32": np.eye(128, dtype=np.float32),
        "c_ones_row": np.ones((1, 128), dtype=np.float32),
        "c_invd_col": np.full((128, 1), 1.0 / D, dtype=np.float32),
        "c_misc": np.broadcast_to(np.array([1.0, 1e-5, 1e-8, 0.0], dtype=np.float32), (128, 4)).copy(),
        "c_selq": _mk_selq(),
        "c_ones_col_bf": np.ones((128, 1), dtype=ml_dtypes.bfloat16),
        "c_ones_row_bf": np.ones((1, 128), dtype=ml_dtypes.bfloat16),
    }

    es = ExitStack()
    with es:
        B = {"dram": dram}
        B["xr"] = [p.sbuf(es, f"xr{j}", (128, T), F32) for j in range(NDT)]
        B["xb"] = [p.sbuf(es, f"xb{j}", (128, T), BF16) for j in range(NDT)]
        B["hb"] = [p.sbuf(es, f"hb{j}", (128, T), BF16) for j in range(NDT)]
        B["xin"] = [p.sbuf(es, f"xin{k}", (128, D), F32) for k in range(2)]
        B["wj"] = [p.sbuf(es, f"wj{k}", (128, 3 * D), BF16) for k in range(2)]
        B["wm"] = [p.sbuf(es, f"wm{j}", (128, D), BF16) for j in range(NDT)]
        B["U1"] = p.sbuf(es, "U1", (128, T), F32)
        B["U2"] = p.sbuf(es, "U2", (128, T), F32)
        B["U3"] = p.sbuf(es, "U3", (128, T), F32)
        B["G1"] = p.sbuf(es, "G1", (128, 32), F32)
        B["G2"] = p.sbuf(es, "G2", (128, 32), F32)
        B["G3"] = p.sbuf(es, "G3", (128, 32), F32)
        B["dbc"] = p.sbuf(es, "dbc", (128, NDT), F32)
        B["bmc"] = p.sbuf(es, "bmc", (128, NDT), F32)
        B["identb"] = p.sbuf(es, "identb", (128, 128), BF16)
        B["identf"] = p.sbuf(es, "identf", (128, 128), F32)
        B["ones_r"] = p.sbuf(es, "ones_r", (1, 128), F32)
        B["invd_c"] = p.sbuf(es, "invd_c", (128, 1), F32)
        B["cmisc"] = p.sbuf(es, "cmisc", (128, 4), F32)
        B["rowA"] = None
        B["rowB"] = None
        B["yout"] = B["xin"]
        B["amaxc"] = p.sbuf(es, "amaxc", (128, 16), F32)
        B["fbc"] = p.sbuf(es, "fbc", (128, NDT), F32)
        B["wsc"] = p.sbuf(es, "wsc", (1, 1), F32)
        B["U3h"] = B["U3"][:, 0:1024].bitcast(I32)
        B["rowA"] = B["U3"][0:1, 0:T]
        B["rowB"] = B["U1"][0:1, 0:T]
        B["vex"] = [p.sbuf(es, f"vex{k}", (128, 1040), BF16) for k in range(3)]
        B["qer_t"] = p.sbuf(es, "qer", (128, T), BF16)
        B["onesbc"] = p.sbuf(es, "onesbc", (128, 1), BF16)
        B["onesbr"] = p.sbuf(es, "onesbr", (1, 128), BF16)
        B["zbuf"] = p.sbuf(es, "zbuf", (128, 128), BF16)
        B["bfull"] = p.sbuf(es, "bfull", (128, 512), BF16)
        if n_blocks >= 4:
            B["gext"] = p.sbuf(es, "gext", (128, T), BF16)
            B["b1c"] = p.sbuf(es, "b1c", (128, 44), F32)
            B["sel8h"] = p.sbuf(es, "sel8h", (128, 512), BF16)
            B["selfq"] = p.sbuf(es, "selfq", (128, 32), BF16)
            B["cnL"] = p.sbuf(es, "cnL", (128, 128), BF16)
            B["cnR"] = p.sbuf(es, "cnR", (128, 128), BF16)
        B["pm"] = p.psum(es, "pm", (128, T), F32)
        B["pb"] = p.psum(es, "pb", (128, T), F32)

        s_ld = p.sem("s_ld")
        p.dma("sync", B["identb"][:], dram["c_id_bf"], s_ld)
        p.dma("sync", B["identf"][:], dram["c_id_f32"], s_ld)
        p.dma("sync", B["ones_r"][:], dram["c_ones_row"], s_ld)
        p.dma("sync", B["invd_c"][:], dram["c_invd_col"], s_ld)
        p.dma("sync", B["cmisc"][:], dram["c_misc"], s_ld)
        if n_blocks >= 4:
            p.dma("sync", B["sel8h"][:], dram["c_sel8"], s_ld)
            p.dma("sync", B["selfq"][:], dram["c_selfq"], s_ld)
            p.dma("sync", B["cnL"][:], dram["c_cnL"], s_ld)
            p.dma("sync", B["cnR"][:], dram["c_cnR"], s_ld)
            consts.update(_mk_b3_consts())

        build_block0(p, es, B)
        if n_blocks >= 2:
            build_block1(p, es, B)
        if n_blocks >= 3:
            build_block2(p, es, B)
        if n_blocks >= 4:
            build_block3(p, es, B)
        build_output_store(p, B, B["xr"], dram["y"])

        p.finalize(es)
    return nc, consts



def kernel(**inputs):
    f32 = np.float32
    nc, consts = build_program(n_blocks=4)
    qs = np.asarray(inputs["L1_ffn_q_scale"], f32)
    ks = np.asarray(inputs["L1_ffn_k_scale"], f32)
    common = {
        "W_aiv": np.asarray(inputs["L0_attn_W_aiv"], f32),
        "decay_bias": np.asarray(inputs["L0_attn_decay_bias"], f32),
        "W_mix": np.asarray(inputs["L0_attn_W_mix"], f32),
        "b_mix": np.asarray(inputs["L0_attn_b_mix"], f32),
        "pk": np.asarray(inputs["L0_ffn_packed_weight"], np.int32),
        "wscale": np.asarray(inputs["L0_ffn_weight_scale"], f32).reshape(1, 1),
        "fbias": np.asarray(inputs["L0_ffn_bias"], f32),
        "W_qkv": np.asarray(inputs["L1_attn_W_qkv"], f32),
        "b_qkv": np.asarray(inputs["L1_attn_b_qkv"], f32),
        "W_out": np.asarray(inputs["L1_attn_W_out"], f32),
        "b_out": np.asarray(inputs["L1_attn_b_out"], f32),
        "rel_emb_k": np.asarray(inputs["L1_attn_rel_emb_k"], f32),
        "rel_emb_v": np.asarray(inputs["L1_attn_rel_emb_v"], f32),
        "l1f_wqkv": np.asarray(inputs["L1_ffn_wqkv"], f32).reshape(-1),
        "l1f_qs": np.tile(qs, (1, 2)).reshape(-1),
        "l1f_ks": np.tile(ks, (1, 2)).reshape(-1),
        "l1f_wo": np.asarray(inputs["L1_ffn_wo"], f32).reshape(-1),
        "l1f_lng": np.asarray(inputs["L1_ffn_ln_g"], f32).reshape(-1),
        "l1f_lnb": np.asarray(inputs["L1_ffn_ln_b"], f32).reshape(-1),
        "l1f_w1": np.asarray(inputs["L1_ffn_w1"], f32).reshape(-1),
        "l1f_b1": np.asarray(inputs["L1_ffn_b1"], f32).reshape(-1),
        "l1f_w2": np.asarray(inputs["L1_ffn_w2"], f32).reshape(-1),
        "l1f_b2": np.asarray(inputs["L1_ffn_b2"], f32).reshape(-1),
    }
    common.update(consts)
    xall = np.asarray(inputs["L0_attn_x"], f32)
    in_maps = []
    for b in range(8):
        m = dict(common)
        m["x"] = xall[b]
        in_maps.append(m)
    from concourse.bass_utils import run_bass_kernel_spmd
    res = run_bass_kernel_spmd(nc, in_maps, core_ids=list(range(8)))
    return np.stack([np.asarray(res.results[b]["y"], f32) for b in range(8)], axis=0)


# version 18
# speedup vs baseline: 1.0117x; 1.0010x over previous
import numpy as np
from contextlib import ExitStack
import concourse.bass as bass
import concourse.mybir as mybir
from concourse.alu_op_type import AluOpType as OP

F32 = mybir.dt.float32
BF16 = mybir.dt.bfloat16
I32 = mybir.dt.int32
AF = mybir.ActivationFunctionType
AX = mybir.AxisListType

D = 1024
T = 2048
NDT = D // 128
NTT = T // 128
NB = T // 512
MAGIC = 12582912.0


class Sem:
    __slots__ = ("name", "n", "h")
    def __init__(self, name):
        self.name = name
        self.n = 0
        self.h = None


class Prog:
    ENGS = ("sync", "gpsimd", "vector", "scalar", "tensor")

    def __init__(self, nc):
        self.nc = nc
        self.thunks = {e: [] for e in self.ENGS}
        self.sems = {}

    def sem(self, name):
        s = self.sems.get(name)
        if s is None:
            s = Sem(name)
            self.sems[name] = s
        return s

    def on(self, eng, fn):
        self.thunks[eng].append(fn)

    def dma(self, eng, out, in_, sem):
        sem.n += 16
        self.on(eng, lambda q, o=out, i=in_, s=sem: q.dma_start(out=o, in_=i).then_inc(s.h, 16))

    def wait(self, eng, sem, val=None):
        v = sem.n if val is None else val
        if v > 0:
            self.on(eng, lambda q, s=sem, vv=v: q.wait_ge(s.h, vv))

    def op(self, eng, fn, sem=None, k=1):
        if sem is not None:
            sem.n += k
            self.on(eng, lambda q, f=fn, s=sem, kk=k: f(q).then_inc(s.h, kk))
            return sem.n
        self.on(eng, lambda q, f=fn: f(q))
        return None

    def drain(self, eng):
        self.on(eng, lambda q: q.drain())

    def finalize(self, es):
        for s in self.sems.values():
            s.h = es.enter_context(self.nc.semaphore(s.name))
        with self.nc.Block() as block:
            @block.sync
            def _(q):
                for t in self.thunks["sync"]:
                    t(q)

            @block.gpsimd
            def _(q):
                for t in self.thunks["gpsimd"]:
                    t(q)

            @block.vector
            def _(q):
                for t in self.thunks["vector"]:
                    t(q)

            @block.scalar
            def _(q):
                for t in self.thunks["scalar"]:
                    t(q)

            @block.tensor
            def _(q):
                for t in self.thunks["tensor"]:
                    t(q)


def col_ap(dram_ap, base, n=128):
    return bass.AP(dram_ap.tensor, base, [[1, n], [1, 1]])


def build_load_x(p, es, B):
    xr, xb = B["xr"], B["xb"]
    pm = B["pm"]
    identf = B["identf"]
    x_in = B["dram"]["x"]
    s_ld = p.sem("s_ld")
    s_tr = p.sem("s_tr")
    s_ve = p.sem("s_ve")
    s_se = p.sem("s_se")
    xin = B["xin"]
    s_x = [p.sem("s_x0"), p.sem("s_x1")]
    tr_done = [0, 0]
    ev_done = [0, 0, 0, 0]
    es_done = [0, 0, 0, 0]
    p.wait("tensor", s_ld, s_ld.n)
    for i in range(NTT):
        buf = i % 2
        if tr_done[buf]:
            p.wait("sync", s_tr, tr_done[buf])
        p.dma("sync", xin[buf][:], x_in[128 * i:128 * (i + 1), :], s_x[buf])
        p.wait("tensor", s_x[buf], s_x[buf].n)
        for half in range(2):
            bank = buf * 2 + half
            if ev_done[bank]:
                p.wait("tensor", s_ve, ev_done[bank])
                p.wait("tensor", s_se, es_done[bank])
            for k in range(4):
                j = half * 4 + k
                tok = p.op("tensor", lambda q, o=pm[:, bank * 512 + 128 * k: bank * 512 + 128 * (k + 1)],
                           i_=xin[buf][:, 128 * j:128 * (j + 1)]:
                           q.transpose(o, i_, identf[:]), s_tr if k == 3 else None)
            tr_done[buf] = s_tr.n
            p.wait("vector", s_tr, s_tr.n)
            for k in range(4):
                j = half * 4 + k
                src = pm[:, bank * 512 + 128 * k: bank * 512 + 128 * (k + 1)]
                p.op("vector", lambda q, o=xr[j][:, 128 * i:128 * (i + 1)], s=src:
                     q.tensor_copy(out=o, in_=s), s_ve if k == 3 else None)
            ev_done[bank] = s_ve.n
            p.wait("scalar", s_ve, s_ve.n)
            for k in range(4):
                j = half * 4 + k
                p.op("scalar", lambda q, o=xb[j][:, 128 * i:128 * (i + 1)],
                     s=xr[j][:, 128 * i:128 * (i + 1)]:
                     q.copy(out=o, in_=s), s_se if k == 3 else None)
            es_done[bank] = s_se.n




def build_block0(p, es, B):
    nc = p.nc
    xr, xb, hb = B["xr"], B["xb"], B["hb"]
    U1, U2, U3 = B["U1"], B["U2"], B["U3"]
    G1, G2, G3 = B["G1"], B["G2"], B["G3"]
    pm, pb = B["pm"], B["pb"]
    identf = B["identf"]
    dram = B["dram"]

    wj = [p.sbuf(es, f"b0_wj{k}", (128, 3 * D), BF16) for k in range(2)] if False else B["wj"]
    wm = B["wm"]
    dbc, bmc = B["dbc"], B["bmc"]

    s_ld = p.sem("s_ld")
    s_wav2 = [p.sem("s_wav0"), p.sem("s_wav1")]
    s_wmx = p.sem("s_wmx")
    s_tr = p.sem("s_tr")
    s_ve = p.sem("s_ve")
    s_se = p.sem("s_se")
    s_ge = p.sem("s_ge")

    x_in = dram["x"]
    W_aiv = dram["W_aiv"]
    db_in = dram["decay_bias"]
    W_mix = dram["W_mix"]
    bm_in = dram["b_mix"]

    for j in range(NDT):
        p.dma("sync", dbc[:, j:j + 1], col_ap(db_in, 128 * j), s_ld)
        p.dma("sync", bmc[:, j:j + 1], col_ap(bm_in, 128 * j), s_ld)
    for j in range(NDT):
        p.dma("gpsimd", wm[j][:], W_mix[128 * j:128 * (j + 1), :], s_wmx)

    def load_wav(j):
        dst = wj[j % 2]
        s_wav = s_wav2[j % 2]
        for qq in range(3):
            srcap = bass.AP(W_aiv.tensor, qq * D + 128 * j,
                            [[3072, 128], [3072 * 128, 8], [1, 128]])
            p.dma("gpsimd", dst[:, qq * D:(qq + 1) * D], srcap, s_wav)

    load_wav(0)
    load_wav(1)

    build_load_x(p, es, B)

    p.wait("tensor", s_ve, s_ve.n)
    p.wait("tensor", s_se, s_se.n)

    evac_a_done = 0
    evac_i_done = 0
    evac_v_done = 0
    h_tok = 0
    for j in range(NDT):
        wtile = wj[j % 2]
        p.wait("tensor", s_wav2[j % 2], 48 * (j // 2 + 1))
        if evac_v_done:
            p.wait("tensor", s_ve, evac_v_done)
            p.wait("tensor", s_se, evac_i_done)
        mmtok = {}
        for qi, ps in [(0, pm), (1, pb), (2, pm)]:
            if qi == 2:
                p.wait("tensor", s_se, evac_a_done)
            for nb in range(NB):
                for dtl in range(NDT):
                    lhsT = wtile[:, qi * D + dtl * 128: qi * D + (dtl + 1) * 128]
                    rhs = xb[dtl][:, nb * 512:(nb + 1) * 512]
                    p.op("tensor", lambda q, o=ps[:, nb * 512:(nb + 1) * 512], l=lhsT, r=rhs,
                         st=(dtl == 0), sp=(dtl == NDT - 1):
                         q.matmul(o, l, r, start=st, stop=sp),
                         s_tr if (dtl == NDT - 1 and nb == NB - 1) else None)
            mmtok[qi] = s_tr.n
            if qi == 0:
                if h_tok:
                    p.wait("scalar", s_ve, h_tok)
                p.wait("scalar", s_tr, mmtok[0])
                p.drain("scalar")
                for nb in range(NB):
                    p.op("scalar", lambda q, o=U1[:, nb * 512:(nb + 1) * 512],
                         i_=pm[:, nb * 512:(nb + 1) * 512], b=dbc[:, j:j + 1]:
                         q.activation(out=o, in_=i_, func=AF.Sigmoid, bias=b),
                         s_se if nb == NB - 1 else None)
                evac_a_done = s_se.n
            elif qi == 1:
                p.wait("scalar", s_tr, mmtok[1])
                for nb in range(NB):
                    p.op("scalar", lambda q, o=U2[:, nb * 512:(nb + 1) * 512],
                         i_=pb[:, nb * 512:(nb + 1) * 512]:
                         q.activation(out=o, in_=i_, func=AF.Sigmoid),
                         s_se if nb == NB - 1 else None)
                evac_i_done = s_se.n
        p.wait("vector", s_tr, mmtok[2])
        if h_tok:
            p.drain("vector")
        for nb in range(NB):
            p.op("vector", lambda q, o=U3[:, nb * 512:(nb + 1) * 512],
                 i_=pm[:, nb * 512:(nb + 1) * 512]:
                 q.tensor_copy(out=o, in_=i_),
                 s_ve if nb == NB - 1 else None)
        evac_v_done = s_ve.n
        if j + 2 < NDT:
            p.wait("gpsimd", s_tr, mmtok[2])
            load_wav(j + 2)

        p.wait("vector", s_se, evac_i_done)
        p.drain("vector")
        p.op("vector", lambda q: q.tensor_tensor(out=U3[:], in0=U2[:], in1=U3[:], op=OP.mult))
        p.drain("vector")
        sq_tok = p.op("vector", lambda q: q.tensor_tensor(out=U2[:], in0=U1[:], in1=U1[:], op=OP.mult), s_ve)
        p.wait("scalar", s_ve, sq_tok)
        p.drain("scalar")
        cm = B["cmisc"]
        rt_tok = p.op("scalar", lambda q, b=cm[:, 0:1]: q.activation(out=U2[:], in_=U2[:], func=AF.Sqrt,
                                                       scale=-1.0, bias=b), s_se)
        p.wait("vector", s_se, rt_tok)
        p.drain("vector")
        p.op("vector", lambda q: q.tensor_tensor(out=U3[:], in0=U3[:], in1=U2[:], op=OP.mult))
        p.drain("vector")
        for c in range(32):
            sl = slice(c * 64, (c + 1) * 64)
            p.op("vector", lambda q, s=sl: q.tensor_tensor_scan(
                out=U2[:, s], data0=U1[:, s], data1=U1[:, s],
                initial=1.0, op0=OP.mult, op1=OP.bypass))
        p.drain("vector")
        p.op("vector", lambda q: q.tensor_scalar(out=U1[:], in0=U2[:], scalar1=1e-10,
                                                 scalar2=None, op0=OP.max))
        p.drain("vector")
        p.op("vector", lambda q: q.reciprocal(out=U1[:], in_=U1[:]))
        p.drain("vector")
        p.op("vector", lambda q: q.tensor_tensor(out=U3[:], in0=U3[:], in1=U1[:], op=OP.mult))
        p.drain("vector")
        for c in range(32):
            sl = slice(c * 64, (c + 1) * 64)
            p.op("vector", lambda q, s=sl: q.tensor_tensor_scan(
                out=U1[:, s], data0=U3[:, s], data1=U3[:, s],
                initial=0.0, op0=OP.add, op1=OP.bypass))
        p.drain("vector")
        p.op("vector", lambda q: q.tensor_tensor(out=U1[:], in0=U2[:], in1=U1[:], op=OP.mult))
        p.drain("vector")
        ctd = bass.AP(U2, 63, [[T, 128], [64, 32]])
        cfs = bass.AP(U1, 63, [[T, 128], [64, 32]])
        p.op("vector", lambda q: q.tensor_scalar(out=G1[:], in0=ctd, scalar1=1e-10,
                                                 scalar2=None, op0=OP.max))
        p.drain("vector")
        p.op("vector", lambda q: q.tensor_tensor_scan(
            out=G2[:], data0=G1[:], data1=G1[:], initial=1.0, op0=OP.mult, op1=OP.bypass))
        p.drain("vector")
        p.op("vector", lambda q: q.tensor_scalar(out=G1[:], in0=G2[:], scalar1=1e-10,
                                                 scalar2=None, op0=OP.max))
        p.drain("vector")
        p.op("vector", lambda q: q.reciprocal(out=G1[:], in_=G1[:]))
        p.drain("vector")
        p.op("vector", lambda q: q.tensor_tensor(out=G3[:], in0=cfs, in1=G1[:], op=OP.mult))
        p.drain("vector")
        p.op("vector", lambda q: q.tensor_tensor_scan(
            out=G3[:], data0=G3[:], data1=G3[:], initial=0.0, op0=OP.add, op1=OP.bypass))
        p.drain("vector")
        p.op("vector", lambda q: q.tensor_tensor(out=G2[:], in0=G2[:], in1=G3[:], op=OP.mult))
        p.drain("vector")
        incb = bass.AP(G2, 0, [[32, 128], [1, 31], [0, 64]])
        u3v = U3[:, 64:T].rearrange("p (c j) -> p c j", c=31)
        u2v = U2[:, 64:T].rearrange("p (c j) -> p c j", c=31)
        p.op("vector", lambda q: q.tensor_tensor(out=u3v, in0=u2v, in1=incb, op=OP.mult))
        p.drain("vector")
        p.op("vector", lambda q, jj=j: q.tensor_tensor(
            out=hb[jj][:, 64:T], in0=U3[:, 64:T], in1=U1[:, 64:T], op=OP.add))
        h_tok = p.op("vector", lambda q, jj=j: q.tensor_copy(out=hb[jj][:, 0:64], in_=U1[:, 0:64]),
                     s_ve)

    B["tok_b0_mm"] = p.sems["s_tr"].n
    p.wait("tensor", s_ve, h_tok)
    p.wait("tensor", s_wmx, s_wmx.n)
    p.wait("tensor", s_se, s_se.n)
    r_tok = [0] * NDT
    for do in range(NDT):
        ps = pm if do % 2 == 0 else pb
        if do >= 2:
            p.wait("tensor", s_ve, r_tok[do - 2])
        for nb in range(NB):
            for dtl in range(NDT):
                lhsT = wm[dtl][:, do * 128:(do + 1) * 128]
                rhs = hb[dtl][:, nb * 512:(nb + 1) * 512]
                p.op("tensor", lambda q, o=ps[:, nb * 512:(nb + 1) * 512], l=lhsT, r=rhs,
                     st=(dtl == 0), sp=(dtl == NDT - 1):
                     q.matmul(o, l, r, start=st, stop=sp),
                     s_tr if (dtl == NDT - 1 and nb == NB - 1) else None)
        p.wait("vector", s_tr, s_tr.n)
        p.drain("vector")
        for nb in range(NB):
            p.op("vector", lambda q, o=xr[do][:, nb * 512:(nb + 1) * 512],
                 i0=ps[:, nb * 512:(nb + 1) * 512], sc=bmc[:, do:do + 1],
                 i1=xr[do][:, nb * 512:(nb + 1) * 512]:
                 q.scalar_tensor_tensor(out=o, in0=i0, scalar=sc, in1=i1,
                                        op0=OP.add, op1=OP.add),
                 s_ve if nb == NB - 1 else None)
        r_tok[do] = s_ve.n

    p.wait("vector", s_ve, r_tok[-1])
    build_ln(p, B, xr, xr, xb, wait_tok=r_tok[-1], gain=None, bias=None)
    B["tok_ln0_v"] = p.sems["s_ve"].n
    B["tok_ln0_g"] = p.sems["s_ge"].n


def build_block1(p, es, B):
    xr, xb = B["xr"], B["xb"]
    kb = B["hb"]
    wqT = B["wm"]
    wj = B["wj"]
    U1, U2, U3 = B["U1"], B["U2"], B["U3"]
    pm, pb = B["pm"], B["pb"]
    identf, identb = B["identf"], B["identb"]
    rowA, rowB = B["rowA"], B["rowB"]
    ones_r = B["ones_r"]
    dram = B["dram"]
    amaxc = B["amaxc"]
    fbc = B["fbc"]
    wsc = B["wsc"]

    s_ld = p.sem("s_ld1")
    s_tr = p.sem("s_tr")
    s_ve = p.sem("s_ve")
    s_se = p.sem("s_se")
    s_ge = p.sem("s_ge")
    s_am = p.sem("s_am")

    pk_in = dram["pk"]
    scr = dram["sc_amax"]

    pku = [wj[0][:, 0:2048].bitcast(I32), wj[0][:, 2048:4096] if False else wj[0][:, 1024:3072].bitcast(I32)]
    pku = [wj[0][:, 0:2048].bitcast(I32), wj[1][:, 0:2048].bitcast(I32)]
    w_u = wj[0][:, 2048:3072]
    itmp = wj[1][:, 2048:3072].bitcast(I32)
    itmp = B["U3h"]

    p.wait("sync", s_tr, B["tok_b0_mm"])
    p.dma("sync", pku[0], pk_in[0:128 * 1024], s_ld)
    p.dma("sync", pku[1], pk_in[128 * 1024:256 * 1024], s_ld)
    for j in range(NDT):
        p.dma("sync", fbc[:, j:j + 1], col_ap(dram["fbias"], 128 * j), s_ld)
    p.dma("sync", wsc[:], dram["wscale"], s_ld)

    p.wait("vector", s_ld, s_ld.n)
    p.wait("vector", s_ge, B["tok_ln0_g"])
    wq_tok = 0
    for blk in range(8):
        sh = 2 * (blk // 2)
        pt = blk % 2
        p.drain("vector")
        p.op("vector", lambda q, i0=pku[pt], s=sh: q.tensor_scalar(
            out=itmp, in0=i0, scalar1=s, scalar2=3, op0=OP.logical_shift_right,
            op1=OP.bitwise_and))
        p.drain("vector")
        p.op("vector", lambda q: q.tensor_copy(out=w_u, in_=itmp))
        p.drain("vector")
        wu_tok = p.op("vector", lambda q: q.tensor_scalar(
            out=w_u, in0=w_u, scalar1=-1.0, scalar2=None, op0=OP.add), s_ve)
        p.wait("tensor", s_tr, B["tok_b0_mm"])
        p.wait("tensor", s_ve, wu_tok)
        bank = blk % 4
        for jp in range(8):
            sl = pb[:, bank * 512 + 64 * jp: bank * 512 + 64 * (jp + 1)].bitcast(BF16)
            p.op("tensor", lambda q, o=sl, i_=w_u[:, 128 * jp:128 * (jp + 1)]:
                 q.transpose(o, i_, identb[:]),
                 s_tr if jp == 7 else None)
        tok = p.sems["s_tr"].n
        p.wait("vector", s_tr, tok)
        p.drain("vector")
        for jp in range(8):
            ssl = pb[:, bank * 512 + 64 * jp: bank * 512 + 64 * (jp + 1)].bitcast(BF16)
            p.op("vector", lambda q, o=wqT[jp][:, blk * 128:(blk + 1) * 128], s2=ssl:
                 q.tensor_copy(out=o, in_=s2),
                 s_ve if jp == 7 else None)
        wq_tok = p.sems["s_ve"].n

    p.drain("vector")
    p.op("vector", lambda q: q.tensor_tensor(out=U1[:], in0=xr[0][:], in1=xr[1][:], op=OP.max))
    p.op("vector", lambda q: q.tensor_tensor(out=U2[:], in0=xr[0][:], in1=xr[1][:], op=OP.min))
    for j in range(2, NDT):
        p.drain("vector")
        p.op("vector", lambda q, jj=j: q.tensor_tensor(out=U1[:], in0=U1[:], in1=xr[jj][:], op=OP.max))
        p.op("vector", lambda q, jj=j: q.tensor_tensor(out=U2[:], in0=U2[:], in1=xr[jj][:], op=OP.min))
    p.drain("vector")
    p.op("vector", lambda q: q.tensor_scalar(out=U2[:], in0=U2[:], scalar1=-1.0, scalar2=None, op0=OP.mult))
    p.drain("vector")
    p.op("vector", lambda q: q.tensor_tensor(out=U1[:], in0=U1[:], in1=U2[:], op=OP.max), s_ve)
    am_tok = p.sems["s_ve"].n
    p.wait("tensor", s_ve, am_tok)
    tok_red = [0] * NTT
    for i in range(NTT):
        bank = i % 4
        if i >= 4:
            p.wait("tensor", s_ve, tok_red[i - 4])
        p.op("tensor", lambda q, o=pm[:, bank * 512:bank * 512 + 128],
             i_=U1[:, 128 * i:128 * (i + 1)]:
             q.transpose(o, i_, identf[:]), s_tr)
        tok = p.sems["s_tr"].n
        p.wait("vector", s_tr, tok)
        p.drain("vector")
        p.op("vector", lambda q, o=amaxc[:, i:i + 1].rearrange("p (a o) -> p a o", a=1),
             i_=pm[:, bank * 512:bank * 512 + 128].rearrange("p (a j) -> p a j", a=1):
             q.tensor_reduce(out=o, in_=i_, axis=AX.X, op=OP.max), s_ve)
        tok_red[i] = p.sems["s_ve"].n
    red_tok = p.sems["s_ve"].n
    p.wait("sync", s_ve, red_tok)
    for i in range(NTT):
        p.dma("sync", bass.AP(scr.tensor, 128 * i, [[1, 128], [1, 1]]), amaxc[:, i:i + 1], s_am)
    p.wait("sync", s_am, p.sems["s_am"].n)
    p.dma("sync", rowA, bass.AP(scr.tensor, 0, [[T, 1], [1, T]]), s_am)
    p.wait("vector", s_am, p.sems["s_am"].n)
    p.drain("vector")
    p.op("vector", lambda q: q.tensor_scalar(out=rowB, in0=rowA, scalar1=1e-5,
                                             scalar2=None, op0=OP.max))
    p.drain("vector")
    p.op("vector", lambda q: q.reciprocal(out=rowA, in_=rowB))
    p.drain("vector")
    p.op("vector", lambda q: q.tensor_scalar(out=rowA, in0=rowA, scalar1=127.0,
                                             scalar2=None, op0=OP.mult))
    p.drain("vector")
    rows_tok = p.op("vector", lambda q: q.tensor_scalar(out=rowB, in0=rowB,
                                                        scalar1=wsc[0:1, 0:1], scalar2=1.0 / 127.0,
                                                        op0=OP.mult, op1=OP.mult), s_ve)
    p.wait("tensor", s_ve, rows_tok)
    p.wait("tensor", s_ve, wq_tok)
    for nb in range(NB):
        p.op("tensor", lambda q, o=pm[:, nb * 512:(nb + 1) * 512],
             r=rowA[0:1, nb * 512:(nb + 1) * 512]:
             q.matmul(o, ones_r[:], r, start=True, stop=True))
        p.op("tensor", lambda q, o=pb[:, nb * 512:(nb + 1) * 512],
             r=rowB[0:1, nb * 512:(nb + 1) * 512]:
             q.matmul(o, ones_r[:], r, start=True, stop=True),
             s_tr if nb == NB - 1 else None)
    bc_tok = p.sems["s_tr"].n
    p.wait("vector", s_tr, bc_tok)
    p.drain("vector")
    for j in range(NDT):
        p.op("vector", lambda q, jj=j: q.tensor_tensor(out=U2[:], in0=xr[jj][:],
                                                       in1=pm[:, 0:T], op=OP.mult))
        p.drain("vector")
        p.op("vector", lambda q, jj=j: q.tensor_scalar(
            out=kb[jj][:], in0=U2[:], scalar1=MAGIC, scalar2=MAGIC,
            op0=OP.add, op1=OP.subtract), s_ve)
        p.drain("vector")
    u3_tok = p.op("vector", lambda q: q.tensor_copy(out=U3[:], in_=pb[:, 0:T]), s_ve)
    kb_tok = u3_tok

    p.wait("tensor", s_ve, kb_tok)
    ge_tok = [0] * NDT
    ve_tok = [0] * NDT
    for do in range(NDT):
        ps = pm if do % 2 == 0 else pb
        ebuf = U2 if do % 2 == 0 else U1
        if do >= 2:
            p.wait("tensor", s_ve, ve_tok[do - 2])
        for nb in range(NB):
            for dtl in range(NDT):
                lhsT = wqT[dtl][:, do * 128:(do + 1) * 128]
                rhs = kb[dtl][:, nb * 512:(nb + 1) * 512]
                p.op("tensor", lambda q, o=ps[:, nb * 512:(nb + 1) * 512], l=lhsT, r=rhs,
                     st=(dtl == 0), sp=(dtl == NDT - 1):
                     q.matmul(o, l, r, start=st, stop=sp),
                     s_tr if (dtl == NDT - 1 and nb == NB - 1) else None)
        mmtok = p.sems["s_tr"].n
        p.wait("vector", s_tr, mmtok)
        p.drain("vector")
        p.op("vector", lambda q, e=ebuf, pss=ps: q.tensor_tensor(out=e[:], in0=pss[:, 0:T],
                                                                 in1=U3[:], op=OP.mult), s_ve)
        ve_tok[do] = p.sems["s_ve"].n
        p.drain("vector")
        p.op("vector", lambda q, dd=do, e=ebuf: q.scalar_tensor_tensor(
            out=xr[dd][:], in0=e[:], scalar=fbc[:, dd:dd + 1], in1=xr[dd][:],
            op0=OP.add, op1=OP.add), s_ve)
        ge_tok[do] = p.sems["s_ve"].n
        p.drain("vector")

    v_tok, g_tok = build_ln(p, B, xr, xr, xb, wait_tok=ge_tok[-1])
    B["tok_ln1_v"] = p.sems["s_ve"].n
    B["tok_ln1_g"] = p.sems["s_ge"].n


NH = 16
HD = 64
QB = 512
NQB = T // QB


def build_block2(p, es, B):
    xr, xb, hb = B["xr"], B["xb"], B["hb"]
    U1, U2, U3 = B["U1"], B["U2"], B["U3"]
    pm, pb = B["pm"], B["pb"]
    identf = B["identf"]
    dram = B["dram"]
    wj, wm = B["wj"], B["wm"]
    ones_r = B["ones_r"]

    s_tr = p.sem("s_tr")
    s_ve = p.sem("s_ve")
    s_se = p.sem("s_se")
    s_ge = p.sem("s_ge")
    s_ld2 = p.sem("s_ld2")
    s_xsp = p.sem("s_xsp")
    s_xre = p.sem("s_xre")
    s_wq = [p.sem("s_wq0"), p.sem("s_wq1")]
    s_wv = [p.sem("s_wv0"), p.sem("s_wv1")]
    s_wo = p.sem("s_wo")
    s_qw = [p.sem("s_qw0"), p.sem("s_qw1")]
    s_qr = [p.sem("s_qr0"), p.sem("s_qr1")]
    s_qrg = [p.sem("s_qrg0"), p.sem("s_qrg1")]
    s_bt = p.sem("s_bt")
    s_ac = p.sem("s_ac")
    s_rbc = p.sem("s_rbc")
    s_bf = p.sem("s_bf")
    s_ew = [p.sem("s_ew0"), p.sem("s_ew1")]
    s_er = [p.sem("s_er0"), p.sem("s_er1")]
    s_qe = [p.sem("s_qe0"), p.sem("s_qe1")]

    xrb = [xr[j][:, 0:T].bitcast(BF16) for j in range(NDT)]
    qkT = []
    for j in range(NDT):
        qkT.append(xrb[j][:, 0:T])
        qkT.append(xrb[j][:, T:2 * T])

    def a_sl(kt, c0, c1):
        base = (kt % 4) * 512
        return xb[kt // 4][:, base + c0: base + c1]

    u1b = U1[:, 0:T].bitcast(BF16)
    u2b = U2[:, 0:T].bitcast(BF16)
    u3b = U3[:, 0:T].bitcast(BF16)
    VE = []
    for ub in (u1b, u2b, u3b):
        for k in range(3):
            VE.append(ub[:, k * 1040:(k + 1) * 1040])
    for k in range(4, 8):
        VE.append(xb[k][:, 0:1040])
    vex = B["vex"]
    VE.extend(v[:, 0:1040] for v in vex)
    assert len(VE) == 16

    KEBt = wj[0]
    QELt = wj[1]
    qer = B["qer_t"]
    xin = B["xin"]
    MQS = xin[0][:, 0:D].bitcast(BF16)[0:16, 0:T]
    scm = dram_scm = B["dram"]["sc_mqs"]


    pekT = p.sbuf(es, "pekT", (64, 132), BF16)
    pekE = wj[0][0:64, 2048:2560]
    skq = wj[1][:, 2048:3072].bitcast(F32)
    pek_f = skq[:, 0:64]
    pkrow_f = skq[0:1, 64:128]
    pv_f = skq[:, 128:192]
    pvrow_f = skq[0:1, 192:256]
    pvb = p.sbuf(es, "pvb", (128, HD), BF16)
    pvb2 = p.sbuf(es, "pvb2", (1, HD), BF16)
    x1f = xin[1]
    x1bf = xin[1][:, 0:D].bitcast(BF16)
    pvE = [x1bf[0:128, 64 * k:64 * (k + 1)] for k in range(4)]
    onesbc = B["onesbc"]
    onesbr = B["onesbr"]
    selq = x1bf[0:128, 256:384]
    pksel0 = [x1bf[0:128, 384 + 16 * d:400 + 16 * d] for d in range(NDT)]
    pksel1 = [x1bf[0:128, 512 + 16 * d:528 + 16 * d] for d in range(NDT)]
    bqkvc = x1f[0:128, 320:344]
    kmx = x1f[0:16, 344:345]
    kpc = x1f[0:16, 345:346]

    boc = B["bmc"]
    zbuf = B["zbuf"]
    bfull = B["bfull"]
    btT = bfull
    srow = wm[1][:, 0:D].bitcast(F32)[0:1, 0:512]
    pkms = x1f[0:1, 346:347]

    wm5f = wm[5][:, 0:D].bitcast(F32)
    ccf = wm[2][:, 0:D].bitcast(F32)[0:64, 0:512]
    sab = wm[3][0:2, 0:1024]
    rbrow = wm[4][0:1, 0:512]
    qrst = [wm[5][:, 0:D].bitcast(F32)[:, 0:512], wm[0][:, 0:D].bitcast(F32)[:, 0:512]]
    wqb = [wm[6], wm[7]]
    wvb = wqb
    bvrow = wm[1][0:1, 0:D]

    W_qkv = dram["W_qkv"]
    b_qkv = dram["b_qkv"]
    W_out = dram["W_out"]
    rel_k = dram["rel_emb_k"]
    rel_v = dram["rel_emb_v"]
    xsp = dram["sc_xspill"]
    QRE = [dram["sc_qre0"], dram["sc_qre1"]]
    E2 = [dram["sc_e0"], dram["sc_e1"]]

    tok_prev_v = B["tok_ln1_v"]
    tok_prev_g = B["tok_ln1_g"]

    p.wait("sync", s_ve, tok_prev_v)
    p.wait("sync", s_ge, tok_prev_g)
    for j in range(NDT):
        p.dma("sync", bass.AP(xsp.tensor, j * 128 * T, [[T, 128], [1, T]]), xr[j][:, 0:T], s_xsp)

    for c in range(24):
        p.dma("sync", bqkvc[:, c:c + 1], col_ap(b_qkv, 128 * c), s_ld2)
    for j in range(NDT):
        p.dma("sync", boc[:, j:j + 1], col_ap(dram["b_out"], 128 * j), s_ld2)
    s_gd = p.sem("s_gd")
    p.dma("gpsimd", bvrow, bass.AP(b_qkv.tensor, 2048, [[1024, 1], [1, 1024]]), s_gd)
    p.dma("sync", selq, dram["c_selq"], s_ld2)
    p.dma("sync", onesbc[:], dram["c_ones_col_bf"], s_ld2)
    p.dma("sync", onesbr[:], dram["c_ones_row_bf"], s_ld2)
    p.dma("sync", pek_f, rel_k[0:128, :], s_ld2)
    p.dma("sync", pkrow_f, bass.AP(rel_k.tensor, 128 * 64, [[64, 1], [1, 64]]), s_ld2)
    p.dma("sync", pv_f, rel_v[0:128, :], s_ld2)
    p.dma("sync", pvrow_f, bass.AP(rel_v.tensor, 128 * 64, [[64, 1], [1, 64]]), s_ld2)
    ld2_tok = s_ld2.n

    p.wait("vector", s_ve, tok_prev_v)
    p.wait("vector", s_ld2, ld2_tok)
    p.drain("vector")
    zv_tok = p.op("vector", lambda q: q.memset(zbuf[:], 0.0), s_ve)
    p.drain("vector")
    bq_tok = p.op("vector", lambda q: q.tensor_scalar(out=bqkvc[:, 0:8], in0=bqkvc[:, 0:8],
                                                      scalar1=0.125, scalar2=None, op0=OP.mult), s_ve)
    p.wait("sync", s_ve, zv_tok)
    for par in range(2):
        for r in range(16):
            p.dma("sync", bass.AP(E2[par].tensor, r * 128 * 640, [[640, 128], [1, 128]]),
                  zbuf[:, 0:128], s_ew[par])
            p.dma("sync", bass.AP(E2[par].tensor, r * 128 * 640 + 512, [[640, 128], [1, 128]]),
                  zbuf[:, 0:128], s_ew[par])
        p.dma("sync", bass.AP(E2[par].tensor, 128, [[640, 128], [1, 128]]),
              zbuf[:, 0:128], s_ew[par])
        p.dma("sync", bass.AP(E2[par].tensor, 128 * 15 * 640 + 384, [[640, 128], [1, 128]]),
              zbuf[:, 0:128], s_ew[par])
    ez_tok = [s_ew[0].n, s_ew[1].n]

    p.wait("tensor", s_ve, tok_prev_v)
    p.wait("tensor", s_ld2, ld2_tok)
    p.op("tensor", lambda q: q.transpose(pm[0:64, 0:128], pek_f, identf[:]), None)
    p.op("tensor", lambda q: q.transpose(pm[0:64, 128:129], pkrow_f, identf[0:1, 0:1]), s_tr)
    tok_pek = p.sems["s_tr"].n
    p.wait("vector", s_tr, tok_pek)
    p.drain("vector")
    p.op("vector", lambda q: q.tensor_copy(out=pekT[:, 0:128], in_=pm[0:64, 0:128]))
    p.op("vector", lambda q: q.tensor_copy(out=pekT[:, 128:129], in_=pm[0:64, 128:129]))
    p.op("vector", lambda q: q.tensor_copy(out=pvb[:], in_=pv_f))
    p.op("vector", lambda q: q.tensor_copy(out=pvb2[:], in_=pvrow_f), s_ve)
    tok_pvb = p.sems["s_ve"].n
    p.drain("vector")
    p.op("vector", lambda q: q.tensor_copy(out=pekE[:, 0:192], in_=pekT[:, 0:1].to_broadcast((64, 192))))
    p.op("vector", lambda q: q.tensor_copy(out=pekE[:, 192:321], in_=pekT[:, 0:129]))
    p.op("vector", lambda q: q.tensor_copy(out=pekE[:, 321:512], in_=pekT[:, 128:129].to_broadcast((64, 191))), s_ve)
    tok_peke = p.sems["s_ve"].n
    p.drain("vector")
    p.op("vector", lambda q: q.memset(qer[64:128, 0:T], 0.0))
    p.op("vector", lambda q: q.memset(wj[1][64:128, 0:2048], 0.0))
    p.op("vector", lambda q: q.memset(wj[0][64:128, 2048:2560], 0.0), s_ve)
    tok_zpad = p.sems["s_ve"].n
    p.wait("tensor", s_ve, tok_pvb)
    p.op("tensor", lambda q: q.matmul(pb[0:128, 0:64], onesbr[:], pvb[0:1, 0:64], start=True, stop=True), None)
    p.op("tensor", lambda q: q.matmul(pb[0:128, 64:128], onesbr[:], pvb2[0:1, 0:64], start=True, stop=True), s_tr)
    tok_pvbc = p.sems["s_tr"].n
    p.wait("vector", s_tr, tok_pvbc)
    p.drain("vector")
    p.op("vector", lambda q: q.tensor_copy(out=pvE[0], in_=pb[0:128, 0:64]))
    p.op("vector", lambda q: q.tensor_copy(out=pvE[1][0:64, 0:64], in_=pb[0:64, 0:64]))
    p.op("vector", lambda q: q.tensor_copy(out=pvE[1][64:128, 0:64], in_=pvb[0:64, 0:64]))
    p.op("vector", lambda q: q.tensor_copy(out=pvE[2][0:64, 0:64], in_=pvb[64:128, 0:64]))
    p.op("vector", lambda q: q.tensor_copy(out=pvE[2][64:128, 0:64], in_=pb[64:128, 64:128]))
    p.op("vector", lambda q: q.tensor_copy(out=pvE[3], in_=pb[0:128, 64:128]), s_ve)
    tok_pve = p.sems["s_ve"].n
    p.drain("vector")
    for d in range(NDT):
        p.op("vector", lambda q, dd=d: q.memset(pksel0[dd], 0.0))
        p.op("vector", lambda q, dd=d: q.memset(pksel1[dd], 0.0))
    p.drain("vector")
    for d in range(NDT):
        p.op("vector", lambda q, dd=d: q.tensor_copy(out=pksel0[dd][0:64, 2 * dd:2 * dd + 1], in_=pekT[:, 0:1]))
        p.op("vector", lambda q, dd=d: q.tensor_copy(out=pksel0[dd][64:128, 2 * dd + 1:2 * dd + 2], in_=pekT[:, 0:1]))
        p.op("vector", lambda q, dd=d: q.tensor_copy(out=pksel1[dd][0:64, 2 * dd:2 * dd + 1], in_=pekT[:, 128:129]))
        p.op("vector", lambda q, dd=d: q.tensor_copy(out=pksel1[dd][64:128, 2 * dd + 1:2 * dd + 2], in_=pekT[:, 128:129]),
             s_ve if d == NDT - 1 else None)
    tok_sel = p.sems["s_ve"].n
    p.wait("vector", s_ge, tok_prev_g)
    p.drain("vector")
    tok_keb1 = p.op("vector", lambda q: q.memset(KEBt[64:66, 0:T], 1.0), s_ve)

    def load_wq(dd):
        dst = wqb[dd % 2]
        srcap = bass.AP(W_qkv.tensor, 128 * dd, [[3072, 128], [3072 * 128, 8], [1, 128]])
        p.dma("gpsimd", dst[:, 0:1024], srcap, s_wq[dd % 2])

    p.wait("gpsimd", s_xsp, s_xsp.n)
    p.wait("gpsimd", s_ve, tok_prev_v)
    load_wq(0)
    load_wq(1)

    p.wait("tensor", s_xsp, s_xsp.n)
    p.wait("tensor", s_ve, tok_pvb)
    evq_tok = 0
    for dd in range(16):
        p.wait("tensor", s_wq[dd % 2], 16 * (dd // 2 + 1))
        if evq_tok:
            p.wait("tensor", s_se, evq_tok)
        for nb in range(NB):
            for dt in range(NDT):
                lhsT = wqb[dd % 2][:, dt * 128:(dt + 1) * 128]
                rhs = xb[dt][:, nb * 512:(nb + 1) * 512]
                p.op("tensor", lambda q, o=pm[:, nb * 512:(nb + 1) * 512], l=lhsT, r=rhs,
                     st=(dt == 0), sp=(dt == NDT - 1):
                     q.matmul(o, l, r, start=st, stop=sp),
                     s_tr if (dt == NDT - 1 and nb == NB - 1) else None)
        mmtok = p.sems["s_tr"].n
        p.wait("scalar", s_tr, mmtok)
        if dd == 0:
            p.wait("scalar", s_xsp, s_xsp.n)
        sc = 0.125 if dd < 8 else 1.0
        for nb in range(NB):
            p.op("scalar", lambda q, o=qkT[dd][:, nb * 512:(nb + 1) * 512],
                 i_=pm[:, nb * 512:(nb + 1) * 512], b=bqkvc[:, dd:dd + 1], s=sc:
                 q.activation(out=o, in_=i_, func=AF.Identity, bias=b, scale=s),
                 s_se if nb == NB - 1 else None)
        evq_tok = p.sems["s_se"].n
        if dd + 2 < 16:
            p.wait("gpsimd", s_tr, mmtok)
            load_wq(dd + 2)

    def load_wv(dt):
        p.dma("gpsimd", wvb[dt % 2][:], W_qkv[128 * dt:128 * (dt + 1), 2048:3072], s_wv[dt % 2])

    p.wait("tensor", s_se, evq_tok)
    p.wait("tensor", s_ve, tok_pve)
    p.wait("tensor", s_gd, p.sems["s_gd"].n)
    ev_tok = 0
    dt_tok = {}
    nwv = [0, 0]
    last_mm_tok = p.sems["s_tr"].n
    for g in range(4):
        p.wait("gpsimd", s_tr, last_mm_tok)
        load_wv(0)
        load_wv(1)
        nwv[0] += 16
        nwv[1] += 16
        if ev_tok:
            p.wait("tensor", s_ve, ev_tok)
        for dt in range(NDT):
            p.wait("tensor", s_wv[dt % 2], nwv[dt % 2])
            for ti in range(4):
                tt = g * 4 + ti
                lhsT = xb[dt][:, tt * 128:(tt + 1) * 128]
                psA = pm if ti < 2 else pb
                off = (ti % 2) * 1024
                for half in range(2):
                    p.op("tensor", lambda q, o=psA[:, off + half * 512:off + (half + 1) * 512],
                         l=lhsT, r=wvb[dt % 2][:, half * 512:(half + 1) * 512], st=(dt == 0):
                         q.matmul(o, l, r, start=st, stop=False),
                         s_tr if (ti == 3 and half == 1) else None)
            dt_tok[(g, dt)] = p.sems["s_tr"].n
            if dt + 2 < NDT:
                p.wait("gpsimd", s_tr, dt_tok[(g, dt)])
                load_wv(dt + 2)
                nwv[dt % 2] += 16
        for ti in range(4):
            tt = g * 4 + ti
            psA = pm if ti < 2 else pb
            off = (ti % 2) * 1024
            for half in range(2):
                p.op("tensor", lambda q, o=psA[:, off + half * 512:off + (half + 1) * 512],
                     r=bvrow[0:1, half * 512:(half + 1) * 512]:
                     q.matmul(o, onesbr[:], r, start=False, stop=True),
                     s_tr if (ti == 3 and half == 1) else None)
        last_mm_tok = p.sems["s_tr"].n
        vtok = last_mm_tok
        p.wait("vector", s_tr, vtok)
        p.drain("vector")
        for ti in range(4):
            tt = g * 4 + ti
            psA = pm if ti < 2 else pb
            off = (ti % 2) * 1024
            for h in range(NH):
                p.op("vector", lambda q, t=tt, hh=h, ps2=psA, o2=off: q.tensor_copy(
                    out=VE[t][:, 65 * hh:65 * hh + 64], in_=ps2[:, o2 + hh * 64:o2 + (hh + 1) * 64]),
                    s_ve if (ti == 3 and h == NH - 1) else None)
        ev_tok = p.sems["s_ve"].n
    for tt in range(NTT):
        for h in range(NH):
            p.op("vector", lambda q, t=tt, hh=h: q.tensor_copy(
                out=VE[t][:, 65 * hh + 64:65 * hh + 65], in_=onesbc[:]),
                s_ve if (tt == NTT - 1 and h == NH - 1) else None)
    tok_ve = p.sems["s_ve"].n

    for (base, ps) in ((0, pm), (8, pb)):
        for nb in range(NB):
            for dt in range(NDT):
                p.op("vector", lambda q, d=dt, b=base, n=nb: q.tensor_tensor(
                    out=bfull[:, 0:512], in0=qkT[b + d][:, n * 512:(n + 1) * 512],
                    in1=qkT[b + d][:, n * 512:(n + 1) * 512], op=OP.mult), s_ve)
                sq_tok = p.sems["s_ve"].n
                p.wait("tensor", s_ve, sq_tok)
                p.op("tensor", lambda q, o=ps[0:16, nb * 512:(nb + 1) * 512], d=dt:
                     q.matmul(o, selq[:, 16 * d:16 * (d + 1)], bfull[:, 0:512],
                              start=(d == 0), stop=(d == NDT - 1)), s_tr)
                p.wait("vector", s_tr, p.sems["s_tr"].n)
    nrm_tok = p.sems["s_tr"].n
    p.wait("scalar", s_tr, nrm_tok)
    p.op("scalar", lambda q: q.activation(out=pm[32:48, 0:T], in_=pm[0:16, 0:T], func=AF.Sqrt), None)
    p.op("scalar", lambda q: q.activation(out=pb[32:48, 0:T], in_=pb[0:16, 0:T], func=AF.Sqrt), s_se)
    sq2_tok = p.sems["s_se"].n
    p.wait("vector", s_se, sq2_tok)
    p.drain("vector")
    p.op("vector", lambda q: q.tensor_reduce(out=kmx.rearrange("p (a o) -> p a o", a=1),
                                             in_=pb[32:48, 0:T].rearrange("p (a j) -> p a j", a=1),
                                             axis=AX.X, op=OP.max), s_ve)
    kmx_tok = p.sems["s_ve"].n
    p.drain("vector")
    p.op("vector", lambda q: q.tensor_tensor(out=btT[0:64, 0:129], in0=pekT[:, 0:129],
                                             in1=pekT[:, 0:129], op=OP.mult), s_ve)
    pks_tok = p.sems["s_ve"].n
    p.wait("tensor", s_ve, pks_tok)
    p.op("tensor", lambda q: q.matmul(pb[0:1, 1536:1665], onesbc[0:64, 0:1], btT[0:64, 0:129],
                                      start=True, stop=True), s_tr)
    pkn_tok = p.sems["s_tr"].n
    p.wait("scalar", s_tr, pkn_tok)
    p.op("scalar", lambda q: q.activation(out=srow[0:1, 0:129], in_=pb[0:1, 1536:1665], func=AF.Sqrt), s_se)
    pkn2_tok = p.sems["s_se"].n
    p.wait("vector", s_se, pkn2_tok)
    p.drain("vector")
    p.op("vector", lambda q: q.tensor_reduce(out=pkms.rearrange("p (a o) -> p a o", a=1),
                                             in_=srow[0:1, 0:129].rearrange("p (a j) -> p a j", a=1),
                                             axis=AX.X, op=OP.max), s_ve)
    pkm_tok = p.sems["s_ve"].n
    p.wait("tensor", s_ve, pkm_tok)
    p.op("tensor", lambda q: q.matmul(pb[0:16, 1792:1793], ones_r[0:1, 0:16], pkms,
                                      start=True, stop=True), s_tr)
    pkb_tok = p.sems["s_tr"].n
    p.wait("vector", s_tr, pkb_tok)
    p.drain("vector")
    p.op("vector", lambda q: q.tensor_tensor(out=kpc, in0=kmx, in1=pb[0:16, 1792:1793],
                                             op=OP.add), s_ve)
    p.drain("vector")
    m16_tok = p.op("vector", lambda q: q.tensor_scalar(out=MQS, in0=pm[32:48, 0:T], scalar1=kpc,
                                                       scalar2=-1.0, op0=OP.mult, op1=OP.mult), s_ve)
    p.wait("sync", s_ve, m16_tok)
    p.dma("sync", bass.AP(scm.tensor, 0, [[T, 16], [1, T]]), MQS, s_ld2)
    p.wait("tensor", s_ve, m16_tok)
    p.wait("tensor", s_ve, tok_sel)
    for (sel, ps) in ((pksel0, pm), (pksel1, pb)):
        for nb in range(NB):
            for d in range(NDT):
                p.op("tensor", lambda q, o=ps[0:16, nb * 512:(nb + 1) * 512], dd=d, s=sel:
                     q.matmul(o, s[dd][:], qkT[dd][:, nb * 512:(nb + 1) * 512],
                              start=(dd == 0), stop=(dd == NDT - 1)),
                     s_tr if d == NDT - 1 else None)
    qs_tok = p.sems["s_tr"].n
    p.wait("scalar", s_tr, qs_tok)
    p.wait("scalar", s_ld2, s_ld2.n)
    qs0_tok = p.op("scalar", lambda q: q.activation(out=MQS, in_=pm[0:16, 0:T], func=AF.Copy), s_se)
    p.wait("sync", s_se, qs0_tok)
    p.dma("sync", bass.AP(scm.tensor, 16 * T, [[T, 16], [1, T]]), MQS, s_ld2)
    p.wait("scalar", s_ld2, s_ld2.n)
    qs1_tok = p.op("scalar", lambda q: q.activation(out=MQS, in_=pb[0:16, 0:T], func=AF.Copy), s_se)
    p.wait("sync", s_se, qs1_tok)
    p.dma("sync", bass.AP(scm.tensor, 32 * T, [[T, 16], [1, T]]), MQS, s_ld2)
    qs2_tok = p.sems["s_se"].n

    p.wait("vector", s_se, qs2_tok)
    p.wait("vector", s_ve, tok_ve)
    p.wait("vector", s_ve, tok_keb1)
    prev_pe_done = 0
    last_sc = 0
    er_need = [0, 0]
    last_skq_use = 0
    prev_exp_tok = [0, 0]
    hb_tok = 0
    stA = [xb[4 + q][:, 1040:1424] for q in range(4)]
    stB = [xb[4 + q][:, 1424:1936] for q in range(4)]
    stC = [u1b[:, 3120:3632], u2b[:, 3120:3632], u3b[:, 3120:3632], x1bf[0:128, 694:1206]]
    pchunk = [pb[:, 512:1024].bitcast(BF16), pb[:, 1024:1536].bitcast(BF16)]

    def bchunk(c):
        return pchunk[c // 8][:, (c % 8) * 128:(c % 8 + 1) * 128]

    st2 = {"pend": None, "be": 0, "hb": 0, "ctxfree": [0, 0]}

    def b2_tail():
        pd = st2["pend"]
        if pd is None:
            return
        qbp, ctxp, rbcp, hp, parp, ern, fep = pd
        p.wait("tensor", s_er[parp], ern)
        p.wait("tensor", s_se, fep)
        for qtl in range(4):
            for ws in range(4):
                p.op("tensor", lambda q, o=bchunk(qtl * 4 + ws),
                     i_=stB[qtl][:, ws * 128:(ws + 1) * 128], idb=B["identb"]:
                     q.matmul(o, i_, idb[:], is_transpose=True, start=True, stop=True),
                     s_bt if (qtl == 3 and ws == 3) else None)
        bt2 = p.sems["s_bt"].n
        p.wait("scalar", s_bt, bt2)
        for qtl in range(4):
            p.op("scalar", lambda q, o=stC[qtl],
                 i_=pchunk[qtl // 2][:, (qtl % 2) * 512:(qtl % 2 + 1) * 512]:
                 q.activation(out=o, in_=i_, func=AF.Copy),
                 s_se if qtl == 3 else None)
        st2["be"] = p.sems["s_se"].n
        p.op("scalar", lambda q: q.activation(out=sab[0:1, 0:512], in_=pb[0:1, 0:512], func=AF.Copy), None)
        p.op("scalar", lambda q: q.activation(out=sab[0:1, 512:1024], in_=pb[0:1, 1536:2048], func=AF.Copy), s_se)
        sat_tok = p.sems["s_se"].n
        p.wait("tensor", s_se, st2["be"])
        for qtl in range(4):
            qtoff = qtl * 128
            for ws in range(4):
                p.op("tensor", lambda q, o=ctxp[0:64, qtoff:qtoff + 128], w=ws, s2=stC[qtl]:
                     q.matmul(o, pvE[w], s2[:, w * 128:(w + 1) * 128], start=False, stop=False), None)
        p.wait("tensor", s_se, sat_tok)
        p.op("tensor", lambda q, o=ctxp: q.matmul(o[0:64, 0:512], pvb[0:1, 0:64], sab[0:1, 0:512], start=False, stop=False), None)
        p.op("tensor", lambda q, o=ctxp: q.matmul(o[0:64, 0:512], pvb2[0:1, 0:64], sab[0:1, 512:1024], start=False, stop=False), None)
        p.op("tensor", lambda q, o=ctxp: q.matmul(o[0:65, 0:512], zbuf[0:1, 0:65], MQS[0:1, 0:512], start=False, stop=True), s_ac)
        a_close = p.sems["s_ac"].n
        p.wait("vector", s_ac, a_close)
        p.drain("vector")
        p.op("vector", lambda q, c2=ctxp: q.reciprocal(out=srow[0:1, 0:512], in_=c2[64:65, 0:512]))
        p.drain("vector")
        rb_tok = p.op("vector", lambda q: q.tensor_copy(out=rbrow[:], in_=srow[0:1, 0:512]), s_ve)
        p.wait("tensor", s_ve, rb_tok)
        p.wait("tensor", s_se, sat_tok)
        p.op("tensor", lambda q, o=rbcp: q.matmul(o, onesbr[0:1, 0:64], rbrow[:], start=True, stop=True), s_rbc)
        rbc_tok = p.sems["s_rbc"].n
        p.drain("vector")
        cc2 = p.op("vector", lambda q, c2=ctxp: q.tensor_copy(out=ccf[:, 0:512], in_=c2[0:64, 0:512]), s_ve)
        p.wait("vector", s_rbc, rbc_tok)
        p.drain("vector")
        st2["hb"] = p.op("vector", lambda q, hh=hp, b0=qbp * 512, r2=rbcp: q.tensor_tensor(
            out=hb[hh // 2][64 * (hh % 2):64 * (hh % 2) + 64, b0:b0 + 512],
            in0=ccf[:, 0:512], in1=r2, op=OP.mult), s_ve)
        st2["ctxfree"][qbp % 2] = cc2
        st2["pend"] = None

    for h in range(NH):
        par = h % 2
        if last_sc:
            p.wait("vector", s_tr, last_sc)
        p.drain("vector")
        p.op("vector", lambda q, hh=h: q.tensor_copy(
            out=KEBt[0:64, 0:T], in_=qkT[8 + hh // 2][64 * (hh % 2):64 * (hh % 2) + 64, 0:T]))
        p.op("vector", lambda q, hh=h: q.tensor_copy(
            out=QELt[0:64, 0:T], in_=qkT[hh // 2][64 * (hh % 2):64 * (hh % 2) + 64, 0:T]))
        cp_tok = p.op("vector", lambda q, hh=h: q.tensor_copy(
            out=qer[0:64, 0:T], in_=qkT[hh // 2][64 * (hh % 2):64 * (hh % 2) + 64, 0:T]), s_ve)
        p.wait("tensor", s_ve, cp_tok)
        if h >= 2:
            p.wait("tensor", s_qr[par], s_qr[par].n)
        p.wait("scalar", s_tr, last_skq_use)
        p.wait("tensor", s_se, p.sems["s_se"].n)
        qr_ev = [0, 0]
        qw_done = [0, 0]
        for qt in range(NTT):
            pp = qt % 2
            bank = pb[:, 512 + pp * 512:1024 + pp * 512]
            if qr_ev[pp]:
                p.wait("tensor", s_se, qr_ev[pp])
            p.op("tensor", lambda q, o=bank, t=qt:
                 q.matmul(o, QELt[0:128, 128 * t:128 * (t + 1)],
                          wj[0][0:128, 2048:2560], start=True, stop=True), s_tr)
            t_tok = p.sems["s_tr"].n
            p.wait("scalar", s_tr, t_tok)
            if qw_done[pp]:
                p.wait("scalar", s_qw[par], s_qw[par].n)
            p.op("scalar", lambda q, o=qrst[pp], i_=bank: q.activation(out=o, in_=i_, func=AF.Copy), s_se)
            qr_ev[pp] = p.sems["s_se"].n
            p.wait("scalar", s_se, qr_ev[pp])
            p.dma("scalar", bass.AP(QRE[par].tensor, 128 * qt * 512, [[512, 128], [1, 512]]),
                  qrst[pp], s_qw[par])
            qw_done[pp] = s_qw[par].n
        qw_tok = s_qw[par].n
        p.wait("tensor", s_se, p.sems["s_se"].n)
        p.wait("sync", s_ve, cp_tok)
        p.wait("sync", s_ld2, s_ld2.n)
        p.wait("sync", s_tr, p.sems["s_tr"].n)
        p.dma("sync", QELt[64:65, 0:T], bass.AP(scm.tensor, h * T, [[T, 1], [1, T]]), s_qe[par])
        p.dma("sync", QELt[65:66, 0:T], bass.AP(scm.tensor, (16 + h) * T, [[T, 1], [1, T]]), s_qe[par])
        p.dma("sync", qer[64:65, 0:T], bass.AP(scm.tensor, h * T, [[T, 1], [1, T]]), s_qe[par])
        p.dma("sync", qer[65:66, 0:T], bass.AP(scm.tensor, (32 + h) * T, [[T, 1], [1, T]]), s_qe[par])
        qe_tok = s_qe[par].n

        tstg = [skq[:, 0:128], skq[:, 128:256], skq[:, 256:384], skq[:, 384:512],
                qrst[0][:, 0:128], qrst[0][:, 128:256], qrst[0][:, 256:384], qrst[0][:, 384:512],
                qrst[1][:, 0:128], qrst[1][:, 128:256], qrst[1][:, 256:384], qrst[1][:, 384:512]]
        for qb in range(NQB):
            qb0 = qb * 512
            cbank = 1024 + 512 * (qb % 2)
            ctxA = pm[0:66, cbank:cbank + 512]
            rbcA = pb[64:128, 0:512]
            tchunks = {}
            tlist = []
            for kt in range(NTT):
                k0 = 128 * kt
                b0 = max(qb0, k0 - 64)
                b1 = min(qb0 + 512, k0 + 192)
                if b0 < b1:
                    c0 = b0
                    while c0 < b1:
                        cw = min(128, b1 - c0)
                        tchunks[(kt, c0)] = (len(tlist), cw)
                        tlist.append((kt, c0, cw))
                        c0 += cw
            assert len(tlist) <= 12
            p.wait("sync", s_qw[par], qw_tok)
            p.wait("sync", s_tr, last_skq_use)
            p.wait("gpsimd", s_qw[par], qw_tok)
            p.wait("gpsimd", s_tr, last_skq_use)
            for idx, (kt, c0, cw) in enumerate(tlist):
                p.dma("sync" if idx % 2 == 0 else "gpsimd", tstg[idx][0:cw, 0:128],
                      bass.AP(QRE[par].tensor, c0 * 511 + 128 * kt + 256, [[511, cw], [1, 128]]),
                      s_qr[par] if idx % 2 == 0 else s_qrg[par])
            tload_n = s_qr[par].n
            tload_g = s_qrg[par].n
            first_t = True
            p.wait("tensor", s_qe[par], qe_tok)
            if st2["ctxfree"][qb % 2]:
                p.wait("tensor", s_ve, st2["ctxfree"][qb % 2])
            for kt in range(NTT):
                if kt == 3:
                    b2_tail()
                k0 = 128 * kt
                b0 = max(qb0, k0 - 64)
                b1 = min(qb0 + 512, k0 + 192)
                Ls = max(qb0, k0 + 192)
                Rs = min(qb0 + 512, k0 - 64)
                pp = kt % 2
                sc_ps = pm[:, pp * 512:(pp + 1) * 512]
                if prev_exp_tok[pp]:
                    p.wait("tensor", s_se, prev_exp_tok[pp])
                ops = []
                if b0 < b1:
                    c0 = b0
                    while c0 < b1:
                        cw = min(128, b1 - c0)
                        ops.append(("T", c0, cw))
                        c0 += cw
                if Rs > qb0:
                    ops.append(("R", qb0, Rs))
                if b0 < b1:
                    ops.append(("B", b0, b1))
                if Ls < qb0 + 512:
                    ops.append(("L", Ls, qb0 + 512))
                nops = len(ops)
                for oi, op_ in enumerate(ops):
                    st = (oi == 0)
                    sp = (oi == nops - 1)
                    if op_[0] == "T":
                        _, c0, cw = op_
                        if first_t:
                            p.wait("tensor", s_qr[par], tload_n)
                            p.wait("tensor", s_qrg[par], tload_g)
                            first_t = False
                        idx, _cw = tchunks[(kt, c0)]
                        p.op("tensor", lambda q, o=sc_ps[:, c0 - qb0:c0 - qb0 + cw], i_=tstg[idx][0:cw, 0:128],
                             idf=identf[0:cw, 0:cw], s1=st, s2=sp:
                             q.matmul(o, i_, idf, is_transpose=True, start=s1, stop=s2), s_tr)
                    elif op_[0] == "R":
                        _, a0, a1 = op_
                        p.op("tensor", lambda q, o=sc_ps[:, a0 - qb0:a1 - qb0], l=KEBt[0:128, k0:k0 + 128],
                             r=qer[0:128, a0:a1], s1=st, s2=sp:
                             q.matmul(o, l, r, start=s1, stop=s2), s_tr)
                    elif op_[0] == "B":
                        _, a0, a1 = op_
                        p.op("tensor", lambda q, o=sc_ps[:, a0 - qb0:a1 - qb0], l=KEBt[0:65, k0:k0 + 128],
                             r=QELt[0:65, a0:a1], s1=st, s2=sp:
                             q.matmul(o, l, r, start=s1, stop=s2), s_tr)
                    else:
                        _, a0, a1 = op_
                        p.op("tensor", lambda q, o=sc_ps[:, a0 - qb0:a1 - qb0], l=KEBt[0:128, k0:k0 + 128],
                             r=QELt[0:128, a0:a1], s1=st, s2=sp:
                             q.matmul(o, l, r, start=s1, stop=s2), s_tr)
                sc_tok = p.sems["s_tr"].n
                last_sc = sc_tok
                if any(o[0] == "T" for o in ops):
                    last_skq_use = sc_tok
                p.wait("scalar", s_tr, sc_tok)
                p.op("scalar", lambda q, o=a_sl(kt, 0, 512), i_=sc_ps:
                     q.activation(out=o, in_=i_, func=AF.Exp), s_se)
                prev_exp_tok[pp] = p.sems["s_se"].n
                p.wait("tensor", s_se, prev_exp_tok[pp])
                p.op("tensor", lambda q, o=ctxA[0:65, 0:512], l=VE[kt][:, 65 * h:65 * h + 65],
                     r=a_sl(kt, 0, 512), st=(kt == 0):
                     q.matmul(o, l, r, start=st, stop=False), None)
            if st2["hb"]:
                p.wait("tensor", s_ve, st2["hb"])
            p.op("tensor", lambda q: q.matmul(pb[0:1, 0:512], zbuf[0:1, 0:1], MQS[0:1, 0:512],
                                              start=True, stop=False), None)
            p.op("tensor", lambda q: q.matmul(pb[0:1, 1536:2048], zbuf[0:1, 0:1], MQS[0:1, 0:512],
                                              start=True, stop=False), None)
            for kt in range(NTT):
                k0 = 128 * kt
                Ls = max(qb0, k0 + 192)
                Rs = min(qb0 + 512, k0 - 64)
                if Ls < qb0 + 512:
                    p.op("tensor", lambda q, o=pb[0:1, Ls - qb0:512], r=a_sl(kt, Ls - qb0, 512):
                         q.matmul(o, onesbc[:], r, start=False, stop=False), None)
                if Rs > qb0:
                    p.op("tensor", lambda q, o=pb[0:1, 1536:1536 + Rs - qb0], r=a_sl(kt, 0, Rs - qb0):
                         q.matmul(o, onesbc[:], r, start=False, stop=False), None)
            p.op("tensor", lambda q: q.matmul(pb[0:1, 0:512], zbuf[0:1, 0:1], MQS[0:1, 0:512],
                                              start=False, stop=True), None)
            p.op("tensor", lambda q: q.matmul(pb[0:1, 1536:2048], zbuf[0:1, 0:1], MQS[0:1, 0:512],
                                              start=False, stop=True), s_tr)
            if st2["be"]:
                p.wait("tensor", s_se, st2["be"])
            fops = [(qtl, i, qb * 4 + qtl + i - 1) for qtl in range(4) for i in range(3)
                    if 0 <= qb * 4 + qtl + i - 1 < NTT]
            for oi, (qtl, i, kt2) in enumerate(fops):
                p.op("tensor", lambda q, o=bchunk(qtl * 4 + i),
                     i_=a_sl(kt2, qtl * 128, qtl * 128 + 128), idb=B["identb"]:
                     q.matmul(o, i_, idb[:], is_transpose=True, start=True, stop=True),
                     s_bf if oi == len(fops) - 1 else None)
            ft = p.sems["s_bf"].n
            vi = [(1 if qb * 4 + qtl == 0 else 0, 2 if qb * 4 + qtl == NTT - 1 else 3)
                  for qtl in range(4)]
            p.wait("scalar", s_bf, ft)
            p.wait("scalar", s_ew[par], s_ew[par].n)
            for qtl in range(4):
                v0, v1 = vi[qtl]
                p.op("scalar", lambda q, o=stA[qtl][:, 0:(v1 - v0) * 128],
                     i_=pchunk[qtl // 2][:, (qtl % 2 * 4 + v0) * 128:(qtl % 2 * 4 + v1) * 128]:
                     q.activation(out=o, in_=i_, func=AF.Copy),
                     s_se if qtl == 3 else None)
            fe = p.sems["s_se"].n
            p.wait("sync", s_se, fe)
            p.wait("scalar", s_se, fe)
            if h >= 2:
                p.wait("sync", s_er[par], er_need[par])
                p.wait("scalar", s_er[par], er_need[par])
            for qtl in range(4):
                qt = qb * 4 + qtl
                v0, v1 = vi[qtl]
                p.dma("sync" if qtl % 2 == 0 else "scalar",
                      bass.AP(E2[par].tensor, 128 * qt * 640 + 128 + v0 * 128,
                              [[640, 128], [1, (v1 - v0) * 128]]),
                      stA[qtl][:, 0:(v1 - v0) * 128], s_ew[par])
            p.wait("sync", s_ew[par], s_ew[par].n)
            for qtl in range(4):
                qt = qb * 4 + qtl
                p.dma("sync", stB[qtl], bass.AP(E2[par].tensor, 128 * qt * 640,
                                                [[641, 128], [1, 512]]), s_er[par])
            er_need[par] = s_er[par].n
            st2["pend"] = (qb, ctxA, rbcA, h, par, s_er[par].n, fe)
        p.op("tensor", lambda q: q.nop(), s_tr)
        prev_pe_done = p.sems["s_tr"].n

    b2_tail()
    hb_tok = st2["hb"]
    p.op("tensor", lambda q: q.nop(), s_tr)
    prev_pe_done = p.sems["s_tr"].n

    p.wait("sync", s_tr, prev_pe_done)
    for j in range(NDT):
        p.dma("sync", xr[j][:, 0:T], bass.AP(xsp.tensor, j * 128 * T, [[T, 128], [1, T]]), s_xre)
    p.wait("gpsimd", s_tr, prev_pe_done)
    for dt in range(NDT):
        p.dma("gpsimd", VE[dt][:, 0:1024], W_out[128 * dt:128 * (dt + 1), :], s_wo)
    p.wait("tensor", s_wo, s_wo.n)
    p.wait("tensor", s_xre, s_xre.n)
    p.wait("tensor", s_ve, hb_tok)
    r_tok = [0] * NDT
    for do in range(NDT):
        ps = pm if do % 2 == 0 else pb
        if do >= 2:
            p.wait("tensor", s_ve, r_tok[do - 2])
        for nb in range(NB):
            for dt in range(NDT):
                p.op("tensor", lambda q, o=ps[:, nb * 512:(nb + 1) * 512], d=dt, dd=do, n=nb:
                     q.matmul(o, VE[d][:, dd * 128:(dd + 1) * 128], hb[d][:, n * 512:(n + 1) * 512],
                              start=(d == 0), stop=(d == NDT - 1)),
                     s_tr if (dt == NDT - 1 and nb == NB - 1) else None)
        mmtok = p.sems["s_tr"].n
        p.wait("vector", s_tr, mmtok)
        p.drain("vector")
        for nb in range(NB):
            p.op("vector", lambda q, o=xr[do][:, nb * 512:(nb + 1) * 512],
                 i0=ps[:, nb * 512:(nb + 1) * 512], sc=boc[:, do:do + 1],
                 i1=xr[do][:, nb * 512:(nb + 1) * 512]:
                 q.scalar_tensor_tensor(out=o, in0=i0, scalar=sc, in1=i1,
                                        op0=OP.add, op1=OP.add),
                 s_ve if nb == NB - 1 else None)
        r_tok[do] = p.sems["s_ve"].n

    build_ln(p, B, xr, xr, xb, wait_tok=r_tok[-1])
    B["tok_ln2_v"] = p.sems["s_ve"].n
    B["tok_ln2_g"] = p.sems["s_ge"].n


LH3 = 8
NW3 = 4
NHT3 = 22


def build_block3(p, es, B):
    xr, xb, hb = B["xr"], B["xb"], B["hb"]
    U1, U2, U3 = B["U1"], B["U2"], B["U3"]
    pm, pb = B["pm"], B["pb"]
    dram = B["dram"]
    wj, wm = B["wj"], B["wm"]
    xin = B["xin"]
    vex = B["vex"]
    identb = B["identb"]
    onesbc, onesbr = B["onesbc"], B["onesbr"]
    zbuf, bfull = B["zbuf"], B["bfull"]
    qert = B["qer_t"]
    gext = B["gext"]
    tbk = gext
    cnL, cnR = B["cnL"], B["cnR"]
    sel8h, selfq = B["sel8h"], B["selfq"]
    cm = B["cmisc"]
    b1c = B["b1c"]
    amx = B["amaxc"]

    s_tr = p.sem("s_tr")
    s_ve = p.sem("s_ve")
    s_se = p.sem("s_se")
    s_ge = p.sem("s_ge")
    s_l3 = p.sem("s_l3")
    s_w3 = [p.sem("s_w30"), p.sem("s_w31")]
    s_wo3 = p.sem("s_wo3")
    s_mb = p.sem("s_mb")
    s_mg = p.sem("s_mg")
    s_mb2 = p.sem("s_mb2")
    s_mg2 = p.sem("s_mg2")
    s_tg = p.sem("s_tg")
    s_w2 = [p.sem("s_w20"), p.sem("s_w21")]
    s_x3 = p.sem("s_x3")
    s_xr3 = [p.sem("s_xr3a"), p.sem("s_xr3b"), p.sem("s_xr3c")]

    u1b = U1[:, 0:T].bitcast(BF16)
    u2b = U2[:, 0:T].bitcast(BF16)
    u3b = U3[:, 0:T].bitcast(BF16)
    x0b = xin[0][:, 0:D].bitcast(BF16)
    x1b = xin[1][:, 0:D].bitcast(BF16)

    QKV = [hb[j][:, 0:T] for j in range(8)] + \
          [u1b[:, 0:T], u1b[:, T:2 * T], u2b[:, 0:T], u2b[:, T:2 * T]]
    hsh = [xb[j][:, 0:T] for j in range(4)] + \
          [u3b[:, 0:T], u3b[:, T:2 * T], x0b[0:128, 0:T], x1b[0:128, 0:T]]
    T1q = wj[0][:, 0:T]
    T2q = wj[1][:, 0:T]
    T1k = qert[:, 0:T]
    T2k = tbk[:, 0:T]
    VEL = [wj[0][:, c * 520:(c + 1) * 520] for c in range(3)] + \
          [wj[1][:, c * 520:(c + 1) * 520] for c in range(3)] + \
          [qert[:, c * 520:(c + 1) * 520] for c in range(3)] + \
          [gext[:, c * 520:(c + 1) * 520] for c in range(3)] + \
          [vex[0][:, 0:520], vex[0][:, 520:1040], vex[1][:, 0:520], vex[1][:, 520:1040]]
    assert len(VEL) == 16
    gg = [u1b[:, 0:T], u1b[:, T:2 * T], u2b[:, 0:T], u2b[:, T:2 * T]] + \
         [hb[j][:, 0:T] for j in range(8)] + \
         [xb[j][:, 0:T] for j in range(4, 8)] + \
         [wj[0][:, 0:T], wj[1][:, 0:T], qert[:, 0:T], gext[:, 0:T]]
    assert len(gg) == NHT3 - 2
    gg20c = [vex[0][:, 0:512], vex[0][:, 512:1024], vex[1][:, 0:512], vex[1][:, 512:1024]]
    gg21c = [wj[0][0:42, 2048:2560], wj[0][0:42, 2560:3072],
             wj[1][0:42, 2048:2560], wj[1][0:42, 2560:3072]]

    def ggsl(k, nb):
        if k < 20:
            return gg[k][0:(128 if k < 21 else 42), nb * 512:(nb + 1) * 512]
        if k == 20:
            return gg20c[nb]
        return gg21c[nb]

    HLq = (u3b[0:8, 0:T], u3b[0:8, T:2 * T])
    HLk = (x0b[0:8, 0:T], x1b[0:8, 0:T])
    wqb = [wm[6], wm[7]]
    wodst = [wm[6], wm[7], wm[1], wm[3]]
    w2slabs = [(wm[0], wm[5], wm[3]), (wm[6], wm[7], wm[1])]
    rbrow3 = wm[4][0:1, 0:512]
    ccrow = wm[2][:, 0:D].bitcast(F32)[0:1, 0:512]
    ctxs = wm[2][:, 0:D].bitcast(F32)[0:64, 0:512]
    mu_bf = u2b[0:1, 0:T]
    rstd_bf = u2b[0:1, T:2 * T]
    gbc, bbc, b2c = B["dbc"], B["bmc"], B["fbc"]
    qsc = amx
    scb = dram["sc_amax"]

    def asl(kt):
        return xb[kt // 4][:, (kt % 4) * 512:((kt % 4) + 1) * 512]

    tok_v = B["tok_ln2_v"]
    tok_g = B["tok_ln2_g"]

    p.wait("sync", s_ve, tok_v)
    p.wait("sync", s_ge, tok_g)
    p.wait("vector", s_tr, p.sems["s_tr"].n)
    p.wait("gpsimd", s_tr, p.sems["s_tr"].n)
    p.wait("gpsimd", s_ve, tok_v)
    p.wait("scalar", s_ge, tok_g)
    p.wait("tensor", s_ve, tok_v)
    p.wait("tensor", s_ge, tok_g)

    xsp3 = dram["sc_xspill"]
    for j in range(NDT):
        p.dma("sync", bass.AP(xsp3.tensor, j * 128 * T, [[T, 128], [1, T]]),
              xr[j][:, 0:T], s_x3)
    x3_spilled = s_x3.n

    carry = {"w2mm": 0, "w2ve": 0}

    for L in range(2):
        if L == 1:
            p.wait("sync", s_tr, carry["w2mm"])
            p.wait("sync", s_ve, carry["w2ve"])
            p.wait("vector", s_tr, carry["w2mm"])
            p.wait("gpsimd", s_tr, carry["w2mm"])
            p.wait("gpsimd", s_ve, carry["w2ve"])
            p.wait("scalar", s_ve, carry["w2ve"])
            p.wait("tensor", s_ve, carry["w2ve"])
        p.dma("sync", qsc[:, 0:1], col_ap(dram["l1f_qs"], L * 128), s_l3)
        p.dma("sync", qsc[:, 2:3], col_ap(dram["l1f_ks"], L * 128), s_l3)
        for j in range(NDT):
            p.dma("sync", gbc[:, j:j + 1], col_ap(dram["l1f_lng"], L * D + 128 * j), s_l3)
            p.dma("sync", bbc[:, j:j + 1], col_ap(dram["l1f_lnb"], L * D + 128 * j), s_l3)
            p.dma("sync", b2c[:, j:j + 1], col_ap(dram["l1f_b2"], L * D + 128 * j), s_l3)
        for half in range(2):
            for c in range(22):
                n = 128 if c < 21 else 42
                p.dma("sync", b1c[0:n, 22 * half + c:22 * half + c + 1],
                      col_ap(dram["l1f_b1"], L * 5460 + half * 2730 + 128 * c, n=n), s_l3)
        p.dma("sync", T1q, dram["c_cosq"][0:128, :], s_tg)
        p.dma("sync", T2q, dram["c_sinqS"][0:128, :], s_tg)
        p.dma("sync", T1k, dram["c_cosk"][0:128, :], s_tg)
        p.dma("sync", T2k, dram["c_sinkS"][0:128, :], s_tg)
        ld3 = s_l3.n
        tg3 = s_tg.n

        p.wait("vector", s_l3, ld3)
        p.wait("vector", s_tg, tg3)
        p.drain("vector")
        p.op("vector", lambda q: q.tensor_scalar(out=T1q, in0=T1q, scalar1=qsc[:, 0:1],
                                                 scalar2=8.0, op0=OP.mult, op1=OP.mult))
        p.op("vector", lambda q: q.tensor_scalar(out=T2q, in0=T2q, scalar1=qsc[:, 0:1],
                                                 scalar2=8.0, op0=OP.mult, op1=OP.mult))
        p.op("vector", lambda q: q.tensor_scalar(out=T1k, in0=T1k, scalar1=qsc[:, 2:3],
                                                 scalar2=None, op0=OP.mult))
        tab_tok = p.op("vector", lambda q: q.tensor_scalar(out=T2k, in0=T2k, scalar1=qsc[:, 2:3],
                                                           scalar2=None, op0=OP.mult), s_ve)

        p.drain("vector")
        if L == 1:
            for j in range(4):
                p.op("vector", lambda q, jj=j: q.tensor_copy(out=hsh[jj], in_=xr[jj][:, 0:T]))
        for j in range(4):
            p.op("vector", lambda q, jj=j: q.memset(hsh[4 + jj][:, 0:1], 0.0))
            p.op("vector", lambda q, jj=j: q.tensor_copy(out=hsh[4 + jj][:, 1:T],
                                                         in_=xr[4 + jj][:, 0:T - 1]),
                 s_ve if j == 3 else None)
        hsh_tok = p.sems["s_ve"].n

        wqkv_d = dram["l1f_wqkv"]

        def load_wq3(dd):
            srcap = bass.AP(wqkv_d.tensor, L * D * 1536 + 128 * dd,
                            [[1536, 128], [1536 * 128, 8], [1, 128]])
            p.dma("gpsimd", wqb[dd % 2][:, 0:1024], srcap, s_w3[dd % 2])

        bq0, bq1 = s_w3[0].n, s_w3[1].n
        load_wq3(0)
        load_wq3(1)
        p.wait("tensor", s_ve, hsh_tok)
        evq = 0
        qkv_mm_last = 0
        for dd in range(12):
            p.wait("tensor", s_w3[dd % 2], (bq0 if dd % 2 == 0 else bq1) + 16 * (dd // 2 + 1))
            if evq:
                p.wait("tensor", s_se, evq)
            for nb in range(NB):
                for dt in range(NDT):
                    p.op("tensor", lambda q, o=pm[:, nb * 512:(nb + 1) * 512],
                         l=wqb[dd % 2][:, dt * 128:(dt + 1) * 128],
                         r=hsh[dt][:, nb * 512:(nb + 1) * 512],
                         st=(dt == 0), sp=(dt == NDT - 1):
                         q.matmul(o, l, r, start=st, stop=sp),
                         s_tr if (dt == NDT - 1 and nb == NB - 1) else None)
            mmtok = p.sems["s_tr"].n
            qkv_mm_last = mmtok
            p.wait("scalar", s_tr, mmtok)
            for nb in range(NB):
                p.op("scalar", lambda q, o=QKV[dd][:, nb * 512:(nb + 1) * 512],
                     i_=pm[:, nb * 512:(nb + 1) * 512]:
                     q.activation(out=o, in_=i_, func=AF.Copy),
                     s_se if nb == NB - 1 else None)
            evq = p.sems["s_se"].n
            if dd + 2 < 12:
                p.wait("gpsimd", s_tr, mmtok)
                load_wq3(dd + 2)
        qkv_tok = evq

        p.wait("gpsimd", s_tr, qkv_mm_last)
        for dt in range(4):
            p.dma("gpsimd", wodst[dt][:, 0:1024],
                  bass.AP(dram["l1f_wo"].tensor, L * 512 * 1024 + dt * 128 * 1024,
                          [[1024, 128], [1, 1024]]), s_wo3)
        wo_ld = s_wo3.n

        p.wait("vector", s_se, qkv_tok)
        p.wait("tensor", s_se, qkv_tok)
        for (b0, ps) in ((0, pm), (4, pb)):
            for nb in range(NB):
                for dt in range(4):
                    p.op("vector", lambda q, d=dt, b=b0, n=nb: q.tensor_tensor(
                        out=bfull[:, 0:512], in0=QKV[b + d][:, n * 512:(n + 1) * 512],
                        in1=QKV[b + d][:, n * 512:(n + 1) * 512], op=OP.mult), s_ve)
                    sqtok = p.sems["s_ve"].n
                    p.wait("tensor", s_ve, sqtok)
                    p.op("tensor", lambda q, o=ps[0:8, nb * 512:(nb + 1) * 512], d=dt:
                         q.matmul(o, selfq[:, 8 * d:8 * (d + 1)], bfull[:, 0:512],
                                  start=(d == 0), stop=(d == 3)), s_tr)
                    p.wait("vector", s_tr, p.sems["s_tr"].n)
        nrm_tok = p.sems["s_tr"].n
        p.wait("scalar", s_tr, nrm_tok)
        p.op("scalar", lambda q: q.activation(out=pm[32:40, 0:T], in_=pm[0:8, 0:T], func=AF.Sqrt), None)
        p.op("scalar", lambda q: q.activation(out=pb[32:40, 0:T], in_=pb[0:8, 0:T], func=AF.Sqrt), s_se)
        sq2 = p.sems["s_se"].n
        p.wait("vector", s_se, sq2)
        p.drain("vector")
        p.op("vector", lambda q: q.memset(u3b[0:128, 0:2 * T], 0.0))
        p.op("vector", lambda q: q.memset(x0b[0:128, 0:T], 0.0))
        p.op("vector", lambda q: q.memset(x1b[0:128, 0:T], 0.0))
        p.drain("vector")
        for (ps, HL) in ((pm, HLq), (pb, HLk)):
            p.drain("vector")
            p.op("vector", lambda q, ps2=ps: q.tensor_scalar(out=ps2[64:72, 0:T], in0=ps2[32:40, 0:T],
                                                             scalar1=1e-12, scalar2=None, op0=OP.max))
            p.drain("vector")
            p.op("vector", lambda q, ps2=ps: q.reciprocal(out=ps2[64:72, 0:T], in_=ps2[64:72, 0:T]))
            p.drain("vector")
            p.op("vector", lambda q, ps2=ps, hl=HL: q.tensor_copy(out=hl[0], in_=ps2[64:72, 0:T]))
            p.drain("vector")
            p.op("vector", lambda q, ps2=ps, hl=HL: q.tensor_tensor(out=hl[1], in0=ps2[64:72, 0:T],
                                                                    in1=hl[0], op=OP.subtract), s_ve)
        hl_tok = p.sems["s_ve"].n
        hlz_tok = hl_tok
        HLFq = (u3b[0:128, 0:T], u3b[0:128, T:2 * T])
        HLFk = (x0b[0:128, 0:T], x1b[0:128, 0:T])

        rotscr = xb[0]
        prodscr = xb[1]
        prev_qp = 0
        prev_s = 0
        p.wait("gpsimd", s_ve, tab_tok)
        p.wait("gpsimd", s_se, qkv_tok)
        for side in range(2):
            b0 = 4 * side
            TT1 = T1q if side == 0 else T1k
            TT2 = T2q if side == 0 else T2k
            HL = HLFq if side == 0 else HLFk
            for dt in range(4):
                src = QKV[b0 + dt]
                p.wait("tensor", s_ve, hl_tok)
                p.wait("tensor", s_ve, hlz_tok)
                if prev_qp:
                    p.wait("tensor", s_ve, prev_qp)
                for nb in range(NB):
                    p.op("tensor", lambda q, o=pm[:, nb * 512:(nb + 1) * 512], d=dt, hl=HL, n=nb:
                         q.matmul(o, sel8h[0:128, 128 * d:128 * (d + 1)],
                                  hl[0][0:128, n * 512:(n + 1) * 512], start=True, stop=False), None)
                    p.op("tensor", lambda q, o=pm[:, nb * 512:(nb + 1) * 512], d=dt, hl=HL, n=nb:
                         q.matmul(o, sel8h[0:128, 128 * d:128 * (d + 1)],
                                  hl[1][0:128, n * 512:(n + 1) * 512], start=False, stop=True),
                         s_tr if nb == NB - 1 else None)
                rsb = p.sems["s_tr"].n
                if prev_qp:
                    p.wait("gpsimd", s_ve, prev_qp)
                p.drain("gpsimd")
                p.op("gpsimd", lambda q, s2=src, t1=TT1: q.tensor_tensor(
                    out=prodscr[:, 0:T], in0=s2, in1=t1, op=OP.mult), s_ge)
                m1tok = p.sems["s_ge"].n
                p.wait("vector", s_ve, hl_tok)
                if prev_s:
                    p.wait("vector", s_ge, prev_s)
                for blk in range(4):
                    r0 = 32 * blk
                    rin = r0 + 32 if blk % 2 == 0 else r0 - 32
                    p.op("vector", lambda q, a=r0, b=rin, s2=src, t2=TT2: q.tensor_tensor(
                        out=rotscr[a:a + 32, 0:T], in0=s2[b:b + 32, 0:T],
                        in1=t2[b:b + 32, 0:T], op=OP.mult),
                        s_ve if blk == 3 else None)
                m2tok = p.sems["s_ve"].n
                p.wait("gpsimd", s_ve, m2tok)
                p.drain("gpsimd")
                prev_s = p.op("gpsimd", lambda q: q.tensor_tensor(
                    out=prodscr[:, 0:T], in0=prodscr[:, 0:T], in1=rotscr[:, 0:T], op=OP.add), s_ge)
                p.wait("vector", s_ge, prev_s)
                p.wait("vector", s_tr, rsb)
                p.drain("vector")
                prev_qp = p.op("vector", lambda q, s2=src: q.tensor_tensor(
                    out=s2, in0=prodscr[:, 0:T], in1=pm[0:128, 0:T], op=OP.mult), s_ve)
        rot_tok = prev_qp
        rot_s_tok = prev_s

        p.wait("vector", s_ge, rot_s_tok)
        p.wait("tensor", s_ve, hl_tok)
        evtok = [0, 0, 0, 0]
        for vd in range(4):
            for tt in range(NTT):
                bank = tt % 4
                if evtok[bank]:
                    p.wait("tensor", s_ve, evtok[bank])
                p.op("tensor", lambda q, o=pb[:, bank * 512:(bank + 1) * 512].bitcast(BF16)[:, 0:128],
                     i_=QKV[8 + vd][:, tt * 128:(tt + 1) * 128], idb=identb:
                     q.matmul(o, i_, idb[:], is_transpose=True, start=True, stop=True), s_tr)
                ttok = p.sems["s_tr"].n
                p.wait("vector", s_tr, ttok)
                p.drain("vector")
                for hh in range(2):
                    h = 2 * vd + hh
                    p.op("vector", lambda q, t=tt, b=bank, h2=h, hh2=hh: q.tensor_copy(
                        out=VEL[t][:, 65 * h2:65 * h2 + 64],
                        in_=pb[:, b * 512:(b + 1) * 512].bitcast(BF16)[:, 64 * hh2:64 * (hh2 + 1)]),
                        s_ve if hh == 1 else None)
                evtok[bank] = p.sems["s_ve"].n
        p.drain("vector")
        for tt in range(NTT):
            for h in range(LH3):
                p.op("vector", lambda q, t=tt, h2=h: q.tensor_copy(
                    out=VEL[t][:, 65 * h2 + 64:65 * h2 + 65], in_=onesbc[:]),
                    s_ve if (tt == NTT - 1 and h == LH3 - 1) else None)
        vel_tok = p.sems["s_ve"].n

        def ke_ap(bb, r0, r1, c0, c1):
            if bb == 0:
                return wj[0][r0:r1, 2048 + c0:2048 + c1]
            return vex[2][r0:r1, c0:c1]

        def qe_ap(bb, r0, r1, c0, c1):
            return wj[1][r0:r1, 2048 + 512 * bb + c0:2048 + 512 * bb + c1]

        smb = [s_mb, s_mb2]
        smg = [s_mg, s_mg2]
        mcs = [amx[:, 8:12], amx[:, 12:16]]
        scoff = [0, 1024]
        p.drain("vector")
        p.op("vector", lambda q: q.memset(ke_ap(0, 64, 65, 0, 1024), 1.0))
        p.drain("vector")
        p.op("vector", lambda q: q.memset(ke_ap(1, 64, 65, 0, 1024), 1.0))
        p.drain("vector")
        p.op("vector", lambda q: q.memset(qe_ap(0, 64, 128, 0, 512), 0.0))
        p.drain("vector")
        ke1_tok = p.op("vector", lambda q: q.memset(qe_ap(1, 64, 128, 0, 512), 0.0), s_ve)
        p.wait("tensor", s_ve, vel_tok)
        p.wait("tensor", s_ve, ke1_tok)
        p.wait("scalar", s_ve, rot_tok)
        last_p2b = [0, 0]
        st = {"av": 0, "sel": 0, "ctx": 0, "pend": None}
        prev_exp = [0, 0]

        def attn_tail():
            pd = st["pend"]
            if pd is None:
                return
            h2, w2, fk = pd
            p.wait("tensor", s_ge, st["sel"])
            for kt in range(fk, 8):
                ktt = 4 * w2 + kt - 4
                p.op("tensor", lambda q, t=ktt, hh=h2, a=kt:
                     q.matmul(pb[0:65, 0:512], VEL[t][:, 65 * hh:65 * hh + 65], asl(a),
                              start=False, stop=(a == 7)), s_tr)
            st["av"] = p.sems["s_tr"].n
            p.wait("vector", s_tr, st["av"])
            p.op("vector", lambda q: q.reciprocal(out=ccrow, in_=pb[64:65, 0:512]))
            p.drain("vector")
            rb = p.op("vector", lambda q: q.tensor_copy(out=rbrow3, in_=ccrow), s_ve)
            p.wait("tensor", s_ve, rb)
            p.op("tensor", lambda q: q.matmul(pb[0:64, 1536:2048], onesbr[0:1, 0:64],
                                              rbrow3, start=True, stop=True), s_tr)
            rbc = p.sems["s_tr"].n
            p.wait("scalar", s_ve, rb)
            p.op("scalar", lambda q: q.activation(out=ctxs, in_=pb[0:64, 0:512],
                                                  func=AF.Copy), s_se)
            ctok = p.sems["s_se"].n
            p.wait("vector", s_tr, rbc)
            p.wait("vector", s_se, ctok)
            p.drain("vector")
            st["ctx"] = p.op("vector", lambda q, hh=h2, a=512 * w2: q.tensor_tensor(
                out=xb[4 + hh // 2][64 * (hh % 2):64 * (hh % 2) + 64, a:a + 512],
                in0=ctxs, in1=pb[0:64, 1536:2048], op=OP.mult), s_ve)
            st["pend"] = None

        iters = [(h, w) for h in range(LH3) for w in range(NW3)]
        cpq_tok = [0] * (len(iters) + 1)

        def stage_i(i2):
            if i2 >= len(iters):
                return
            h2, w2 = iters[i2]
            b2 = i2 % 2
            r0 = 64 * (h2 % 2)
            kt2 = QKV[4 + h2 // 2]
            qt2 = QKV[h2 // 2]
            if last_p2b[b2]:
                p.wait("vector", s_tr, last_p2b[b2])
            p.drain("vector")
            if w2 == 0:
                p.op("vector", lambda q, o=ke_ap(b2, 0, 64, 512, 1024), kt=kt2, rr=r0:
                     q.tensor_copy(out=o, in_=kt[rr:rr + 64, 0:512]))
            else:
                p.op("vector", lambda q, o=ke_ap(b2, 0, 64, 0, 1024), kt=kt2, rr=r0,
                     a=(w2 - 1) * 512:
                     q.tensor_copy(out=o, in_=kt[rr:rr + 64, a:a + 1024]))
            p.op("vector", lambda q, o=qe_ap(b2, 0, 64, 0, 512), qt=qt2, rr=r0, a=w2 * 512:
                 q.tensor_copy(out=o, in_=qt[rr:rr + 64, a:a + 512]))
            cpq_tok[i2] = p.op("vector", lambda q, o=qe_ap(b2, 64, 65, 0, 512):
                               q.memset(o, 0.0), s_ve)

        stage_i(0)
        for idx, (h, w) in enumerate(iters):
            b = idx % 2
            hb0 = 64 * (h % 2)
            qtile = QKV[h // 2]
            ktile = QKV[4 + h // 2]
            q0 = w * 512
            p.wait("vector", smb[b], smb[b].n)
            redtok = {}
            p.wait("tensor", s_ve, cpq_tok[idx])
            for qt in range(4):
                base = (qt % 2) * 1024
                if qt >= 2:
                    p.wait("tensor", s_ve, redtok[qt - 2])
                lq = qe_ap(b, 0, 128, qt * 128, (qt + 1) * 128)
                if w >= 1:
                    segs = [(0, 128, cnL), (128, 384, None), (512, 128, cnR)]
                    rlo = 0
                else:
                    segs = ([(512 - qt * 128, qt * 128, None)] if qt > 0 else []) + \
                           [(512, 128, cnR)]
                    rlo = 512 - qt * 128
                for si, (r0, nlen, cn) in enumerate(segs):
                    last = (si == len(segs) - 1) and cn is None
                    p.op("tensor", lambda q, o=pm[:, base + r0:base + r0 + nlen], l=lq,
                         r=ke_ap(b, 0, 128, qt * 128 + r0, qt * 128 + r0 + nlen),
                         sp=(cn is None):
                         q.matmul(o, l, r, start=True, stop=sp),
                         s_tr if last else None)
                    if cn is not None:
                        p.op("tensor", lambda q, o=pm[:, base + r0:base + r0 + nlen],
                             c2=cn, idb=identb:
                             q.matmul(o, idb[:], c2[:], start=False, stop=True),
                             s_tr if si == len(segs) - 1 else None)
                p1tok = p.sems["s_tr"].n
                p.wait("vector", s_tr, p1tok)
                p.op("vector", lambda q, bs=base, r=rlo, qt2=qt, bb=b: q.tensor_reduce(
                    out=mcs[bb][:, qt2:qt2 + 1].rearrange("p (a o) -> p a o", a=1),
                    in_=pm[:, bs + r:bs + 640].rearrange("p (a j) -> p a j", a=1),
                    axis=AX.X, op=OP.max), s_ve)
                redtok[qt] = p.sems["s_ve"].n
            p.drain("vector")
            neg_tok = p.op("vector", lambda q, bb=b: q.tensor_scalar(
                out=mcs[bb], in0=mcs[bb], scalar1=-1.0, scalar2=None, op0=OP.mult), s_ve)
            stage_i(idx + 1)
            p.wait("sync", s_ve, neg_tok)
            p.wait("sync", smg[b], smg[b].n)
            for qt in range(4):
                p.dma("sync", bass.AP(scb.tensor, scoff[b] + qt * 128, [[1, 128], [1, 1]]),
                      mcs[b][:, qt:qt + 1], smb[b])
            p.wait("gpsimd", smb[b], smb[b].n)
            if last_p2b[b]:
                p.wait("gpsimd", s_tr, last_p2b[b])
            p.dma("gpsimd", qe_ap(b, 64, 65, 0, 512),
                  bass.AP(scb.tensor, scoff[b], [[512, 1], [1, 512]]), smg[b])
            attn_tail()
            p.wait("tensor", smg[b], smg[b].n)
            if st["ctx"]:
                p.wait("tensor", s_ve, st["ctx"])
            p.op("tensor", lambda q: q.matmul(pb[0:65, 0:512], zbuf[0:1, 0:65],
                                              rbrow3, start=True, stop=False), None)
            first_kt = 4 if w == 0 else 0
            for kt in range(8):
                if kt < first_kt:
                    continue
                pp = kt % 2
                sc_ps = pb[:, 512 + pp * 512:1024 + pp * 512]
                if prev_exp[pp]:
                    p.wait("tensor", s_se, prev_exp[pp])
                p.op("tensor", lambda q, o=sc_ps, l=ke_ap(b, 0, 128, kt * 128, (kt + 1) * 128),
                     r=qe_ap(b, 0, 128, 0, 512):
                     q.matmul(o, l, r, start=True, stop=True), s_tr)
                sctok = p.sems["s_tr"].n
                if kt == 7:
                    last_p2b[b] = sctok
                p.wait("scalar", s_tr, sctok)
                if kt == first_kt:
                    if st["av"]:
                        p.wait("scalar", s_tr, st["av"])
                    if st["sel"]:
                        p.wait("scalar", s_ge, st["sel"])
                p.op("scalar", lambda q, o=asl(kt), i_=sc_ps:
                     q.activation(out=o, in_=i_, func=AF.Exp), s_se)
                prev_exp[pp] = p.sems["s_se"].n
            p.wait("gpsimd", s_se, p.sems["s_se"].n)
            p.drain("gpsimd")
            if w >= 1:
                p.op("gpsimd", lambda q: q.affine_select(
                    out=xb[0][:, 0:T], in_=xb[0][:, 0:T], pattern=[[128, 4], [-1, 512]],
                    base=0, channel_multiplier=1, compare_op=OP.is_ge, fill=0.0), None)
                p.drain("gpsimd")
            p.op("gpsimd", lambda q: q.affine_select(
                out=xb[1][:, 0:T], in_=xb[1][:, 0:T], pattern=[[-128, 4], [1, 512]],
                base=0, channel_multiplier=-1, compare_op=OP.is_ge, fill=0.0), s_ge)
            st["sel"] = p.sems["s_ge"].n
            st["pend"] = (h, w, first_kt)
        attn_tail()
        ctx_ev = st["ctx"]
        last_av = st["av"]

        p.wait("tensor", s_wo3, wo_ld)
        p.wait("tensor", s_ve, ctx_ev)
        if L == 0:
            p.wait("vector", s_x3, x3_spilled)
        r3tok = [0] * NDT
        womm_last = 0
        for do in range(NDT):
            ps = pm if do % 2 == 0 else pb
            if do >= 2:
                p.wait("tensor", s_ve, r3tok[do - 2])
            for nb in range(NB):
                for dt in range(4):
                    p.op("tensor", lambda q, o=ps[:, nb * 512:(nb + 1) * 512], d=dt, dd=do, n=nb:
                         q.matmul(o, wodst[d][:, dd * 128:(dd + 1) * 128],
                                  xb[4 + d][:, n * 512:(n + 1) * 512],
                                  start=(d == 0), stop=(d == 3)),
                         s_tr if (dt == 3 and nb == NB - 1) else None)
            womm_last = p.sems["s_tr"].n
            p.wait("vector", s_tr, womm_last)
            for nb in range(NB):
                p.op("vector", lambda q, o=xr[do][:, nb * 512:(nb + 1) * 512],
                     i0=ps[:, nb * 512:(nb + 1) * 512], i1=xr[do][:, nb * 512:(nb + 1) * 512]:
                     q.tensor_tensor(out=o, in0=i0, in1=i1, op=OP.add),
                     s_ve if nb == NB - 1 else None)
            r3tok[do] = p.sems["s_ve"].n

        p.wait("gpsimd", s_ve, r3tok[-1])
        p.wait("gpsimd", s_tr, last_av)
        p.drain("gpsimd")
        for j in range(4):
            p.op("gpsimd", lambda q, jj=j: q.tensor_copy(out=hsh[jj], in_=xr[jj][:, 0:T]),
                 s_ge if j == 3 else None)
            p.drain("gpsimd")
        h2g = p.sems["s_ge"].n
        p.wait("vector", s_ve, r3tok[-1])
        p.drain("vector")
        for j in range(4):
            p.op("vector", lambda q, jj=j: q.memset(hsh[4 + jj][:, 0:1], 0.0))
            p.op("vector", lambda q, jj=j: q.tensor_copy(out=hsh[4 + jj][:, 1:T],
                                                         in_=xr[4 + jj][:, 0:T - 1]),
                 s_ve if j == 3 else None)
        h2v = p.sems["s_ve"].n
        p.wait("tensor", s_ge, h2g)
        p.wait("tensor", s_ve, h2v)
        p.wait("vector", s_ge, h2g)
        for nb in range(NB):
            for j in range(8):
                p.op("tensor", lambda q, o=pm[0:1, nb * 512:(nb + 1) * 512], jj=j, n=nb:
                     q.matmul(o, onesbc[:], hsh[jj][:, n * 512:(n + 1) * 512],
                              start=(jj == 0), stop=(jj == 7)),
                     s_tr if j == 7 else None)
        for j in range(8):
            for nb in range(NB):
                p.op("vector", lambda q, jj=j, n=nb: q.tensor_tensor(
                    out=bfull[:, 0:512], in0=hsh[jj][:, n * 512:(n + 1) * 512],
                    in1=hsh[jj][:, n * 512:(n + 1) * 512], op=OP.mult), s_ve)
                sqtok = p.sems["s_ve"].n
                p.wait("tensor", s_ve, sqtok)
                p.op("tensor", lambda q, o=pb[0:1, nb * 512:(nb + 1) * 512], jj=j:
                     q.matmul(o, onesbc[:], bfull[:, 0:512],
                              start=(jj == 0), stop=(jj == 7)), s_tr)
                p.wait("vector", s_tr, p.sems["s_tr"].n)
        st_tok = p.sems["s_tr"].n
        p.wait("vector", s_tr, st_tok)
        p.drain("vector")
        p.op("vector", lambda q: q.tensor_scalar(out=U1[0:1, 0:T], in0=pm[0:1, 0:T],
                                                 scalar1=1.0 / D, scalar2=None, op0=OP.mult))
        p.drain("vector")
        p.op("vector", lambda q: q.tensor_scalar(out=pb[0:1, 0:T], in0=pb[0:1, 0:T],
                                                 scalar1=1.0 / D, scalar2=None, op0=OP.mult))
        p.drain("vector")
        p.op("vector", lambda q: q.tensor_tensor(out=U2[0:1, 0:T], in0=U1[0:1, 0:T],
                                                 in1=U1[0:1, 0:T], op=OP.mult))
        p.drain("vector")
        var_tok = p.op("vector", lambda q: q.tensor_tensor(out=pb[0:1, 0:T], in0=pb[0:1, 0:T],
                                                           in1=U2[0:1, 0:T], op=OP.subtract), s_ve)
        p.wait("scalar", s_ve, var_tok)
        srt = p.op("scalar", lambda q, b=cm[0:1, 1:2]: q.activation(out=pb[0:1, 0:T], in_=pb[0:1, 0:T],
                                                                    func=AF.Sqrt, bias=b), s_se)
        p.wait("vector", s_se, srt)
        p.drain("vector")
        p.op("vector", lambda q: q.reciprocal(out=pb[0:1, 0:T], in_=pb[0:1, 0:T]))
        p.drain("vector")
        p.op("vector", lambda q: q.tensor_copy(out=mu_bf, in_=U1[0:1, 0:T]))
        p.drain("vector")
        rows_tok = p.op("vector", lambda q: q.tensor_copy(out=rstd_bf, in_=pb[0:1, 0:T]), s_ve)
        p.wait("tensor", s_ve, rows_tok)
        for nb in range(NB):
            p.op("tensor", lambda q, o=pb[:, nb * 512:(nb + 1) * 512],
                 r=mu_bf[0:1, nb * 512:(nb + 1) * 512]:
                 q.matmul(o, onesbr[:], r, start=True, stop=True))
            p.op("tensor", lambda q, o=pm[:, nb * 512:(nb + 1) * 512],
                 r=rstd_bf[0:1, nb * 512:(nb + 1) * 512]:
                 q.matmul(o, onesbr[:], r, start=True, stop=True),
                 s_tr if nb == NB - 1 else None)
        bc_tok = p.sems["s_tr"].n
        p.wait("vector", s_tr, bc_tok)
        p.drain("vector")
        for j in range(8):
            p.op("vector", lambda q, jj=j: q.tensor_tensor(out=U1[:], in0=hsh[jj],
                                                           in1=pb[:, 0:T], op=OP.subtract))
            p.drain("vector")
            p.op("vector", lambda q: q.tensor_tensor(out=U1[:], in0=U1[:],
                                                     in1=pm[:, 0:T], op=OP.mult))
            p.drain("vector")
            p.op("vector", lambda q, jj=j: q.tensor_scalar(
                out=hsh[jj], in0=U1[:], scalar1=gbc[:, jj:jj + 1], scalar2=bbc[:, jj:jj + 1],
                op0=OP.mult, op1=OP.add), s_ve)
            p.drain("vector")
        ln_tok = p.sems["s_ve"].n

        w1_d = dram["l1f_w1"]
        w2_d = dram["l1f_w2"]
        w2cnt = {}

        def load_w2(do):
            sl = w2slabs[do % 2]
            base = L * 2730 * 1024 + do * 128
            p.dma("gpsimd", sl[0][:, 0:1024],
                  bass.AP(w2_d.tensor, base, [[1024, 128], [128 * 1024, 8], [1, 128]]), s_w2[do % 2])
            p.dma("gpsimd", sl[1][:, 0:1024],
                  bass.AP(w2_d.tensor, base + 8 * 128 * 1024,
                          [[1024, 128], [128 * 1024, 8], [1, 128]]), s_w2[do % 2])
            p.dma("gpsimd", sl[2][:, 0:640],
                  bass.AP(w2_d.tensor, base + 16 * 128 * 1024,
                          [[1024, 128], [128 * 1024, 5], [1, 128]]), s_w2[do % 2])
            p.dma("gpsimd", sl[2][0:42, 640:768],
                  bass.AP(w2_d.tensor, base + 21 * 128 * 1024, [[1024, 42], [1, 128]]), s_w2[do % 2])
            w2cnt[do] = s_w2[do % 2].n

        def load_w1(k, half):
            M = 128 if k < 21 else 42
            off = (2730 if half == 0 else 0) + k * 128
            srcw = bass.AP(w1_d.tensor, L * D * 5460 + off,
                           [[5460, 128], [5460 * 128, 8], [1, M]])
            p.dma("gpsimd", wqb[half][:, 0:8 * M], srcw, s_w3[half])

        p.wait("gpsimd", s_tr, womm_last)
        load_w2(0)
        b30, b31 = s_w3[0].n, s_w3[1].n
        load_w1(0, 0)
        load_w1(0, 1)
        prevst = 0
        prevge = 0
        amm = 0
        for k in range(NHT3):
            M = 128 if k < 21 else 42
            p.wait("tensor", s_w3[0], b30 + 16 * (k + 1))
            if prevge:
                p.wait("tensor", s_se, prevge)
            if k == 0:
                p.wait("tensor", s_ve, ln_tok)
            for nb in range(NB):
                for dt in range(8):
                    p.op("tensor", lambda q, o=pm[0:M, nb * 512:(nb + 1) * 512],
                         l=wqb[0][:, dt * M:(dt + 1) * M], r=hsh[dt][:, nb * 512:(nb + 1) * 512],
                         st=(dt == 0), sp=(dt == 7):
                         q.matmul(o, l, r, start=st, stop=sp),
                         s_tr if (dt == 7 and nb == NB - 1) else None)
            gmm = p.sems["s_tr"].n
            p.wait("scalar", s_tr, gmm)
            for nb in range(NB):
                p.op("scalar", lambda q, o=ggsl(k, nb),
                     i_=pm[0:M, nb * 512:(nb + 1) * 512], b=b1c[0:M, 22 + k:23 + k]:
                     q.activation(out=o, in_=i_, func=AF.Gelu, bias=b),
                     s_se if nb == NB - 1 else None)
            prevge = p.sems["s_se"].n
            if k + 1 < NHT3:
                p.wait("gpsimd", s_tr, gmm)
                load_w1(k + 1, 0)
            p.wait("tensor", s_w3[1], b31 + 16 * (k + 1))
            if prevst:
                p.wait("tensor", s_ve, prevst)
            for nb in range(NB):
                for dt in range(8):
                    p.op("tensor", lambda q, o=pb[0:M, nb * 512:(nb + 1) * 512],
                         l=wqb[1][:, dt * M:(dt + 1) * M], r=hsh[dt][:, nb * 512:(nb + 1) * 512],
                         st=(dt == 0), sp=(dt == 7):
                         q.matmul(o, l, r, start=st, stop=sp),
                         s_tr if (dt == 7 and nb == NB - 1) else None)
            amm = p.sems["s_tr"].n
            p.wait("vector", s_tr, amm)
            p.wait("vector", s_se, prevge)
            if k < 20:
                p.op("vector", lambda q, kk=k, M2=M: q.scalar_tensor_tensor(
                    out=gg[kk][0:M2, 0:T], in0=pb[0:M2, 0:T], scalar=b1c[0:M2, kk:kk + 1],
                    in1=gg[kk][0:M2, 0:T], op0=OP.add, op1=OP.mult), s_ve)
            else:
                for nb in range(NB):
                    p.op("vector", lambda q, o=ggsl(k, nb), n=nb, M2=M, kk=k:
                         q.scalar_tensor_tensor(
                             out=o, in0=pb[0:M2, n * 512:(n + 1) * 512],
                             scalar=b1c[0:M2, kk:kk + 1], in1=o, op0=OP.add, op1=OP.mult),
                         s_ve if nb == NB - 1 else None)
            prevst = p.sems["s_ve"].n
            if k + 1 < NHT3:
                p.wait("gpsimd", s_tr, amm)
                load_w1(k + 1, 1)
        gg_done = prevst

        p.wait("gpsimd", s_tr, amm)
        load_w2(1)
        w2ve = [0] * NDT
        w2mm = [0] * NDT
        for do in range(NDT):
            p.wait("tensor", s_w2[do % 2], w2cnt[do])
            ps = pm if do % 2 == 0 else pb
            if do == 0:
                p.wait("tensor", s_se, prevge)
                p.wait("tensor", s_ve, gg_done)
            if do >= 2:
                p.wait("tensor", s_ve, w2ve[do - 2])
            sl = w2slabs[do % 2]
            for nb in range(NB):
                for k in range(NHT3):
                    if k < 16:
                        lhsT = sl[k // 8][:, (k % 8) * 128:(k % 8 + 1) * 128]
                    elif k < 21:
                        lhsT = sl[2][:, (k - 16) * 128:(k - 15) * 128]
                    else:
                        lhsT = sl[2][0:42, 640:768]
                    p.op("tensor", lambda q, o=ps[:, nb * 512:(nb + 1) * 512], l=lhsT,
                         r=ggsl(k, nb), st=(k == 0), sp=(k == NHT3 - 1):
                         q.matmul(o, l, r, start=st, stop=sp),
                         s_tr if (k == NHT3 - 1 and nb == NB - 1) else None)
            w2mm[do] = p.sems["s_tr"].n
            p.wait("vector", s_tr, w2mm[do])
            for nb in range(NB):
                p.op("vector", lambda q, o=xr[do][:, nb * 512:(nb + 1) * 512],
                     i0=ps[:, nb * 512:(nb + 1) * 512], sc=b2c[:, do:do + 1],
                     i1=xr[do][:, nb * 512:(nb + 1) * 512]:
                     q.scalar_tensor_tensor(out=o, in0=i0, scalar=sc, in1=i1,
                                            op0=OP.add, op1=OP.add),
                     s_ve if nb == NB - 1 else None)
            w2ve[do] = p.sems["s_ve"].n
            if do + 2 < NDT:
                p.wait("gpsimd", s_tr, w2mm[do])
                load_w2(do + 2)
        carry["w2mm"] = w2mm[-1]
        carry["w2ve"] = w2ve[-1]

    xbufs = [U1[:, 0:T], U2[:, 0:T], U3[:, 0:T]]
    p.wait("sync", s_tr, carry["w2mm"])
    p.wait("sync", s_ve, carry["w2ve"])
    addtok = [0] * NDT
    for j in range(NDT):
        if j >= 3:
            p.wait("sync", s_ve, addtok[j - 3])
        p.dma("sync", xbufs[j % 3], bass.AP(xsp3.tensor, j * 128 * T, [[T, 128], [1, T]]),
              s_xr3[j % 3])
        p.wait("vector", s_xr3[j % 3], s_xr3[j % 3].n)
        p.drain("vector")
        addtok[j] = p.op("vector", lambda q, jj=j, b=xbufs[j % 3]: q.tensor_tensor(
            out=xr[jj][:, 0:T], in0=xr[jj][:, 0:T], in1=b, op=OP.add), s_ve)

    build_ln(p, B, xr, xr, xb, wait_tok=addtok[-1])
    B["tok_ln3_v"] = p.sems["s_ve"].n
    B["tok_ln3_g"] = p.sems["s_ge"].n



def build_ln(p, B, rin, yf, ybf, wait_tok, gain=None, bias=None, wait_ge_tok=0):
    U1, U2 = B["U1"], B["U2"]
    pm, pb = B["pm"], B["pb"]
    rowA, rowB = B["rowA"], B["rowB"]
    invd_c, ones_r = B["invd_c"], B["ones_r"]
    s_tr, s_ve, s_se, s_ge = p.sem("s_tr"), p.sem("s_ve"), p.sem("s_se"), p.sem("s_ge")
    n = len(rin)

    p.wait("tensor", s_ve, wait_tok)
    if wait_ge_tok:
        p.wait("tensor", s_ge, wait_ge_tok)
        p.wait("vector", s_ge, wait_ge_tok)
    for nb in range(NB):
        for j in range(n):
            p.op("tensor", lambda q, o=pb[0:1, nb * 512:(nb + 1) * 512],
                 r=rin[j][:, nb * 512:(nb + 1) * 512], st=(j == 0), sp=(j == n - 1):
                 q.matmul(o, invd_c[:], r, start=st, stop=sp),
                 s_tr if j == n - 1 else None)
    p.wait("vector", s_ve, wait_tok)
    p.drain("vector")
    for j in range(n):
        sq_tok = p.op("vector", lambda q, jj=j: q.tensor_tensor(
            out=U1[:], in0=rin[jj][:], in1=rin[jj][:], op=OP.mult), s_ve)
        p.wait("tensor", s_ve, sq_tok)
        for nb in range(NB):
            p.op("tensor", lambda q, o=pm[0:1, nb * 512:(nb + 1) * 512],
                 r=U1[:, nb * 512:(nb + 1) * 512], st=(j == 0), sp=(j == n - 1):
                 q.matmul(o, invd_c[:], r, start=st, stop=sp),
                 s_tr if nb == NB - 1 else None)
        if j < n - 1:
            p.wait("vector", s_tr, s_tr.n)
            p.drain("vector")
    m2_tok = p.sems["s_tr"].n
    p.wait("vector", s_tr, m2_tok)
    p.drain("vector")
    p.op("vector", lambda q: q.tensor_copy(out=rowA, in_=pb[0:1, :]))
    p.drain("vector")
    p.op("vector", lambda q: q.tensor_tensor(out=rowB, in0=rowA, in1=rowA, op=OP.mult))
    p.drain("vector")
    var_tok = p.op("vector", lambda q: q.tensor_tensor(out=rowB, in0=pm[0:1, :], in1=rowB,
                                                       op=OP.subtract), s_ve)
    p.wait("scalar", s_ve, var_tok)
    cm = B["cmisc"]
    sr_tok = p.op("scalar", lambda q, b=cm[0:1, 1:2]: q.activation(out=rowB, in_=rowB, func=AF.Sqrt,
                                                   bias=b), s_se)
    p.wait("vector", s_se, sr_tok)
    p.drain("vector")
    p.op("vector", lambda q: q.reciprocal(out=rowB, in_=rowB))
    p.drain("vector")
    rows_tok = p.op("vector", lambda q: q.engine_nop(), s_ve) if False else p.sems["s_ve"].n
    rows_tok = p.op("vector", lambda q: q.tensor_copy(out=rowA, in_=rowA), s_ve)
    p.wait("tensor", s_ve, rows_tok)
    for nb in range(NB):
        p.op("tensor", lambda q, o=pm[:, nb * 512:(nb + 1) * 512],
             r=rowB[0:1, nb * 512:(nb + 1) * 512]:
             q.matmul(o, ones_r[:], r, start=True, stop=True))
        p.op("tensor", lambda q, o=pb[:, nb * 512:(nb + 1) * 512],
             r=rowA[0:1, nb * 512:(nb + 1) * 512]:
             q.matmul(o, ones_r[:], r, start=True, stop=True),
             s_tr if nb == NB - 1 else None)
    bc_tok = p.sems["s_tr"].n
    p.wait("vector", s_tr, bc_tok)
    p.drain("vector")
    last_ge = 0
    for j in range(len(rin)):
        p.op("vector", lambda q, jj=j: q.tensor_tensor(out=U2[:], in0=rin[jj][:],
                                                       in1=pb[:, 0:T], op=OP.subtract))
        p.drain("vector")
        if gain is not None:
            p.op("vector", lambda q: q.tensor_tensor(out=U2[:], in0=U2[:], in1=pm[:, 0:T],
                                                     op=OP.mult))
            p.drain("vector")
            ytok = p.op("vector", lambda q, jj=j, g=gain, b=bias: q.tensor_scalar(
                out=yf[jj][:], in0=U2[:], scalar1=g[jj], scalar2=b[jj],
                op0=OP.mult, op1=OP.add), s_ve)
        else:
            ytok = p.op("vector", lambda q, jj=j: q.tensor_tensor(out=yf[jj][:], in0=U2[:],
                                                                  in1=pm[:, 0:T], op=OP.mult), s_ve)
        if ybf is not None:
            p.wait("gpsimd", s_ve, ytok)
            if last_ge:
                p.drain("gpsimd")
            last_ge = p.op("gpsimd", lambda q, jj=j: q.tensor_copy(out=ybf[jj][:], in_=yf[jj][:]), s_ge)
        p.drain("vector")
    return p.sems["s_ve"].n, last_ge


def build_output_store(p, B, src_tiles, out_y):
    pm = B["pm"]
    identf = B["identf"]
    yout = B["yout"]
    s_tr, s_ve = p.sem("s_tr"), p.sem("s_ve")
    s_st2 = [p.sem("s_st0"), p.sem("s_st1")]
    st_done = [0, 0]
    p.wait("tensor", s_ve, p.sems["s_ve"].n)
    for i in range(NTT):
        buf = i % 2
        for half in range(2):
            bank = buf * 2 + half
            for k in range(4):
                j = half * 4 + k
                p.op("tensor", lambda q, o=pm[:, bank * 512 + 128 * k: bank * 512 + 128 * (k + 1)],
                     i_=src_tiles[j][:, 128 * i:128 * (i + 1)]:
                     q.transpose(o, i_, identf[:]),
                     s_tr if k == 3 else None)
            p.wait("vector", s_tr, p.sems["s_tr"].n)
            if st_done[buf]:
                p.wait("vector", s_st2[buf], st_done[buf])
            if i >= 1:
                p.drain("vector")
            for k in range(4):
                j = half * 4 + k
                p.op("vector", lambda q, o=yout[buf][:, 128 * j:128 * (j + 1)],
                     s=pm[:, bank * 512 + 128 * k: bank * 512 + 128 * (k + 1)]:
                     q.tensor_copy(out=o, in_=s),
                     s_ve if (half == 1 and k == 3) else None)
        evtok = p.sems["s_ve"].n
        p.wait("sync", s_ve, evtok)
        p.dma("sync", out_y[128 * i:128 * (i + 1), :], yout[buf][:], s_st2[buf])
        st_done[buf] = s_st2[buf].n
        p.wait("tensor", s_ve, evtok)
    p.wait("sync", s_st2[0], s_st2[0].n)
    p.wait("sync", s_st2[1], s_st2[1].n)


class ProgP(Prog):
    def sbuf(self, es, name, shape, dtype):
        return es.enter_context(self.nc.sbuf_tensor(name, list(shape), dtype))

    def psum(self, es, name, shape, dtype=F32):
        return es.enter_context(self.nc.psum_tensor(name, list(shape), dtype))


def _mk_selq():
    import ml_dtypes
    s = np.zeros((128, 128), dtype=np.float32)
    for d in range(8):
        for h in range(16):
            if h // 2 == d:
                r0 = 64 * (h % 2)
                s[r0:r0 + 64, 16 * d + h] = 1.0
    return s.astype(ml_dtypes.bfloat16)


def _mk_b3_consts():
    import ml_dtypes
    bf = ml_dtypes.bfloat16
    t = np.arange(T, dtype=np.float64)
    half_idx = np.arange(0, 64, 2, dtype=np.float64)
    inv_freq = 1.0 / 10000.0 ** (half_idx / 64.0)
    fh = t[:, None] * inv_freq[None, :]
    freqs = np.concatenate([fh, fh], axis=1)
    base = (half_idx + 0.4 * 64) / (1.4 * 64)
    power = (t - T // 2) / 256.0
    sh = base[None, :] ** power[:, None]
    scale = np.concatenate([sh, sh], axis=1)
    cosq = (np.cos(freqs) * scale).T.astype(np.float32)
    sinq = (np.sin(freqs) * scale).T.astype(np.float32)
    cosk = (np.cos(freqs) / scale).T.astype(np.float32)
    sink = (np.sin(freqs) / scale).T.astype(np.float32)
    sw = (np.arange(64) + 32) % 64
    sgn2 = np.where(np.arange(64) < 32, 1.0, -1.0).astype(np.float32)[:, None]
    out = {
        "c_cosq": np.tile(cosq, (2, 1)).astype(bf),
        "c_sinqS": np.tile(sinq[sw, :] * sgn2, (2, 1)).astype(bf),
        "c_cosk": np.tile(cosk, (2, 1)).astype(bf),
        "c_sinkS": np.tile(sink[sw, :] * sgn2, (2, 1)).astype(bf),
    }
    s8 = np.zeros((128, 512), dtype=np.float32)
    for d in range(4):
        for r in range(128):
            s8[2 * d + (r >= 64), 128 * d + r] = 1.0
    out["c_sel8"] = s8.astype(bf)
    sq = np.zeros((128, 32), dtype=np.float32)
    for d in range(4):
        for r in range(128):
            sq[r, 8 * d + 2 * d + (r >= 64)] = 1.0
    out["c_selfq"] = sq.astype(bf)
    pp = np.arange(128)[:, None]
    uu = np.arange(128)[None, :]
    out["c_cnL"] = np.where(uu < pp, -100000.0, 0.0).astype(bf)
    out["c_cnR"] = np.where(uu > pp, -100000.0, 0.0).astype(bf)
    return out


def build_program(n_blocks=1):
    import ml_dtypes
    nc = bass.Bass("TRN2", target_bir_lowering=False)
    p = ProgP(nc)

    dram = {}
    def din(name, shape, dt=F32):
        dram[name] = nc.dram_tensor(name, list(shape), dt, kind="ExternalInput").ap()
    def dout(name, shape, dt=F32):
        dram[name] = nc.dram_tensor(name, list(shape), dt, kind="ExternalOutput").ap()

    din("x", [T, D])
    din("W_aiv", [D, 3 * D])
    din("decay_bias", [D])
    din("W_mix", [D, D])
    din("b_mix", [D])
    din("c_id_bf", [128, 128], BF16)
    din("c_id_f32", [128, 128], F32)
    din("c_ones_row", [1, 128], F32)
    din("c_invd_col", [128, 1], F32)
    din("c_misc", [128, 4], F32)
    din("pk", [256 * 1024], I32)
    din("wscale", [1, 1], F32)
    din("fbias", [D], F32)
    din("W_qkv", [D, 3 * D])
    din("b_qkv", [3 * D])
    din("W_out", [D, D])
    din("b_out", [D])
    din("rel_emb_k", [129, 64])
    din("rel_emb_v", [129, 64])
    din("c_selq", [128, 128], BF16)
    din("c_ones_col_bf", [128, 1], BF16)
    din("c_ones_row_bf", [1, 128], BF16)
    din("l1f_wqkv", [2 * D * 1536])
    din("l1f_qs", [256])
    din("l1f_ks", [256])
    din("l1f_wo", [2 * 512 * D])
    din("l1f_lng", [2 * D])
    din("l1f_lnb", [2 * D])
    din("l1f_w1", [2 * D * 5460])
    din("l1f_b1", [2 * 5460])
    din("l1f_w2", [2 * 2730 * D])
    din("l1f_b2", [2 * D])
    din("c_cosq", [128, T], BF16)
    din("c_sinqS", [128, T], BF16)
    din("c_cosk", [128, T], BF16)
    din("c_sinkS", [128, T], BF16)
    din("c_sel8", [128, 512], BF16)
    din("c_selfq", [128, 32], BF16)
    din("c_cnL", [128, 128], BF16)
    din("c_cnR", [128, 128], BF16)
    dram["sc_amax"] = nc.dram_tensor("sc_amax", [T], F32, kind="Internal").ap()
    dram["sc_xspill"] = nc.dram_tensor("sc_xspill", [D * T], F32, kind="Internal").ap()
    dram["sc_qre0"] = nc.dram_tensor("sc_qre0", [T * 512], F32, kind="Internal").ap()
    dram["sc_qre1"] = nc.dram_tensor("sc_qre1", [T * 512], F32, kind="Internal").ap()
    dram["sc_e0"] = nc.dram_tensor("sc_e0", [T * 640], BF16, kind="Internal").ap()
    dram["sc_e1"] = nc.dram_tensor("sc_e1", [T * 640], BF16, kind="Internal").ap()
    dram["sc_mqs"] = nc.dram_tensor("sc_mqs", [48 * T], BF16, kind="Internal").ap()
    dout("y", [T, D])

    consts = {
        "c_id_bf": np.eye(128).astype(ml_dtypes.bfloat16),
        "c_id_f32": np.eye(128, dtype=np.float32),
        "c_ones_row": np.ones((1, 128), dtype=np.float32),
        "c_invd_col": np.full((128, 1), 1.0 / D, dtype=np.float32),
        "c_misc": np.broadcast_to(np.array([1.0, 1e-5, 1e-8, 0.0], dtype=np.float32), (128, 4)).copy(),
        "c_selq": _mk_selq(),
        "c_ones_col_bf": np.ones((128, 1), dtype=ml_dtypes.bfloat16),
        "c_ones_row_bf": np.ones((1, 128), dtype=ml_dtypes.bfloat16),
    }

    es = ExitStack()
    with es:
        B = {"dram": dram}
        B["xr"] = [p.sbuf(es, f"xr{j}", (128, T), F32) for j in range(NDT)]
        B["xb"] = [p.sbuf(es, f"xb{j}", (128, T), BF16) for j in range(NDT)]
        B["hb"] = [p.sbuf(es, f"hb{j}", (128, T), BF16) for j in range(NDT)]
        B["xin"] = [p.sbuf(es, f"xin{k}", (128, D), F32) for k in range(2)]
        B["wj"] = [p.sbuf(es, f"wj{k}", (128, 3 * D), BF16) for k in range(2)]
        B["wm"] = [p.sbuf(es, f"wm{j}", (128, D), BF16) for j in range(NDT)]
        B["U1"] = p.sbuf(es, "U1", (128, T), F32)
        B["U2"] = p.sbuf(es, "U2", (128, T), F32)
        B["U3"] = p.sbuf(es, "U3", (128, T), F32)
        B["G1"] = p.sbuf(es, "G1", (128, 32), F32)
        B["G2"] = p.sbuf(es, "G2", (128, 32), F32)
        B["G3"] = p.sbuf(es, "G3", (128, 32), F32)
        B["dbc"] = p.sbuf(es, "dbc", (128, NDT), F32)
        B["bmc"] = p.sbuf(es, "bmc", (128, NDT), F32)
        B["identb"] = p.sbuf(es, "identb", (128, 128), BF16)
        B["identf"] = p.sbuf(es, "identf", (128, 128), F32)
        B["ones_r"] = p.sbuf(es, "ones_r", (1, 128), F32)
        B["invd_c"] = p.sbuf(es, "invd_c", (128, 1), F32)
        B["cmisc"] = p.sbuf(es, "cmisc", (128, 4), F32)
        B["rowA"] = None
        B["rowB"] = None
        B["yout"] = B["xin"]
        B["amaxc"] = p.sbuf(es, "amaxc", (128, 16), F32)
        B["fbc"] = p.sbuf(es, "fbc", (128, NDT), F32)
        B["wsc"] = p.sbuf(es, "wsc", (1, 1), F32)
        B["U3h"] = B["U3"][:, 0:1024].bitcast(I32)
        B["rowA"] = B["U3"][0:1, 0:T]
        B["rowB"] = B["U1"][0:1, 0:T]
        B["vex"] = [p.sbuf(es, f"vex{k}", (128, 1040), BF16) for k in range(3)]
        B["qer_t"] = p.sbuf(es, "qer", (128, T), BF16)
        B["onesbc"] = p.sbuf(es, "onesbc", (128, 1), BF16)
        B["onesbr"] = p.sbuf(es, "onesbr", (1, 128), BF16)
        B["zbuf"] = p.sbuf(es, "zbuf", (128, 128), BF16)
        B["bfull"] = p.sbuf(es, "bfull", (128, 512), BF16)
        if n_blocks >= 4:
            B["gext"] = p.sbuf(es, "gext", (128, T), BF16)
            B["b1c"] = p.sbuf(es, "b1c", (128, 44), F32)
            B["sel8h"] = p.sbuf(es, "sel8h", (128, 512), BF16)
            B["selfq"] = p.sbuf(es, "selfq", (128, 32), BF16)
            B["cnL"] = p.sbuf(es, "cnL", (128, 128), BF16)
            B["cnR"] = p.sbuf(es, "cnR", (128, 128), BF16)
        B["pm"] = p.psum(es, "pm", (128, T), F32)
        B["pb"] = p.psum(es, "pb", (128, T), F32)

        s_ld = p.sem("s_ld")
        p.dma("sync", B["identb"][:], dram["c_id_bf"], s_ld)
        p.dma("sync", B["identf"][:], dram["c_id_f32"], s_ld)
        p.dma("sync", B["ones_r"][:], dram["c_ones_row"], s_ld)
        p.dma("sync", B["invd_c"][:], dram["c_invd_col"], s_ld)
        p.dma("sync", B["cmisc"][:], dram["c_misc"], s_ld)
        if n_blocks >= 4:
            p.dma("sync", B["sel8h"][:], dram["c_sel8"], s_ld)
            p.dma("sync", B["selfq"][:], dram["c_selfq"], s_ld)
            p.dma("sync", B["cnL"][:], dram["c_cnL"], s_ld)
            p.dma("sync", B["cnR"][:], dram["c_cnR"], s_ld)
            consts.update(_mk_b3_consts())

        build_block0(p, es, B)
        if n_blocks >= 2:
            build_block1(p, es, B)
        if n_blocks >= 3:
            build_block2(p, es, B)
        if n_blocks >= 4:
            build_block3(p, es, B)
        build_output_store(p, B, B["xr"], dram["y"])

        p.finalize(es)
    return nc, consts



def kernel(**inputs):
    f32 = np.float32
    nc, consts = build_program(n_blocks=4)
    qs = np.asarray(inputs["L1_ffn_q_scale"], f32)
    ks = np.asarray(inputs["L1_ffn_k_scale"], f32)
    common = {
        "W_aiv": np.asarray(inputs["L0_attn_W_aiv"], f32),
        "decay_bias": np.asarray(inputs["L0_attn_decay_bias"], f32),
        "W_mix": np.asarray(inputs["L0_attn_W_mix"], f32),
        "b_mix": np.asarray(inputs["L0_attn_b_mix"], f32),
        "pk": np.asarray(inputs["L0_ffn_packed_weight"], np.int32),
        "wscale": np.asarray(inputs["L0_ffn_weight_scale"], f32).reshape(1, 1),
        "fbias": np.asarray(inputs["L0_ffn_bias"], f32),
        "W_qkv": np.asarray(inputs["L1_attn_W_qkv"], f32),
        "b_qkv": np.asarray(inputs["L1_attn_b_qkv"], f32),
        "W_out": np.asarray(inputs["L1_attn_W_out"], f32),
        "b_out": np.asarray(inputs["L1_attn_b_out"], f32),
        "rel_emb_k": np.asarray(inputs["L1_attn_rel_emb_k"], f32),
        "rel_emb_v": np.asarray(inputs["L1_attn_rel_emb_v"], f32),
        "l1f_wqkv": np.asarray(inputs["L1_ffn_wqkv"], f32).reshape(-1),
        "l1f_qs": np.tile(qs, (1, 2)).reshape(-1),
        "l1f_ks": np.tile(ks, (1, 2)).reshape(-1),
        "l1f_wo": np.asarray(inputs["L1_ffn_wo"], f32).reshape(-1),
        "l1f_lng": np.asarray(inputs["L1_ffn_ln_g"], f32).reshape(-1),
        "l1f_lnb": np.asarray(inputs["L1_ffn_ln_b"], f32).reshape(-1),
        "l1f_w1": np.asarray(inputs["L1_ffn_w1"], f32).reshape(-1),
        "l1f_b1": np.asarray(inputs["L1_ffn_b1"], f32).reshape(-1),
        "l1f_w2": np.asarray(inputs["L1_ffn_w2"], f32).reshape(-1),
        "l1f_b2": np.asarray(inputs["L1_ffn_b2"], f32).reshape(-1),
    }
    common.update(consts)
    xall = np.asarray(inputs["L0_attn_x"], f32)
    in_maps = []
    for b in range(8):
        m = dict(common)
        m["x"] = xall[b]
        in_maps.append(m)
    from concourse.bass_utils import run_bass_kernel_spmd
    res = run_bass_kernel_spmd(nc, in_maps, core_ids=list(range(8)))
    return np.stack([np.asarray(res.results[b]["y"], f32) for b in range(8)], axis=0)
